# version 02
# baseline (speedup 1.0000x reference)
import numpy as np
import ml_dtypes
from contextlib import ExitStack
import concourse.bass as bass
import bass_rust
import concourse.mybir as mybir
from concourse.bass_utils import run_bass_kernel_spmd

F32 = mybir.dt.float32
BF16 = mybir.dt.bfloat16
AF = mybir.ActivationFunctionType
ALU = mybir.AluOpType
BF = ml_dtypes.bfloat16

NC = 8
S = 8192
D = 4096
T = S // NC
HALO = 256
XT = T + 2 * HALO
NKT = XT // 128
DH = 128
EPS_LN = 1e-5
SCALE0 = 1.0 / float(np.sqrt(DH))



def _prep(I):
    p = {}
    perm = np.asarray(I["L0_ffn_perm"]).astype(np.int64)
    p["perm"] = perm

    x = np.asarray(I["L0_attn_x"]).reshape(S, D)
    p["x_p"] = np.ascontiguousarray(x[:, perm])

    p["wqT"] = np.ascontiguousarray(np.asarray(I["L0_attn_Wq"]).T[perm, :])
    p["wkT"] = np.ascontiguousarray(np.asarray(I["L0_attn_Wk"]).T[perm, :])
    p["wvT"] = np.ascontiguousarray(np.asarray(I["L0_attn_Wv"]).T[perm, :])
    p["woT"] = np.ascontiguousarray(np.asarray(I["L0_attn_Wo"])[perm, :].T)

    base = 10000.0
    inv_freq = 1.0 / base ** (np.arange(0, DH, 2, dtype=np.float32) / DH)
    t_ = np.arange(S, dtype=np.float32)
    f = np.outer(t_, inv_freq)
    cos = np.zeros((S, DH), dtype=np.float32)
    sin = np.zeros((S, DH), dtype=np.float32)
    cos[:, 0::2] = np.cos(f); cos[:, 1::2] = np.cos(f)
    sin[:, 0::2] = np.sin(f); sin[:, 1::2] = np.sin(f)
    cosT = np.ascontiguousarray(cos.T)
    sinT = np.ascontiguousarray(sin.T)
    sinrotT = np.concatenate([sinT[64:], sinT[:64]], axis=0)

    P = np.zeros((128, 128), dtype=np.float32)
    for a in range(64):
        P[a + 64, a] = -1.0
        P[a, a + 64] = 1.0
    p["P_rot"] = P
    p["ident"] = np.eye(128, dtype=np.float32)

    p["up"] = np.ascontiguousarray(np.asarray(I["L0_ffn_up"]))
    p["down"] = np.ascontiguousarray(np.asarray(I["L0_ffn_down"]))

    p["winP"] = np.ascontiguousarray(np.asarray(I["L1_attn_W_in"])[perm, :])
    p["woutP"] = np.ascontiguousarray(np.asarray(I["L1_attn_W_out"])[:, perm])
    p["conv_w"] = np.ascontiguousarray(np.asarray(I["L1_attn_conv_w"]))
    p["dt_bias"] = np.ascontiguousarray(np.asarray(I["L1_attn_dt_bias"]).reshape(1, 64))
    p["A_log"] = np.ascontiguousarray(np.asarray(I["L1_attn_A_log"]).reshape(1, 64))
    p["D_param"] = np.ascontiguousarray(np.asarray(I["L1_attn_D_param"]).reshape(1, 64))
    p["norm_w"] = np.ascontiguousarray(np.asarray(I["L1_attn_norm_weight"]).reshape(1, 8192))

    p["wbT"] = np.ascontiguousarray(np.asarray(I["L1_ffn_Wb"]).T[perm, :][:, perm])
    p["bias_p"] = np.ascontiguousarray(np.asarray(I["L1_ffn_bias"])[perm].reshape(1, D))
    p["wrT"] = np.ascontiguousarray(np.asarray(I["L1_ffn_Wr"]).T[perm, :])
    A = np.asarray(I["L1_ffn_A"])
    p["laT"] = np.ascontiguousarray(A[:, :, perm].transpose(2, 0, 1).reshape(D, 128))
    Bm = np.asarray(I["L1_ffn_Bm"])
    p["lbT"] = np.ascontiguousarray(Bm[:, perm, :].transpose(0, 2, 1).reshape(128, D))

    xs, cs, sr, mk, uwts = [], [], [], [], []
    for c in range(NC):
        g0 = c * T - HALO
        xin = np.zeros((XT, D), dtype=np.float32)
        lo = max(0, g0); hi = min(S, g0 + XT)
        xin[lo - g0: hi - g0] = p["x_p"][lo:hi]
        xs.append(xin)
        ct = np.zeros((128, XT), dtype=np.float32)
        st = np.zeros((128, XT), dtype=np.float32)
        ct[:, lo - g0: hi - g0] = cosT[:, lo:hi]
        st[:, lo - g0: hi - g0] = sinrotT[:, lo:hi]
        cs.append(ct); sr.append(st)
        m = np.zeros((128, NKT), dtype=np.float32)
        for kt in range(NKT):
            for r_ in range(128):
                g = g0 + kt * 128 + r_
                if not (0 <= g < S):
                    m[r_, kt] = -1e9
        mk.append(m)
        u = np.zeros((64, T), dtype=np.float32)
        for q in range(T):
            g = c * T + q
            kk = g // 256
            r = g - 256 * kk
            ua, ub = (0.0, 1.0) if kk == 0 else (1.0 - r / 255.0, r / 255.0)
            u[:32, q] = ua
            u[32:, q] = ub
        uwts.append(np.ascontiguousarray(u))
    p["xin"] = xs; p["cosT"] = cs; p["sinrotT"] = sr
    p["maskneg"] = mk; p["uw"] = uwts
    return p



class Kx:
    def __init__(self, nc, es, names):
        self.nc = nc
        self.t = nc.tensor
        self.v = nc.vector
        self.s = nc.scalar
        self.g = nc.gpsimd
        self.y = nc.sync
        self.sem = {}
        self.n = {}
        for nm in names:
            self.sem[nm] = es.enter_context(nc.semaphore(nm))
            self.n[nm] = 0

    def inc(self, instr, nm, k=1):
        instr.then_inc(self.sem[nm], k)
        self.n[nm] += k
        return self.n[nm]

    def wait(self, eng, nm, v=None):
        eng.wait_ge(self.sem[nm], self.n[nm] if v is None else v)

    def dinc(self, instr, nm):
        return self.inc(instr, nm, 16)

    def join(self, eng, names=("pe", "ve", "se", "dT", "dW", "dX", "dS")):
        for nm in names:
            if self.n[nm]:
                eng.wait_ge(self.sem[nm], self.n[nm])


SEMS = ["dW", "dW2", "dV", "dX", "dX2", "dG", "dG2", "dT", "dS", "dS2", "pe", "ve", "se", "ge", "cc"]


def load_w_cast(k, dst, src):
    return k.dinc(k.g.dma_start(out=dst, in_=src), "dW")



def emit_prelnT(k, rin_rows, xh, n_tiles, rt, xht, dumps, stats, scol0, epscol=None):
    dump1, dump2 = dumps
    for tt in list(range(n_tiles))[::-1]:
        k.wait(k.y, "ve")
        k.dinc(k.y.dma_start(out=rt[:], in_=rin_rows(tt)), "dX")
        c0 = scol0 + 4 * tt
        k.wait(k.v, "dX")
        k.inc(k.v.tensor_scalar(dump1, rt[:], 1.0, 0.0, op0=ALU.mult, op1=ALU.add,
                                accum_out=stats[:, c0:c0 + 1]), "ve")
        k.wait(k.s, "dX")
        k.inc(k.s.activation(dump2, rt[:], AF.Square,
                             accum_out=stats[:, c0 + 3:c0 + 4]), "se")
        mcol = stats[:, c0 + 1:c0 + 2]
        vcol = stats[:, c0 + 2:c0 + 3]
        k.wait(k.v, "se")
        k.wait(k.v, "ve")
        k.inc(k.v.tensor_scalar(mcol, stats[:, c0:c0 + 1], 1.0 / D, None, op0=ALU.mult), "ve")
        k.wait(k.v, "ve")
        k.inc(k.v.tensor_tensor(vcol, mcol, mcol, op=ALU.mult), "ve")
        k.wait(k.v, "ve")
        k.inc(k.v.scalar_tensor_tensor(vcol, stats[:, c0 + 3:c0 + 4], 1.0 / D, vcol,
                                       op0=ALU.mult, op1=ALU.subtract), "ve")
        k.wait(k.s, "ve")
        k.inc(k.s.activation(vcol, vcol, AF.Ln, bias=epscol[:]), "se")
        k.wait(k.s, "se")
        k.inc(k.s.activation(vcol, vcol, AF.Exp, scale=-0.5), "se")
        k.wait(k.v, "se")
        k.wait(k.v, "dT")
        k.inc(k.v.tensor_scalar(xht[:], rt[:], mcol, vcol,
                                op0=ALU.subtract, op1=ALU.mult), "ve")
        k.inc(k.v.engine_nop(), "ve")
        k.wait(k.s, "ve")
        k.dinc(k.s.dma_start(out=xh[:, :, tt * 128:(tt + 1) * 128],
                            in_=xht[:], transpose=True), "dT")


def emit_postln_residual(k, f_ch, fT, r_rows, rout_rows, n_tiles, rt, ru,
                        dumps, stats, scol0, epscol=None):
    dump1, dump2 = dumps
    nt = n_tiles
    k.wait(k.s, "se")
    k.wait(k.s, "ve")
    tdone = k.dinc(k.s.dma_start(out=fT,
                                in_=f_ch.rearrange("p a b -> p (a b)"),
                                transpose=True), "dT")
    for tt in range(nt):
        k.wait(k.y, "dS")
        k.wait(k.y, "ve")
        k.dinc(k.y.dma_start(out=rt[:], in_=r_rows(tt)), "dX")
        fview = fT[:, tt:32 * nt:nt, :]
        c0 = scol0 + 4 * tt
        k.wait(k.v, "dT", tdone)
        k.inc(k.v.tensor_scalar(dump1, fview, 1.0, 0.0, op0=ALU.mult, op1=ALU.add,
                                accum_out=stats[:, c0:c0 + 1]), "ve")
        k.wait(k.s, "dT", tdone)
        k.inc(k.s.activation(dump2, fview, AF.Square,
                             accum_out=stats[:, c0 + 3:c0 + 4]), "se")
        mcol = stats[:, c0 + 1:c0 + 2]
        vcol = stats[:, c0 + 2:c0 + 3]
        k.wait(k.v, "se")
        k.wait(k.v, "ve")
        k.inc(k.v.tensor_scalar(mcol, stats[:, c0:c0 + 1], 1.0 / D, None, op0=ALU.mult), "ve")
        k.wait(k.v, "ve")
        k.inc(k.v.tensor_tensor(vcol, mcol, mcol, op=ALU.mult), "ve")
        k.wait(k.v, "ve")
        k.inc(k.v.scalar_tensor_tensor(vcol, stats[:, c0 + 3:c0 + 4], 1.0 / D, vcol,
                                       op0=ALU.mult, op1=ALU.subtract), "ve")
        k.wait(k.s, "ve")
        k.inc(k.s.activation(vcol, vcol, AF.Ln, bias=epscol[:]), "se")
        k.wait(k.s, "se")
        k.inc(k.s.activation(vcol, vcol, AF.Exp, scale=-0.5), "se")
        k.wait(k.v, "se")
        k.wait(k.v, "dX")
        k.inc(k.v.tensor_scalar(ru[:], fview, mcol, vcol, op0=ALU.subtract,
                                op1=ALU.mult), "ve")
        k.wait(k.v, "ve")
        k.inc(k.v.tensor_tensor(ru[:], ru[:], rt[:], op=ALU.add), "ve")
        k.wait(k.y, "ve")
        k.dinc(k.y.dma_start(out=rout_rows(tt), in_=ru[:]), "dS")



def _emit_l0attn(k, es, sb, nc, io, dbg=False, dbg_stop=None):
    kr = sb("kr", [128, XT], BF16)
    vt = sb("vt", [128, NKT, 128], BF16)
    cos_sb = sb("cos_sb", [128, XT], BF16)
    sinr_sb = sb("sinr_sb", [128, XT], BF16)
    p_sb = sb("p_sb", [128, 128], BF16)
    id_bf = sb("id_bf", [128, 128], BF16)
    id_f32 = sb("id_f32", [128, 128], F32)
    mask_sb = sb("mask_sb", [128, NKT], F32)
    stats = sb("stats", [128, 96], F32)
    ones_sb = sb("ones_sb", [128, 1], BF16)
    epscol = sb("epscol", [128, 1], F32)

    load_w_cast(k, cos_sb[:], io["cosT"])
    load_w_cast(k, sinr_sb[:], io["sinrotT"])
    load_w_cast(k, p_sb[:], io["prot"])
    load_w_cast(k, id_bf[:], io["ident"])
    k.dinc(k.y.dma_start(out=id_f32[:], in_=io["ident"]), "dX")
    k.dinc(k.y.dma_start(out=mask_sb[:], in_=io["maskneg"]), "dX")
    k.g.memset(ones_sb[:], 1.0)
    k.g.memset(epscol[:], EPS_LN)
    ge_consts = k.inc(k.g.engine_nop(), "ge")
    dw_consts = k.n["dW"]

    dummyT = sb("dummyT", [128, 16], BF16)
    k.wait(k.s, "dW")
    k.dinc(k.s.dma_start(out=dummyT[:], in_=cos_sb[0:16, 0:128], transpose=True), "dT")
    k.wait(k.s, "ge")
    k.wait(k.v, "ge")
    k.wait(k.t, "ge")

    BIGA = sb("BIGA", [128, 32 * XT], BF16)
    BIGB = sb("BIGB", [128, 32 * T], BF16)
    q_sb = BIGB.rearrange("p (a b) -> p a b", a=32)

    if True:
        xh = BIGA.rearrange("p (a b) -> p a b", a=32)
        with nc.sbuf_tensor("rt1", [128, D], F32) as rt, \
             nc.sbuf_tensor("xht1", [128, D], BF16) as xht:
            dump1 = q_sb[:, 0:4, :].rearrange("p a b -> p (a b)")
            dump2 = q_sb[:, 4:8, :].rearrange("p a b -> p (a b)")
            emit_prelnT(k, lambda tt: io["xin"][tt * 128:(tt + 1) * 128, :],
                        xh, NKT, rt, xht, (dump1, dump2), stats, 0, epscol)

        with nc.sbuf_tensor("wbuf", [128, 2, 32, 128], BF16) as wbuf, \
             nc.sbuf_tensor("qraw", [128, 2, 512], BF16) as qraw, \
             nc.sbuf_tensor("qs2", [128, 2, 512], BF16) as qs2, \
             nc.sbuf_tensor("t1b", [128, 2, 512], BF16) as t1b, \
             nc.psum_tensor("psp", [128, 2, 512], F32) as psp, \
             nc.psum_tensor("psr", [128, 2, 512], F32) as psr, \
             nc.psum_tensor("psT", [128, 128], BF16) as psT:

            xh_done = k.n["dT"]
            units = [("q", j) for j in range(32)] + [("k", 0), ("v", 0)]
            wload = []

            def load_unit(uj):
                kind, j = units[uj]
                src = (io["wqT"][:, j * 128:(j + 1) * 128] if kind == "q"
                       else (io["wkT"] if kind == "k" else io["wvT"]))
                nm = "dW" if uj % 2 == 0 else "dW2"
                wload.append((nm, k.dinc(k.g.dma_start(
                    out=wbuf[:, uj % 2],
                    in_=src.rearrange("(a p) b -> p a b", p=128)), nm)))

            k.join(k.g, ("ve", "dT"))
            load_unit(0)
            load_unit(1)
            k.wait(k.t, "dT", xh_done)
            k.wait(k.t, "ge", ge_consts)
            k.wait(k.t, "dW", dw_consts)

            evac_ve = {}
            fin_ve = {}
            proj_pe = {}
            hfc = 0
            for uj, (kind, j) in enumerate(units):
                k.wait(k.t, wload[uj][0], wload[uj][1])
                nhalf = 2 if kind == "q" else 3
                for hf in range(nhalf):
                    off = (HALO if kind == "q" else 0) + hf * 512
                    pp = psp[:, hfc % 2, :]
                    if hfc >= 2:
                        k.wait(k.t, "ve", evac_ve[hfc - 2])
                    for ci in range(32):
                        mm = k.t.matmul(pp, wbuf[:, uj % 2, ci, :],
                                        xh[:, ci, off:off + 512],
                                        start=(ci == 0), stop=(ci == 31))
                    pj = k.inc(mm, "pe")
                    proj_pe[uj] = pj
                    k.wait(k.v, "pe", pj)
                    evac_ve[hfc] = k.inc(k.v.tensor_copy(qraw[:, hfc % 2, :], pp), "ve")
                    k.wait(k.v, "ve")
                    if kind == "v":
                        for c4 in range(4):
                            kt = 4 * hf + c4
                            k.wait(k.t, "ve")
                            tp = k.inc(k.t.transpose(
                                psT[:], qraw[:, hfc % 2, c4 * 128:(c4 + 1) * 128],
                                id_bf[:]), "pe")
                            k.wait(k.v, "pe", tp)
                            k.inc(k.v.tensor_copy(vt[:, kt, :], psT[:]), "ve")
                        fin_ve[hfc] = k.n["ve"]
                        hfc += 1
                        continue
                    k.inc(k.v.tensor_tensor(qs2[:, hfc % 2, :], qraw[:, hfc % 2, :],
                                            sinr_sb[:, off:off + 512], op=ALU.mult),
                          "ve")
                    k.wait(k.t, "ve")
                    rmm = k.t.matmul(psr[:, hfc % 2, :], p_sb[:], qs2[:, hfc % 2, :],
                                     start=True, stop=True)
                    rot_pe = k.inc(rmm, "pe")
                    k.inc(k.v.tensor_tensor(t1b[:, hfc % 2, :], qraw[:, hfc % 2, :],
                                            cos_sb[:, off:off + 512], op=ALU.mult),
                          "ve")
                    k.wait(k.v, "pe", rot_pe)
                    k.wait(k.v, "ve")
                    dst = (q_sb[:, j, hf * 512:(hf + 1) * 512] if kind == "q"
                           else kr[:, off:off + 512])
                    fin_ve[hfc] = k.inc(
                        k.v.tensor_tensor(dst, t1b[:, hfc % 2, :], psr[:, hfc % 2, :],
                                          op=ALU.add), "ve")
                    hfc += 1
                if uj + 2 < len(units):
                    k.wait(k.g, "pe", proj_pe[uj])
                    load_unit(uj + 2)

    if dbg:
        k.wait(k.y, "dT")
        k.dinc(k.y.dma_start(out=io["dbg_xh"], in_=BIGA[:]), "dS")
        k.dinc(k.y.dma_start(out=io["dbg_stats"], in_=stats[:]), "dS")
        k.wait(k.y, "ve")
        k.wait(k.y, "dS")
        k.dinc(k.y.dma_start(out=io["dbg_q"], in_=BIGB[:]), "dS")
        k.dinc(k.y.dma_start(out=io["dbg_kr"], in_=kr[:]), "dS")
        k.dinc(k.y.dma_start(out=io["dbg_vt"], in_=vt.rearrange("p a b -> p (a b)")), "dS")
        k.wait(k.y, "dS")
    if dbg_stop == "xh":
        return

    o_ch = BIGA[:, 0:32 * T].rearrange("p (a b) -> p a b", a=32)
    avA = BIGA[:, 32 * T + 0:32 * T + 4096].rearrange("p (a b) -> p a b", a=32)
    avB = BIGA[:, 32 * T + 4096:32 * T + 8192].rearrange("p (a b) -> p a b", a=32)
    avTA = BIGA[:, 32 * T + 8192:32 * T + 12288].rearrange("p (a b) -> p a b", a=32)
    avTB = BIGA[:, 32 * T + 12288:32 * T + 16384].rearrange("p (a b) -> p a b", a=32)
    with nc.sbuf_tensor("uw_sb", [64, T], F32) as uw_sb, \
         nc.sbuf_tensor("sig", [128, 2, 1024], BF16) as sig, \
         nc.sbuf_tensor("oT", [128, 32, 128], BF16) as oT, \
         nc.sbuf_tensor("rsrow", [1, 8192], BF16) as rsrow, \
         nc.sbuf_tensor("rsT", [64, 128], F32) as rsT, \
         nc.sbuf_tensor("uTc", [128, 64], F32) as uTc, \
         nc.sbuf_tensor("tmp2", [128, 128], BF16) as tmp2, \
         nc.psum_tensor("sc", [128, 2, 1024], F32) as sc, \
         nc.psum_tensor("av", [128, 2, 512], F32) as av, \
         nc.psum_tensor("rsp", [1, 1024], F32) as rsp:

        k.join(k.y, ("ve", "se", "pe"))
        k.join(k.t, ("ve", "se"))
        k.dinc(k.y.dma_start(out=uw_sb[:], in_=io["uw"]), "dX")
        for qt in range(8):
            ka = qt & ~1
            for w in range(2):
                av_sb = avA if w == 0 else avB
                for hg in range(4):
                    for kt4 in range(4):
                        ktile = ka + 2 * w + kt4
                        k.wait(k.t, "se")
                        for half in range(2):
                            mm = k.t.matmul(
                                sc[:, kt4 % 2, half * 512:(half + 1) * 512],
                                kr[:, ktile * 128:(ktile + 1) * 128],
                                q_sb[:, 8 * hg + 4 * half: 8 * hg + 4 * half + 4,
                                     qt * 128:(qt + 1) * 128],
                                start=True, stop=True)
                        sc_pe = k.inc(mm, "pe")
                        k.wait(k.s, "pe")
                        sg = k.inc(k.s.activation(
                            sig[:, kt4 % 2, :], sc[:, kt4 % 2, :], AF.Sigmoid,
                            bias=mask_sb[:, ktile:ktile + 1], scale=SCALE0), "se")
                        k.wait(k.t, "se", sg)
                        for half in range(2):
                            k.t.matmul(rsp[:, half * 512:(half + 1) * 512],
                                       ones_sb[:],
                                       sig[:, kt4 % 2, half * 512:(half + 1) * 512],
                                       start=(kt4 == 0), stop=(kt4 == 3))
                        for half in range(2):
                            mm = k.t.matmul(
                                av[:, half, :], vt[:, ktile, :],
                                sig[:, kt4 % 2, half * 512:(half + 1) * 512],
                                start=(kt4 == 0), stop=(kt4 == 3))
                        k.inc(mm, "pe")
                    k.wait(k.s, "pe")
                    k.inc(k.s.activation(
                        rsrow[:, w * 4096 + hg * 1024: w * 4096 + (hg + 1) * 1024],
                        rsp[:], AF.Identity), "se")
                    for half in range(2):
                        k.inc(k.s.activation(
                            av_sb[:, 8 * hg + 4 * half: 8 * hg + 4 * half + 4, :],
                            av[:, half, :], AF.Identity), "se")
                    k.wait(k.t, "se")

            k.wait(k.y, "se")
            k.dinc(k.y.dma_start(out=io["rs_scr"].rearrange("w g h q -> (w g h q)"),
                                in_=rsrow[0:1, :]), "dS")
            k.wait(k.g, "dS")
            k.dinc(k.g.dma_start(out=rsT[:],
                                in_=io["rs_scr"].rearrange("w g h q -> (w g h) q")),
                  "dG")
            k.wait(k.v, "dG")
            k.inc(k.v.tensor_scalar(rsT[:], rsT[:], 1e-30, None, op0=ALU.add), "ve")
            k.wait(k.v, "ve")
            k.inc(k.v.reciprocal(rsT[:], rsT[:]), "ve")
            k.wait(k.v, "ve")
            k.wait(k.v, "dX")
            umul = k.inc(k.v.tensor_tensor(rsT[:], rsT[:],
                                           uw_sb[:, qt * 128:(qt + 1) * 128],
                                           op=ALU.mult), "ve")
            k.wait(k.t, "ve", umul)
            k.wait(k.t, "se")
            tp = k.inc(k.t.transpose(sc[:, 0, 0:64], rsT[:], id_f32[0:64, 0:64]), "pe")
            k.wait(k.v, "pe", tp)
            k.inc(k.v.tensor_copy(uTc[:], sc[:, 0, 0:64]), "ve")
            k.wait(k.t, "ve")

            k.wait(k.s, "se")
            k.dinc(k.s.dma_start(out=avTA, in_=avA.rearrange("p a b -> p (a b)"),
                                transpose=True), "dT")
            dtb = k.dinc(k.s.dma_start(out=avTB,
                                      in_=avB.rearrange("p a b -> p (a b)"),
                                      transpose=True), "dT")
            k.wait(k.v, "dT", dtb)
            k.wait(k.v, "ve")
            for h in range(32):
                k.wait(k.v, "ve")
                k.inc(k.v.tensor_scalar(tmp2[:], avTA[:, h, :], uTc[:, h:h + 1], None,
                                  op0=ALU.mult), "ve")
                k.wait(k.v, "ve")
                k.inc(k.v.scalar_tensor_tensor(
                    oT[:, h, :], avTB[:, h, :], uTc[:, 32 + h:33 + h],
                    tmp2[:], op0=ALU.mult, op1=ALU.add), "ve")
            k.wait(k.s, "ve")
            k.dinc(k.s.dma_start(out=o_ch[:, :, qt * 128:(qt + 1) * 128],
                                in_=oT[:].rearrange("p a b -> p (a b)"),
                                transpose=True), "dT")
            k.wait(k.t, "dT")
            k.wait(k.v, "dT")

    if dbg:
        k.wait(k.y, "dT")
        k.wait(k.y, "ve")
        k.dinc(k.y.dma_start(out=io["dbg_och"], in_=BIGA[:, 0:32 * T]), "dS")
        k.wait(k.y, "dS")

    f_ch = BIGB.rearrange("p (a b) -> p a b", a=32)
    with nc.sbuf_tensor("wbuf3", [128, 2, 32, 128], BF16) as wbuf, \
         nc.psum_tensor("psp3", [128, 2, 512], F32) as psp:
        wl = []

        def load_w3(j):
            nm = "dW" if j % 2 == 0 else "dW2"
            wl.append((nm, k.dinc(k.g.dma_start(
                out=wbuf[:, j % 2],
                in_=io["woT"][:, j * 128:(j + 1) * 128]
                .rearrange("(a p) b -> p a b", p=128)), nm)))

        k.join(k.g, ("ve", "se", "pe", "dT", "dS"))
        k.join(k.t, ("ve", "se"))
        load_w3(0)
        load_w3(1)
        k.wait(k.t, "dT")
        evac_se = {}
        for j in range(32):
            k.wait(k.t, wl[j][0], wl[j][1])
            for hf in range(2):
                u = 2 * j + hf
                pp = psp[:, u % 2, :]
                if u >= 2:
                    k.wait(k.t, "se", evac_se[u - 2])
                for ci in range(32):
                    mm = k.t.matmul(pp, wbuf[:, j % 2, ci, :],
                                    o_ch[:, ci, hf * 512:(hf + 1) * 512],
                                    start=(ci == 0), stop=(ci == 31))
                pej = k.inc(mm, "pe")
                k.wait(k.s, "pe", pej)
                evac_se[u] = k.inc(
                    k.s.activation(f_ch[:, j, hf * 512:(hf + 1) * 512], pp,
                                   AF.Identity), "se")
            if j + 2 < 32:
                k.wait(k.g, "pe")
                load_w3(j + 2)

    if dbg:
        k.wait(k.y, "se")
        k.dinc(k.y.dma_start(out=io["dbg_fch"], in_=BIGB[:]), "dS")
        k.wait(k.y, "dS")

    fT = BIGA[:, 0:32 * T].rearrange("p (a b) -> p a b", a=256)
    dmp1 = BIGA[:, 32 * T:32 * T + 4096]
    dmp2 = BIGA[:, 32 * T + 4096:32 * T + 8192]
    k.join(k.y, ("pe", "se"))
    with nc.sbuf_tensor("rt4", [128, D], F32) as rt4, \
         nc.sbuf_tensor("ru4", [128, D], F32) as ru4:
        emit_postln_residual(
            k, f_ch, fT,
            lambda tt: io["xin"][HALO + tt * 128: HALO + (tt + 1) * 128, :],
            lambda tt: io["r_out"][tt * 128:(tt + 1) * 128, :],
            8, rt4, ru4, (dmp1, dmp2), stats, 48, epscol)

    k.wait(k.y, "dS")


def _build_l0attn(dbg=False, dbg_stop=None):
    nc = bass.Bass("TRN2", target_bir_lowering=False)
    io = {}
    io["xin"] = nc.dram_tensor("xin", [XT, D], F32, kind="ExternalInput").ap()
    io["wqT"] = nc.dram_tensor("wqT", [D, D], F32, kind="ExternalInput").ap()
    io["wkT"] = nc.dram_tensor("wkT", [D, DH], F32, kind="ExternalInput").ap()
    io["wvT"] = nc.dram_tensor("wvT", [D, DH], F32, kind="ExternalInput").ap()
    io["woT"] = nc.dram_tensor("woT", [D, D], F32, kind="ExternalInput").ap()
    io["cosT"] = nc.dram_tensor("cosT", [128, XT], F32, kind="ExternalInput").ap()
    io["sinrotT"] = nc.dram_tensor("sinrotT", [128, XT], F32, kind="ExternalInput").ap()
    io["prot"] = nc.dram_tensor("prot", [128, 128], F32, kind="ExternalInput").ap()
    io["ident"] = nc.dram_tensor("ident", [128, 128], F32, kind="ExternalInput").ap()
    io["maskneg"] = nc.dram_tensor("maskneg", [128, NKT], F32, kind="ExternalInput").ap()
    io["uw"] = nc.dram_tensor("uw", [64, T], F32, kind="ExternalInput").ap()
    io["r_out"] = nc.dram_tensor("r_out", [T, D], F32, kind="ExternalOutput").ap()
    io["rs_scr"] = nc.dram_tensor("rs_scr", [2, 8, 4, 128], BF16).ap()
    if dbg:
        io["dbg_xh"] = nc.dram_tensor("dbg_xh", [128, 32 * XT], BF16, kind="ExternalOutput").ap()
        io["dbg_stats"] = nc.dram_tensor("dbg_stats", [128, 96], F32, kind="ExternalOutput").ap()
        io["dbg_q"] = nc.dram_tensor("dbg_q", [128, 32 * T], BF16, kind="ExternalOutput").ap()
        io["dbg_kr"] = nc.dram_tensor("dbg_kr", [128, XT], BF16, kind="ExternalOutput").ap()
        io["dbg_vt"] = nc.dram_tensor("dbg_vt", [128, NKT * 128], BF16, kind="ExternalOutput").ap()
        io["dbg_och"] = nc.dram_tensor("dbg_och", [128, 32 * T], BF16, kind="ExternalOutput").ap()
        io["dbg_fch"] = nc.dram_tensor("dbg_fch", [128, 32 * T], BF16, kind="ExternalOutput").ap()

    with ExitStack() as es:
        def sb(name, shape, dt):
            return es.enter_context(nc.sbuf_tensor(name, shape, dt))
        k = Kx(nc, es, SEMS)
        block = es.enter_context(nc.Block())

        @block.sync
        def _(sy):
            _emit_l0attn(k, es, sb, nc, io, dbg=dbg, dbg_stop=dbg_stop)
    return nc


def _maps_l0attn(p, R):
    maps = []
    for c in range(NC):
        g0 = c * T - HALO
        xin = np.zeros((XT, D), dtype=np.float32)
        lo = max(0, g0); hi = min(S, g0 + XT)
        xin[lo - g0: hi - g0] = R[lo:hi]
        maps.append(dict(
            xin=xin, wqT=p["wqT"], wkT=p["wkT"], wvT=p["wvT"], woT=p["woT"],
            cosT=p["cosT"][c], sinrotT=p["sinrotT"][c], prot=p["P_rot"],
            ident=p["ident"], maskneg=p["maskneg"][c], uw=p["uw"][c]))
    return maps


def _run_l0attn(p, R):
    nc = _build_l0attn()
    res = run_bass_kernel_spmd(nc, _maps_l0attn(p, R), core_ids=list(range(NC)))
    return np.concatenate([res.results[c]["r_out"] for c in range(NC)], axis=0)





def _emit_l0ffn(k, es, sb, nc, io):
    stats = sb("stats", [128, 96], F32)
    epscol = sb("epscol", [128, 1], F32)
    ARA = sb("ARA", [128, 32 * T], BF16)
    ARB = sb("ARB", [128, 32 * T], BF16)
    dmp = sb("dmp", [128, 2 * D], BF16)
    xh = ARA.rearrange("p (a b) -> p a b", a=32)
    f_ch = ARB.rearrange("p (a b) -> p a b", a=32)

    k.g.memset(epscol[:], EPS_LN)
    k.inc(k.g.engine_nop(), "ge")
    k.wait(k.s, "ge")
    k.wait(k.v, "ge")
    dummyT = sb("dummyT", [128, 16], BF16)
    dummyS = sb("dummyS", [16, 128], BF16)
    k.g.memset(dummyS[:], 0.0)
    ge_dummy = k.inc(k.g.engine_nop(), "ge")
    k.wait(k.s, "ge", ge_dummy)
    k.dinc(k.s.dma_start(out=dummyT[:], in_=dummyS[:], transpose=True), "dT")

    with nc.sbuf_tensor("rt1", [128, D], F32) as rt, \
         nc.sbuf_tensor("xht1", [128, D], BF16) as xht:
        emit_prelnT(k, lambda tt: io["r_in"][tt * 128:(tt + 1) * 128, :],
                    xh, 8, rt, xht, (dmp[:, 0:D], dmp[:, D:2 * D]), stats, 0,
                    epscol)

    k.join(k.g, ("ve", "dT", "se"))
    with nc.sbuf_tensor("wbufU", [128, 2, 2, 1024], BF16) as wbufU, \
         nc.sbuf_tensor("wbufD", [128, 2, 8, 256], BF16) as wbufD, \
         nc.sbuf_tensor("hid", [128, 8, 1024], BF16) as hid, \
         nc.psum_tensor("psu", [128, 2, 512], F32) as psu, \
         nc.psum_tensor("psd", [128, 2, 512], F32) as psd:

        wl = []

        def load_head(h):
            nm = "dW" if h % 2 == 0 else "dW2"
            k.dinc(k.g.dma_start(
                out=wbufU[:, h % 2],
                in_=io["up"][h].rearrange("(a p) m -> p a m", p=128)), nm)
            wl.append((nm, k.dinc(k.g.dma_start(
                out=wbufD[:, h % 2],
                in_=io["down"][h].rearrange("(a p) d -> p a d", p=128)), nm)))

        load_head(0)
        load_head(1)
        k.wait(k.t, "dT")
        k.wait(k.t, "ge", ge_dummy)
        up_pe = {}
        for h in range(16):
            k.wait(k.t, wl[h][0], wl[h][1])
            for mt in range(8):
                for half in range(2):
                    u = 2 * mt + half
                    if h > 0 or u >= 2:
                        k.wait(k.t, "se")
                    for d2 in range(2):
                        mm = k.t.matmul(psu[:, u % 2, :],
                                        wbufU[:, h % 2, d2, mt * 128:(mt + 1) * 128],
                                        xh[:, 2 * h + d2, half * 512:(half + 1) * 512],
                                        start=(d2 == 0), stop=(d2 == 1))
                    pj = k.inc(mm, "pe")
                    k.wait(k.s, "pe", pj)
                    k.inc(k.s.activation(hid[:, mt, half * 512:(half + 1) * 512],
                                         psu[:, u % 2, :], AF.Gelu), "se")
            for dt_ in range(2):
                for half in range(2):
                    u = 2 * dt_ + half
                    if h > 0 or u >= 2:
                        k.wait(k.t, "se")
                    k.wait(k.t, "se")
                    for mt in range(8):
                        mm = k.t.matmul(psd[:, u % 2, :],
                                        wbufD[:, h % 2, mt, dt_ * 128:(dt_ + 1) * 128],
                                        hid[:, mt, half * 512:(half + 1) * 512],
                                        start=(mt == 0), stop=(mt == 7))
                    pj = k.inc(mm, "pe")
                    up_pe[h] = pj
                    k.wait(k.s, "pe", pj)
                    k.inc(k.s.activation(
                        f_ch[:, 2 * h + dt_, half * 512:(half + 1) * 512],
                        psd[:, u % 2, :], AF.Identity), "se")
            if h + 2 < 16:
                k.wait(k.g, "pe", up_pe[h])
                load_head(h + 2)

    fT = ARA.rearrange("p (a b) -> p a b", a=256)
    k.join(k.y, ("pe", "se"))
    with nc.sbuf_tensor("rt4", [128, D], F32) as rt4, \
         nc.sbuf_tensor("ru4", [128, D], F32) as ru4:
        emit_postln_residual(
            k, f_ch, fT,
            lambda tt: io["r_in"][tt * 128:(tt + 1) * 128, :],
            lambda tt: io["r_out"][tt * 128:(tt + 1) * 128, :],
            8, rt4, ru4, (dmp[:, 0:D], dmp[:, D:2 * D]), stats, 48, epscol)
    k.wait(k.y, "dS")


def _build_l0ffn():
    nc = bass.Bass("TRN2", target_bir_lowering=False)
    io = {}
    io["r_in"] = nc.dram_tensor("r_in", [T, D], F32, kind="ExternalInput").ap()
    io["up"] = nc.dram_tensor("up", [16, 256, 1024], F32, kind="ExternalInput").ap()
    io["down"] = nc.dram_tensor("down", [16, 1024, 256], F32, kind="ExternalInput").ap()
    io["r_out"] = nc.dram_tensor("r_out", [T, D], F32, kind="ExternalOutput").ap()
    with ExitStack() as es:
        def sb(name, shape, dt):
            return es.enter_context(nc.sbuf_tensor(name, shape, dt))
        k = Kx(nc, es, SEMS)
        block = es.enter_context(nc.Block())

        @block.sync
        def _(sy):
            _emit_l0ffn(k, es, sb, nc, io)
    return nc


def _run_l0ffn(p, R):
    nc = _build_l0ffn()
    maps = [dict(r_in=np.ascontiguousarray(R[c * T:(c + 1) * T]),
                 up=p["up"], down=p["down"]) for c in range(NC)]
    res = run_bass_kernel_spmd(nc, maps, core_ids=list(range(NC)))
    return np.concatenate([res.results[c]["r_out"] for c in range(NC)], axis=0)





def _emit_l1ffn(k, es, sb, nc, io):
    stats = sb("stats", [128, 96], F32)
    epscol = sb("epscol", [128, 1], F32)
    ARA = sb("ARA", [128, 32 * T], BF16)
    ARB = sb("ARB", [128, 32 * T], BF16)
    dmp = sb("dmp", [128, 2 * D], BF16)
    xh = ARA.rearrange("p (a b) -> p a b", a=32)
    f_ch = ARB.rearrange("p (a b) -> p a b", a=32)

    k.g.memset(epscol[:], EPS_LN)
    k.inc(k.g.engine_nop(), "ge")
    k.wait(k.s, "ge")
    k.wait(k.v, "ge")
    dummyT = sb("dummyT", [128, 16], BF16)
    dummyS = sb("dummyS", [16, 128], BF16)
    k.g.memset(dummyS[:], 0.0)
    ge_dummy = k.inc(k.g.engine_nop(), "ge")
    k.wait(k.s, "ge", ge_dummy)
    k.dinc(k.s.dma_start(out=dummyT[:], in_=dummyS[:], transpose=True), "dT")

    with nc.sbuf_tensor("rt1", [128, D], F32) as rt, \
         nc.sbuf_tensor("xht1", [128, D], BF16) as xht:
        emit_prelnT(k, lambda tt: io["r_in"][tt * 128:(tt + 1) * 128, :],
                    xh, 8, rt, xht, (dmp[:, 0:D], dmp[:, D:2 * D]), stats, 0,
                    epscol)

    k.join(k.g, ("ve", "dT", "se"))
    with nc.sbuf_tensor("wbuf", [128, 2, 32, 128], BF16) as wbuf, \
         nc.psum_tensor("ps2", [128, 2, 512], F32) as ps2:
        es2 = ExitStack()
        lg = es2.enter_context(nc.sbuf_tensor("lg", [8, T], F32))[:]
        wx = es2.enter_context(nc.sbuf_tensor("wx", [8, T], F32))[:]
        lgt_sb = es2.enter_context(nc.sbuf_tensor("lgt_sb", [128, 8, 26], F32))
        id8 = es2.enter_context(nc.sbuf_tensor("id8", [8, 8], F32))
        id_f32 = es2.enter_context(nc.sbuf_tensor("idf", [128, 128], F32))
        k.dinc(k.y.dma_start(out=id8[:], in_=io["ident"][0:8, 0:8]), "dS")
        k.dinc(k.y.dma_start(out=id_f32[:], in_=io["ident"]), "dS")
        wexp = es2.enter_context(nc.sbuf_tensor("wexp", [128, T], F32))
        g_sb = es2.enter_context(nc.sbuf_tensor("g_sb", [128, T], BF16))
        lbT = es2.enter_context(nc.sbuf_tensor("lbT_sb", [128, 32, 128], BF16))
        bias_sb = es2.enter_context(nc.sbuf_tensor("biasv", [128, 32], F32))
        k.dinc(k.g.dma_start(out=lbT[:],
                             in_=io["lbT"].rearrange("p (a b) -> p a b", a=32)), "dV")
        k.dinc(k.y.dma_start(out=bias_sb[:], in_=io["bias_sb"]), "dS")
        k.wait(k.t, "dS")
        with nc.sbuf_tensor("wr_sb", [128, 32, 8], BF16) as wr_sb:
            k.dinc(k.g.dma_start(out=wr_sb[:],
                                 in_=io["wrT"].rearrange("(a p) e -> p a e", p=128)),
                   "dV")
            k.wait(k.t, "dV")
            k.wait(k.t, "dT")
            k.wait(k.t, "ge", ge_dummy)
            for half in range(2):
                for ci in range(32):
                    mm = k.t.matmul(ps2[0:8, half, :], wr_sb[:, ci, :],
                                    xh[:, ci, half * 512:(half + 1) * 512],
                                    start=(ci == 0), stop=(ci == 31))
                k.inc(mm, "pe")
                k.wait(k.v, "pe")
                k.inc(k.v.tensor_copy(lg[:, half * 512:(half + 1) * 512],
                                      ps2[0:8, half, :]), "ve")
            for c8 in range(8):
                k.wait(k.t, "ve")
                tp = k.inc(k.t.transpose(ps2[:, 0, 0:8],
                                         lg[:, c8 * 128:(c8 + 1) * 128],
                                         id8[0:8, 0:8]), "pe")
                k.wait(k.v, "pe", tp)
                k.inc(k.v.tensor_copy(lgt_sb[:, c8, 0:8], ps2[:, 0, 0:8]), "ve")
                sl = lgt_sb[:, c8, 0:8]
                m1c = lgt_sb[:, c8, 8:9]
                m2c = lgt_sb[:, c8, 9:10]
                k.wait(k.v, "ve")
                k.inc(k.v.tensor_reduce(m1c, sl, op=ALU.max,
                                        axis=mybir.AxisListType.X), "ve")
                k.wait(k.v, "ve")
                k.inc(k.v.tensor_scalar(lgt_sb[:, c8, 10:18], sl, m1c, None,
                                        op0=ALU.is_ge), "ve")
                k.wait(k.v, "ve")
                k.inc(k.v.scalar_tensor_tensor(lgt_sb[:, c8, 10:18],
                                               lgt_sb[:, c8, 10:18], -1e30, sl,
                                               op0=ALU.mult, op1=ALU.add), "ve")
                k.wait(k.v, "ve")
                k.inc(k.v.tensor_reduce(m2c, lgt_sb[:, c8, 10:18], op=ALU.max,
                                        axis=mybir.AxisListType.X), "ve")
                k.wait(k.v, "ve")
                k.inc(k.v.tensor_scalar(m1c, m1c, -1.0, None, op0=ALU.mult), "ve")
                k.wait(k.s, "ve")
                k.inc(k.s.activation(lgt_sb[:, c8, 10:18], sl, AF.Exp,
                                     bias=m1c), "se")
                k.wait(k.v, "se")
                k.inc(k.v.tensor_scalar(lgt_sb[:, c8, 18:26], sl, m2c, None,
                                        op0=ALU.is_ge), "ve")
                k.wait(k.v, "ve")
                k.inc(k.v.tensor_tensor(lgt_sb[:, c8, 10:18],
                                        lgt_sb[:, c8, 10:18],
                                        lgt_sb[:, c8, 18:26], op=ALU.mult), "ve")
                k.wait(k.v, "ve")
                k.inc(k.v.tensor_reduce(m2c, lgt_sb[:, c8, 10:18], op=ALU.add,
                                        axis=mybir.AxisListType.X), "ve")
                k.wait(k.v, "ve")
                k.inc(k.v.reciprocal(m2c, m2c), "ve")
                k.wait(k.v, "ve")
                k.inc(k.v.tensor_scalar(lgt_sb[:, c8, 10:18],
                                        lgt_sb[:, c8, 10:18], m2c, 2.0,
                                        op0=ALU.mult, op1=ALU.mult), "ve")
                k.wait(k.t, "ve")
                tp = k.inc(k.t.transpose(ps2[0:8, 1, 0:128],
                                         lgt_sb[:, c8, 10:18],
                                         id_f32[:]), "pe")
                k.wait(k.v, "pe", tp)
                k.inc(k.v.tensor_copy(wx[:, c8 * 128:(c8 + 1) * 128],
                                      ps2[0:8, 1, 0:128]), "ve")
                k.wait(k.t, "ve")
            k.g.memset(wexp[:], 0.0)
            k.inc(k.g.engine_nop(), "ge")
            k.wait(k.y, "ge")
            k.wait(k.y, "ve")
            k.wait(k.y, "dT")
            for qd in range(4):
                k.dinc(k.y.dma_start(out=wexp[32 * qd:32 * qd + 2, :],
                                     in_=wx[2 * qd:2 * qd + 2, :]), "dX")
            k.wait(k.v, "dX")
            mask = [i // 16 for i in range(32)]
            k.inc(k.v.stream_shuffle(wexp[:], wexp[:], mask), "ve")

            k.wait(k.g, "pe")
            k.dinc(k.g.dma_start(out=wbuf[:, 0],
                                 in_=io["laT"].rearrange("(a p) r -> p a r", p=128)),
                   "dV")
            k.wait(k.t, "dV")
            for half in range(2):
                for ci in range(32):
                    mm = k.t.matmul(ps2[:, half, :], wbuf[:, 0, ci, :],
                                    xh[:, ci, half * 512:(half + 1) * 512],
                                    start=(ci == 0), stop=(ci == 31))
                k.inc(mm, "pe")
                k.wait(k.v, "pe")
                k.wait(k.v, "ve")
                k.inc(k.v.tensor_tensor(g_sb[:, half * 512:(half + 1) * 512],
                                        ps2[:, half, :],
                                        wexp[:, half * 512:(half + 1) * 512],
                                        op=ALU.mult), "ve")

        wl = []

        def load_wb(j):
            nm = "dW" if j % 2 == 0 else "dW2"
            wl.append((nm, k.dinc(k.g.dma_start(
                out=wbuf[:, j % 2],
                in_=io["wbT"][:, j * 128:(j + 1) * 128]
                .rearrange("(a p) b -> p a b", p=128)), nm)))

        k.wait(k.g, "ve")
        load_wb(0)
        load_wb(1)
        k.wait(k.s, "dS")
        evac_se = {}
        for j in range(32):
            k.wait(k.t, wl[j][0], wl[j][1])
            for hf in range(2):
                u = 2 * j + hf
                if u >= 2:
                    k.wait(k.t, "se", evac_se[u - 2])
                for ci in range(32):
                    k.t.matmul(ps2[:, u % 2, :], wbuf[:, j % 2, ci, :],
                               xh[:, ci, hf * 512:(hf + 1) * 512],
                               start=(ci == 0), stop=False)
                mm = k.t.matmul(ps2[:, u % 2, :], lbT[:, j, :],
                                g_sb[:, hf * 512:(hf + 1) * 512],
                                start=False, stop=True)
                pej = k.inc(mm, "pe")
                k.wait(k.s, "pe", pej)
                evac_se[u] = k.inc(
                    k.s.activation(f_ch[:, j, hf * 512:(hf + 1) * 512],
                                   ps2[:, u % 2, :], AF.Identity,
                                   bias=bias_sb[:, j:j + 1]), "se")
            if j + 2 < 32:
                k.wait(k.g, "pe", pej)
                load_wb(j + 2)
        es2.close()

    fT = ARA.rearrange("p (a b) -> p a b", a=256)
    k.join(k.y, ("pe", "se"))
    with nc.sbuf_tensor("rt4", [128, D], F32) as rt4, \
         nc.sbuf_tensor("ru4", [128, D], F32) as ru4:
        emit_postln_residual(
            k, f_ch, fT,
            lambda tt: io["r_in"][tt * 128:(tt + 1) * 128, :],
            lambda tt: io["r_out"][tt * 128:(tt + 1) * 128, :],
            8, rt4, ru4, (dmp[:, 0:D], dmp[:, D:2 * D]), stats, 48, epscol)
    k.wait(k.y, "dS")


def _build_l1ffn():
    nc = bass.Bass("TRN2", target_bir_lowering=False)
    io = {}
    io["r_in"] = nc.dram_tensor("r_in", [T, D], F32, kind="ExternalInput").ap()
    io["wbT"] = nc.dram_tensor("wbT", [D, D], F32, kind="ExternalInput").ap()
    io["wrT"] = nc.dram_tensor("wrT", [D, 8], F32, kind="ExternalInput").ap()
    io["laT"] = nc.dram_tensor("laT", [D, 128], F32, kind="ExternalInput").ap()
    io["lbT"] = nc.dram_tensor("lbT", [128, D], F32, kind="ExternalInput").ap()
    io["bias_sb"] = nc.dram_tensor("bias_sb", [128, 32], F32, kind="ExternalInput").ap()
    io["ident"] = nc.dram_tensor("ident", [128, 128], F32, kind="ExternalInput").ap()
    io["r_out"] = nc.dram_tensor("r_out", [T, D], F32, kind="ExternalOutput").ap()
    with ExitStack() as es:
        def sb(name, shape, dt):
            return es.enter_context(nc.sbuf_tensor(name, shape, dt))
        k = Kx(nc, es, SEMS)
        block = es.enter_context(nc.Block())

        @block.sync
        def _(sy):
            _emit_l1ffn(k, es, sb, nc, io)
    return nc


def _run_l1ffn(p, R):
    nc = _build_l1ffn()
    bias128 = np.ascontiguousarray(p["bias_p"].reshape(32, 128).T)
    maps = [dict(r_in=np.ascontiguousarray(R[c * T:(c + 1) * T]),
                 wbT=p["wbT"], wrT=p["wrT"], laT=p["laT"], lbT=p["lbT"],
                 bias_sb=bias128, ident=p["ident"]) for c in range(NC)]
    res = run_bass_kernel_spmd(nc, maps, core_ids=list(range(NC)))
    return np.concatenate([res.results[c]["r_out"] for c in range(NC)], axis=0)




TH = 1152
DSSM = 8192
NHM = 64
CONVD = 8448
EPS_RMS = 1e-5


def _emit_mamba(k, es, sb, nc, io, groups=None):
    if groups is None:
        groups = [list(range(NC))]
    W = len(groups[0])
    stats = sb("stats", [128, 96], F32)
    epscol = sb("epscol", [128, 1], F32)
    onecol = sb("onecol", [128, 1], F32)
    ones1 = sb("ones1", [1, 128], F32)
    id_f32 = sb("id_f32", [128, 128], F32)
    id_bf = sb("id_bf", [128, 128], BF16)
    ltmask = sb("ltmask_sb", [128, 128], BF16)
    dt_sp = sb("dt_sp", [64, T], F32)
    cum = sb("cum", [64, T], F32)
    ctp = sb("ctp", [64, 18], F32)
    dtT = sb("dtT", [128, 8, 64], F32)
    ncumT = sb("ncumT", [128, 8, 64], F32)
    ecumT = sb("ecumT", [128, 8, 64], F32)
    wdtT = sb("wdtT", [128, 8, 64], F32)
    decT = sb("decT", [128, 8, 64], F32)
    pfT = sb("pfT", [128, 8, 64], F32)
    h_sb = sb("h_sb", [128, 64, 128], BF16)
    C_full = sb("C_full", [128, 8, 128], BF16)
    nw_sb = sb("nw_sb", [128, DSSM], BF16)
    Dcol = sb("Dcol", [128, 64], F32)
    cw_sb = sb("cwv", [128, 66, 4], F32)
    dtb = sb("dtbv", [64, 1], F32)
    alg = sb("algv", [64, 1], F32)
    wmsk = sb("wmskv", [8, 8], F32)
    selc = sb("selcv", [8, 1], F32)
    gct = sb("gct", [8, 64], F32)
    wco = sb("wco", [128, 8, 64], F32)
    dummyT = sb("dummyT", [128, 16], BF16)
    dummyS = sb("dummyS", [16, 128], BF16)
    ARE = sb("ARE", [128, 66816], BF16)

    k.g.memset(epscol[:], EPS_RMS)
    k.g.memset(onecol[:], 1.0)
    k.g.memset(ones1[:], 1.0)
    k.g.memset(h_sb[:], 0.0)
    k.g.memset(dummyS[:], 0.0)
    ge_m0 = k.inc(k.g.engine_nop(), "ge")
    k.wait(k.s, "ge", ge_m0)
    k.wait(k.v, "ge", ge_m0)
    k.wait(k.t, "ge", ge_m0)
    k.dinc(k.s.dma_start(out=dummyT[:], in_=dummyS[:], transpose=True), "dT")
    k.dinc(k.y.dma_start(out=id_f32[:], in_=io["ident"]), "dX")
    k.dinc(k.g.dma_start(out=id_bf[:], in_=io["ident"]), "dV")
    k.dinc(k.g.dma_start(out=ltmask[:], in_=io["ltmask"]), "dV")
    k.dinc(k.y.dma_start(out=cw_sb[:],
                         in_=io["conv_w"].rearrange("(a p) t -> p a t", p=128)), "dX")
    k.dinc(k.y.dma_start(out=dtb[:], in_=io["dt_bias"]), "dX")
    k.dinc(k.y.dma_start(out=alg[:], in_=io["A_log"]), "dX")
    k.dinc(k.y.dma_start(out=wmsk[:], in_=io["wmask"]), "dX")
    k.dinc(k.y.dma_start(out=selc[:], in_=io["selmask"]), "dX")
    with nc.sbuf_tensor("drow", [1, 64], F32) as drow, \
         nc.sbuf_tensor("nwrow", [1, 4096], F32) as nwrow, \
         nc.psum_tensor("psb0", [128, 2, 512], F32) as psb0:
        k.dinc(k.y.dma_start(out=drow[:], in_=io["D_param"]), "dX")
        k.wait(k.s, "dX")
        k.inc(k.s.activation(alg[:], alg[:], AF.Exp), "se")
        k.wait(k.t, "dX")
        k.wait(k.t, "ge", ge_m0)
        tp = k.inc(k.t.matmul(psb0[:, 0, 0:64], ones1[:], drow[:],
                              start=True, stop=True), "pe")
        k.wait(k.v, "pe", tp)
        k.inc(k.v.tensor_copy(Dcol[:], psb0[:, 0, 0:64]), "ve")
        for hf_ in range(2):
            k.wait(k.y, "pe")
            ldn = k.dinc(k.y.dma_start(out=nwrow[:],
                                       in_=io["norm_w"][:, hf_ * 4096:
                                                        (hf_ + 1) * 4096]), "dX")
            k.wait(k.t, "dX", ldn)
            for c8 in range(8):
                k.wait(k.t, "ve")
                tp = k.inc(k.t.matmul(psb0[:, c8 % 2, :], ones1[:],
                                      nwrow[:, c8 * 512:(c8 + 1) * 512],
                                      start=True, stop=True), "pe")
                k.wait(k.v, "pe", tp)
                k.inc(k.v.tensor_copy(
                    nw_sb[:, hf_ * 4096 + c8 * 512:hf_ * 4096 + (c8 + 1) * 512],
                    psb0[:, c8 % 2, :]), "ve")

    xh = ARE[:, 0:36864].rearrange("p (a b) -> p a b", a=32)
    dmpA = ARE[:, 36864:40960]
    dmpB = ARE[:, 40960:45056]
    rtA = ARE[:, 45056:53248].bitcast(F32)
    xhtA = ARE[:, 53248:57344]
    emit_prelnT(k, lambda tt: io["r_in"][tt * 128:(tt + 1) * 128, :],
                xh, 9, rtA, xhtA, (dmpA, dmpB), stats, 0, epscol)

    k.join(k.g, ("ve", "dT", "se"))
    wbuf = ARE[:, 36864:45056].rearrange("p (x a b) -> p x a b", x=2, a=32)
    stg = ARE[:, 45056:46080].rearrange("p (x b) -> p x b", x=2)
    with nc.psum_tensor("psA", [128, 2, 512], F32) as psA:
        wl = {}

        def load_win(j):
            nm = "dW" if j % 2 == 0 else "dW2"
            w_ = 128 if j < 130 else 64
            wl[j] = (nm, k.dinc(k.g.dma_start(
                out=wbuf[:, j % 2, :, 0:w_],
                in_=io["winP"][:, j * 128:j * 128 + w_]
                .rearrange("(a p) b -> p a b", p=128)), nm))

        load_win(0)
        load_win(1)
        k.wait(k.t, "dT")
        evac = {}
        sdma = {}
        segs = [(128, 512), (640, 512)]
        segx = [(0, 512), (512, 512), (1024, 128)]
        u = 0
        for j in range(131):
            k.wait(k.t, wl[j][0], wl[j][1])
            isz = j < 64
            isx = 64 <= j < 130
            w_ = 128 if j < 130 else 64
            for (off, ln) in (segx if isx else segs):
                if u >= 2:
                    k.wait(k.t, "se", evac[u - 2])
                for ci in range(32):
                    mm = k.t.matmul(psA[0:w_, u % 2, 0:ln],
                                    wbuf[:, j % 2, ci, 0:w_],
                                    xh[:, ci, off:off + ln],
                                    start=(ci == 0), stop=(ci == 31))
                pj = k.inc(mm, "pe")
                k.wait(k.s, "pe", pj)
                if j == 130:
                    evac[u] = k.inc(k.s.activation(
                        dt_sp[:, off - 128:off - 128 + ln],
                        psA[0:64, u % 2, 0:ln], AF.Identity), "se")
                else:
                    if u >= 2 and (u - 2) in sdma:
                        k.wait(k.s, sdma[u - 2][0], sdma[u - 2][1])
                    evac[u] = k.inc(k.s.activation(stg[:, u % 2, 0:ln],
                                                   psA[:, u % 2, 0:ln],
                                                   AF.Identity), "se")
                    k.wait(k.y, "se", evac[u])
                    if isz:
                        dst = io["z_scr"][j, :, off - 128:off - 128 + ln]
                    else:
                        dst = io["xbc_scr"][j - 64, :, off:off + ln]
                    nmS = "dS" if u % 2 == 0 else "dS2"
                    sdma[u] = (nmS, k.dinc(k.y.dma_start(
                        out=dst, in_=stg[:, u % 2, 0:ln]), nmS))
                u += 1
            if j + 2 < 131:
                k.wait(k.g, "pe")
                load_win(j + 2)

    k.join(k.v, ("se",))
    dtt = ARE[:, 36864:45056].bitcast(F32)[0:64, 0:1024]
    prow = ARE[:, 45056:47232].bitcast(F32)[0:1, 0:1088]
    k.inc(k.v.tensor_scalar(dt_sp[:], dt_sp[:], dtb[:], None, op0=ALU.add), "ve")
    k.wait(k.v, "ve")
    k.inc(k.v.tensor_scalar(dtt[:], dt_sp[:], -1.0, None, op0=ALU.mult), "ve")
    k.wait(k.v, "ve")
    k.inc(k.v.tensor_tensor(dtt[:], dtt[:], dt_sp[:], op=ALU.min), "ve")
    k.wait(k.s, "ve")
    k.inc(k.s.activation(dtt[:], dtt[:], AF.Exp), "se")
    k.wait(k.s, "se")
    k.inc(k.s.activation(dtt[:], dtt[:], AF.Ln, bias=onecol[0:64, :]), "se")
    k.wait(k.v, "se")
    k.inc(k.v.tensor_scalar(dt_sp[:], dt_sp[:], 0.0, None, op0=ALU.max), "ve")
    k.wait(k.v, "ve")
    k.inc(k.v.tensor_tensor(dt_sp[:], dt_sp[:], dtt[:], op=ALU.add), "ve")
    k.wait(k.v, "ve")
    k.inc(k.v.tensor_scalar(dtt[:], dt_sp[:], alg[:], -1.0, op0=ALU.mult,
                            op1=ALU.mult), "ve")
    for ck in range(8):
        k.wait(k.v, "ve")
        k.inc(k.v.tensor_tensor_scan(cum[:, ck * 128:(ck + 1) * 128],
                                     dtt[:, ck * 128:(ck + 1) * 128],
                                     dtt[:, ck * 128:(ck + 1) * 128],
                                     0.0, op0=ALU.add, op1=ALU.bypass), "ve")
        k.wait(k.v, "ve")
        k.inc(k.v.tensor_copy(ctp[:, ck:ck + 1],
                              cum[:, ck * 128 + 127:ck * 128 + 128]), "ve")
    k.g.memset(ctp[0:64, 8:9], 0.0)
    k.inc(k.g.engine_nop(), "ge")
    k.wait(k.v, "ge")
    for ck in range(1, 9):
        k.wait(k.v, "ve")
        k.inc(k.v.tensor_tensor(ctp[:, 8 + ck:9 + ck], ctp[:, 7 + ck:8 + ck],
                                ctp[:, ck - 1:ck], op=ALU.add), "ve")
    k.wait(k.y, "ve")
    k.wait(k.y, "dS2")
    k.dinc(k.y.dma_start(out=io["cum_scr"].rearrange("h t -> (h t)"), in_=cum[:]),
           "dS")
    with nc.psum_tensor("psT3", [128, 2, 512], F32) as psT3, \
         nc.sbuf_tensor("ctpT", [17, 64], F32) as ctpT:
        k.wait(k.t, "ve")
        tpx = k.inc(k.t.transpose(psT3[0:17, 0, 0:64], ctp[0:64, 0:17],
                                  id_f32[0:64, 0:64]), "pe")
        k.wait(k.v, "pe", tpx)
        k.inc(k.v.tensor_copy(ctpT[:], psT3[0:17, 0, 0:64]), "ve")
        k.wait(k.y, "ve")
        k.dinc(k.y.dma_start(out=io["pf_scr"].rearrange("c h -> (c h)"),
                             in_=ctpT[:]), "dS")
        k.wait(k.y, "dS")
        k.dinc(k.y.dma_start(out=prow[:],
                             in_=io["pf_scr"].rearrange("c h -> (c h)")), "dX")
        for ck in range(8):
            k.wait(k.t, "ve")
            k.wait(k.t, "se")
            tp = k.inc(k.t.transpose(psT3[:, 0, 0:64],
                                     dt_sp[:, ck * 128:(ck + 1) * 128],
                                     id_f32[0:64, 0:64]), "pe")
            k.wait(k.v, "pe", tp)
            k.inc(k.v.tensor_copy(dtT[:, ck, :], psT3[:, 0, 0:64]), "ve")
            k.wait(k.t, "ve")
            tp = k.inc(k.t.transpose(psT3[:, 1, 0:64],
                                     cum[:, ck * 128:(ck + 1) * 128],
                                     id_f32[0:64, 0:64]), "pe")
            k.wait(k.v, "pe", tp)
            k.inc(k.v.tensor_scalar(ncumT[:, ck, :], psT3[:, 1, 0:64], -1.0, None,
                                    op0=ALU.mult), "ve")
            k.wait(k.s, "pe", tp)
            k.wait(k.s, "ve")
            k.inc(k.s.activation(ecumT[:, ck, :], psT3[:, 1, 0:64], AF.Exp), "se")
            k.wait(k.v, "se")
            k.inc(k.v.tensor_scalar(dtt[:, ck * 128:(ck + 1) * 128],
                                    cum[:, ck * 128:(ck + 1) * 128],
                                    ctp[:, ck:ck + 1], None, op0=ALU.subtract),
                  "ve")
            k.wait(k.s, "ve")
            k.inc(k.s.activation(dtt[:, ck * 128:(ck + 1) * 128],
                                 dtt[:, ck * 128:(ck + 1) * 128], AF.Exp,
                                 scale=-1.0), "se")
            k.wait(k.v, "se")
            k.inc(k.v.tensor_tensor(dtt[:, ck * 128:(ck + 1) * 128],
                                    dtt[:, ck * 128:(ck + 1) * 128],
                                    dt_sp[:, ck * 128:(ck + 1) * 128],
                                    op=ALU.mult), "ve")
            k.wait(k.t, "ve")
            tp = k.inc(k.t.transpose(psT3[:, 0, 64:128],
                                     dtt[:, ck * 128:(ck + 1) * 128],
                                     id_f32[0:64, 0:64]), "pe")
            k.wait(k.v, "pe", tp)
            k.inc(k.v.tensor_copy(wdtT[:, ck, :], psT3[:, 0, 64:128]), "ve")
            k.wait(k.t, "dX")
            k.wait(k.t, "se")
            tp = k.inc(k.t.matmul(psT3[:, 1, 64:128], ones1[:],
                                  prow[:, ck * 64:(ck + 1) * 64],
                                  start=True, stop=True), "pe")
            k.wait(k.s, "pe", tp)
            k.inc(k.s.activation(decT[:, ck, :], psT3[:, 1, 64:128], AF.Exp), "se")
            k.wait(k.t, "se")
            tp = k.inc(k.t.matmul(psT3[:, 1, 64:128], ones1[:],
                                  prow[:, (8 + ck) * 64:(9 + ck) * 64],
                                  start=True, stop=True), "pe")
            k.wait(k.s, "pe", tp)
            k.inc(k.s.activation(pfT[:, ck, :], psT3[:, 1, 64:128], AF.Exp), "se")
            k.wait(k.v, "se")
            k.inc(k.v.tensor_tensor(pfT[:, ck, :], pfT[:, ck, :], ecumT[:, ck, :],
                                    op=ALU.mult), "ve")
            k.wait(k.t, "se")
            k.wait(k.t, "ve")
    k.wait(k.y, "ve")
    k.dinc(k.y.dma_start(out=io["ct_in"], in_=ctp[0:64, 16:17]), "dS")

    cnv_in = ARE[:, 0:8646].rearrange("p (a b) -> p a b", a=66)
    cnv_f = ARE[:, 8704:17152].rearrange("p (a b) -> p a b", a=66)
    cnv_t = ARE[:, 17152:25600].rearrange("p (a b) -> p a b", a=66)
    xbc_s = ARE[:, 25600:34048].rearrange("p (a b) -> p a b", a=66)
    xT_sb = ARE[:, 34048:42240].rearrange("p (a b) -> p a b", a=64)
    yA = ARE[:, 42240:50432].rearrange("p (a b) -> p a b", a=64)
    cumrow = ARE[:, 50432:66816].bitcast(F32)[0:1, 0:8192]
    with nc.sbuf_tensor("BT_sb", [128, 128], BF16) as BT_sb, \
         nc.sbuf_tensor("Bw_sb", [128, 128], BF16) as Bw_sb, \
         nc.sbuf_tensor("cbt_m", [128, 128], BF16) as cbt_m, \
         nc.sbuf_tensor("E_sb", [128, 128], BF16) as E_sb, \
         nc.sbuf_tensor("MT_sb", [128, 128], BF16) as MT_sb, \
         nc.sbuf_tensor("u_tmp", [128, 128], F32) as u_tmp, \
         nc.psum_tensor("psB", [128, 5, 512], F32) as psB, \
         nc.psum_tensor("psBt", [128, 128], BF16) as psBt:
        for ck in range(8):
            k.wait(k.y, "ve")
            k.wait(k.y, "se")
            k.wait(k.y, "dS")
            k.wait(k.y, "dS2")
            ld = k.dinc(k.y.dma_start(
                out=cnv_in,
                in_=io["xbc_scr"].rearrange("j p t -> p j t")[:, :,
                                                              125 + ck * 128:
                                                              256 + ck * 128]),
                "dX")
            ldc = k.dinc(k.y.dma_start(
                out=cumrow[:],
                in_=io["cum_scr"].rearrange("h (c t) -> c h t", c=8)[ck]), "dX2")
            k.wait(k.v, "dX", ld)
            k.wait(k.v, "pe")
            k.inc(k.v.tensor_tensor(cnv_f, cnv_in[:, :, 0:128],
                                    cw_sb[:, :, 0].broadcast_to([128, 66, 128]),
                                    op=ALU.mult), "ve")
            for tap in range(1, 4):
                k.wait(k.v, "ve")
                k.inc(k.v.tensor_tensor(cnv_t, cnv_in[:, :, tap:tap + 128],
                                        cw_sb[:, :, tap]
                                        .broadcast_to([128, 66, 128]),
                                        op=ALU.mult), "ve")
                k.wait(k.v, "ve")
                k.inc(k.v.tensor_tensor(cnv_f, cnv_f, cnv_t, op=ALU.add), "ve")
            k.wait(k.s, "ve")
            k.wait(k.s, "pe")
            sil = k.inc(k.s.activation(xbc_s, cnv_f, AF.Silu), "se")
            k.inc(k.s.activation(C_full[:, ck, :], cnv_f[:, 65, :], AF.Silu), "se")
            k.wait(k.t, "se")
            k.wait(k.t, "dV")
            tp = k.inc(k.t.transpose(psBt[:], xbc_s[:, 64, :], id_bf[:]), "pe")
            k.wait(k.v, "pe", tp)
            k.inc(k.v.tensor_copy(BT_sb[:], psBt[:]), "ve")
            mm = k.inc(k.t.matmul(psB[:, 0, 0:128], xbc_s[:, 64, :],
                                  xbc_s[:, 65, :], start=True, stop=True), "pe")
            k.wait(k.v, "pe", mm)
            k.inc(k.v.tensor_tensor(cbt_m[:], psB[:, 0, 0:128], ltmask[:],
                                    op=ALU.mult), "ve")
            for h in range(64):
                k.wait(k.t, "ve")
                tp = k.inc(k.t.transpose(psBt[:], xbc_s[:, h, :], id_bf[:]), "pe")
                k.wait(k.v, "pe", tp)
                k.inc(k.v.tensor_copy(xT_sb[:, h, :], psBt[:]), "ve")
            k.wait(k.t, "dX2", ldc)
            for h in range(64):
                k.wait(k.t, "se")
                gm = k.inc(k.t.matmul(psB[:, 1, 0:128], ones1[:],
                                      cumrow[:, h * 128:(h + 1) * 128],
                                      start=True, stop=True), "pe")
                k.wait(k.v, "pe", gm)
                k.wait(k.v, "se")
                k.inc(k.v.tensor_scalar(E_sb[:], psB[:, 1, 0:128],
                                        ncumT[:, ck, h:h + 1], 0.0,
                                        op0=ALU.add, op1=ALU.min), "ve")
                k.wait(k.s, "ve")
                k.inc(k.s.activation(E_sb[:], E_sb[:], AF.Exp), "se")
                k.wait(k.v, "se")
                k.inc(k.v.scalar_tensor_tensor(MT_sb[:], E_sb[:],
                                               dtT[:, ck, h:h + 1], cbt_m[:],
                                               op0=ALU.mult, op1=ALU.mult), "ve")
                k.wait(k.t, "ve")
                mi = k.inc(k.t.matmul(psB[:, 2, 0:128], MT_sb[:], xT_sb[:, h, :],
                                      start=True, stop=True), "pe")
                mj = k.inc(k.t.matmul(psB[:, 3, 0:128], C_full[:, ck, :],
                                      h_sb[:, h, :], start=True, stop=True), "pe")
                k.wait(k.v, "ve")
                k.inc(k.v.tensor_scalar(Bw_sb[:], BT_sb[:], wdtT[:, ck, h:h + 1],
                                        None, op0=ALU.mult), "ve")
                k.wait(k.t, "ve")
                ms = k.inc(k.t.matmul(psB[:, 4, 0:128], Bw_sb[:], xT_sb[:, h, :],
                                      start=True, stop=True), "pe")
                k.wait(k.v, "pe", mi)
                k.inc(k.v.scalar_tensor_tensor(u_tmp[:], xT_sb[:, h, :],
                                               Dcol[:, h:h + 1], psB[:, 2, 0:128],
                                               op0=ALU.mult, op1=ALU.add), "ve")
                k.wait(k.v, "pe", mj)
                k.wait(k.v, "ve")
                k.inc(k.v.scalar_tensor_tensor(yA[:, h, :], psB[:, 3, 0:128],
                                               ecumT[:, ck, h:h + 1], u_tmp[:],
                                               op0=ALU.mult, op1=ALU.add), "ve")
                k.wait(k.v, "pe", ms)
                k.inc(k.v.scalar_tensor_tensor(h_sb[:, h, :], h_sb[:, h, :],
                                               decT[:, ck, h:h + 1],
                                               psB[:, 4, 0:128],
                                               op0=ALU.mult, op1=ALU.add), "ve")
                k.wait(k.t, "ve")
            k.wait(k.y, "ve")
            k.dinc(k.y.dma_start(out=io["ypre_scr"][ck],
                                 in_=yA.rearrange("p a b -> p (a b)")), "dS")
            k.wait(k.v, "dS")

    k.wait(k.y, "ve")
    k.wait(k.y, "dS")
    k.wait(k.y, "dS2")
    k.dinc(k.y.dma_start(out=io["st_in"].rearrange("n h pp -> n (h pp)"),
                         in_=h_sb.rearrange("p a b -> p (a b)")), "dS")
    k.wait(k.g, "dS")
    cc1 = k.inc(k.g.collective_compute(
        "AllGather", ALU.bypass, replica_groups=groups,
        ins=[io["st_in"].rearrange("n h pp -> n (h pp)")],
        outs=[io["st_out"][0:W].rearrange("c n hp -> (c n) hp")]), "cc")
    cc2 = k.inc(k.g.collective_compute(
        "AllGather", ALU.bypass, replica_groups=groups,
        ins=[io["ct_in"]], outs=[io["ct_out"][0:W].rearrange("c h -> (c h)")]), "cc")
    k.wait(k.g, "cc", cc2)
    k.g.memset(gct[:], 0.0)
    k.inc(k.g.engine_nop(), "ge")
    k.wait(k.g, "ge")
    k.dinc(k.g.dma_start(out=gct[0:W, :], in_=io["ct_out"][0:W]), "dV")
    Hf = ARE[:, 34048:50432].rearrange("p (a b c) -> p a b c", a=2, b=64)
    with nc.psum_tensor("psC", [128, 2, 512], F32) as psC:
        k.wait(k.t, "dV")
        mm = k.inc(k.t.matmul(psC[0:8, 0, 0:64], wmsk[:], gct[:],
                              start=True, stop=True), "pe")
        k.wait(k.s, "pe", mm)
        k.inc(k.s.activation(gct[:], psC[0:8, 0, 0:64], AF.Exp), "se")
        k.wait(k.v, "se")
        k.inc(k.v.tensor_scalar(gct[:], gct[:], selc[:], None, op0=ALU.mult), "ve")
        k.wait(k.y, "ve")
        k.dinc(k.y.dma_start(out=io["w_scr"].rearrange("c h -> (c h)"), in_=gct[:]),
               "dS")
        Sg = ARE[:, 0:16384].rearrange("p (x a b) -> p x a b", x=2, a=64)
        with nc.sbuf_tensor("wrow", [1, 512], F32) as wrow:
            k.wait(k.y, "dS")
            k.dinc(k.y.dma_start(out=wrow[:],
                                 in_=io["w_scr"].rearrange("c h -> (c h)")), "dX")
            k.wait(k.t, "dX")
            for cc_ in range(8):
                k.wait(k.t, "ve")
                k.wait(k.t, "se")
                mm = k.inc(k.t.matmul(psC[:, cc_ % 2, 0:64], ones1[:],
                                      wrow[:, cc_ * 64:(cc_ + 1) * 64],
                                      start=True, stop=True), "pe")
                k.wait(k.v, "pe", mm)
                k.inc(k.v.tensor_copy(wco[:, cc_, :], psC[:, cc_ % 2, 0:64]), "ve")
            k.g.memset(Hf[:, 0], 0.0)
            k.inc(k.g.engine_nop(), "ge")
            k.wait(k.g, "cc", cc1)
            lds = {}

            def load_sg(cc_):
                nm = "dG" if cc_ % 2 == 0 else "dG2"
                lds[cc_] = (nm, k.dinc(k.g.dma_start(
                    out=Sg[:, cc_ % 2].rearrange("p a b -> p (a b)"),
                    in_=io["st_out"][cc_]), nm))

            load_sg(0)
            if W > 1:
                load_sg(1)
            for cc_ in range(W):
                k.wait(k.v, lds[cc_][0], lds[cc_][1])
                k.wait(k.v, "ge")
                for h in range(64):
                    k.wait(k.v, "ve")
                    k.inc(k.v.scalar_tensor_tensor(
                        Hf[:, (cc_ + 1) % 2, h, :], Sg[:, cc_ % 2, h, :],
                        wco[:, cc_, h:h + 1], Hf[:, cc_ % 2, h, :],
                        op0=ALU.mult, op1=ALU.add), "ve")
                if cc_ + 2 < W:
                    k.wait(k.g, "ve")
                    load_sg(cc_ + 2)
    Hfin = Hf[:, W % 2]

    yF = ARE[:, 0:32768].rearrange("p (a b) -> p a b", a=64)
    zT = ARE[:, 50432:58624]
    yB2 = ARE[:, 58624:66816].rearrange("p (a b) -> p a b", a=64)
    wbuf3 = ARE[:, 50432:66816].rearrange("p (x a b) -> p x a b", x=2, a=64)
    with nc.sbuf_tensor("fst", [128, 2, 512], BF16) as fst, \
         nc.psum_tensor("psD", [128, 4, 512], F32) as psD:
        evg = {}
        fdma = {}
        for half in range(2):
            for ck4 in range(4):
                ck = half * 4 + ck4
                k.wait(k.y, "pe")
                k.wait(k.y, "ve")
                k.wait(k.y, "se")
                ldy = k.dinc(k.y.dma_start(
                    out=yF[:, :, ck4 * 128:(ck4 + 1) * 128],
                    in_=io["ypre_scr"][ck].rearrange("p (a b) -> p a b", a=64)),
                    "dX")
                k.wait(k.s, "pe")
                k.wait(k.s, "dT")
                ldz = k.dinc(k.s.dma_start(
                    out=zT.rearrange("p f -> p f"),
                    in_=io["z_scr"].rearrange("j p t -> (j p) t")
                    [:, ck * 128:(ck + 1) * 128], transpose=True), "dT")
                k.wait(k.v, "dX", ldy)
                for h in range(64):
                    k.wait(k.t, "ve")
                    mm = k.inc(k.t.matmul(psD[:, h % 2, 0:128], C_full[:, ck, :],
                                          Hfin[:, h, :], start=True, stop=True),
                               "pe")
                    k.wait(k.v, "pe", mm)
                    k.inc(k.v.scalar_tensor_tensor(
                        yB2[:, h, :], psD[:, h % 2, 0:128], pfT[:, ck, h:h + 1],
                        yF[:, h, ck4 * 128:(ck4 + 1) * 128],
                        op0=ALU.mult, op1=ALU.add), "ve")
                scr = yF[:, :, ck4 * 128:(ck4 + 1) * 128]
                k.wait(k.s, "dT", ldz)
                k.wait(k.s, "ve")
                sil = k.inc(k.s.activation(scr, zT.rearrange("p (a b) -> p a b",
                                                             a=64), AF.Silu),
                            "se")
                k.wait(k.v, "se", sil)
                k.inc(k.v.tensor_tensor(zT.rearrange("p (a b) -> p a b", a=64),
                                        yB2, scr, op=ALU.mult), "ve")
                k.wait(k.s, "ve")
                k.inc(k.s.activation(scr, zT.rearrange("p (a b) -> p a b", a=64),
                                     AF.Square, accum_out=stats[:, 80:81]), "se")
                k.wait(k.v, "se")
                k.wait(k.v, "ve")
                k.inc(k.v.tensor_scalar(stats[:, 81:82], stats[:, 80:81],
                                        1.0 / DSSM, None, op0=ALU.mult), "ve")
                k.wait(k.s, "ve")
                k.inc(k.s.activation(stats[:, 81:82], stats[:, 81:82], AF.Ln,
                                     bias=epscol[:]), "se")
                k.wait(k.s, "se")
                k.inc(k.s.activation(stats[:, 81:82], stats[:, 81:82], AF.Exp,
                                     scale=-0.5), "se")
                k.wait(k.v, "se")
                k.inc(k.v.scalar_tensor_tensor(
                    zT.rearrange("p (a b) -> p a b", a=64),
                    zT.rearrange("p (a b) -> p a b", a=64),
                    stats[:, 81:82], nw_sb.rearrange("p (a b) -> p a b", a=64),
                    op0=ALU.mult, op1=ALU.mult), "ve")
                k.wait(k.s, "ve")
                k.dinc(k.s.dma_start(out=yF[:, :, ck4 * 128:(ck4 + 1) * 128],
                                     in_=zT, transpose=True), "dT")
                k.wait(k.v, "dT")
                k.wait(k.t, "dT")
            wl3 = {}

            def load_wo(j):
                nm = "dW" if j % 2 == 0 else "dW2"
                wl3[j] = (nm, k.dinc(k.g.dma_start(
                    out=wbuf3[:, j % 2],
                    in_=io["woutP"][:, j * 128:(j + 1) * 128]
                    .rearrange("(a p) b -> p a b", p=128)), nm))

            k.join(k.g, ("pe", "ve", "se", "dT"))
            load_wo(0)
            load_wo(1)
            for j in range(32):
                g_ = half * 32 + j
                k.wait(k.t, wl3[j][0], wl3[j][1])
                if g_ >= 2:
                    k.wait(k.t, "se", evg[g_ - 2])
                for ci in range(64):
                    mm = k.t.matmul(psD[:, 2 + j % 2, :],
                                    wbuf3[:, j % 2, ci, :], yF[:, ci, :],
                                    start=(ci == 0), stop=(ci == 63))
                pj = k.inc(mm, "pe")
                k.wait(k.s, "pe", pj)
                if g_ >= 2:
                    k.wait(k.s, fdma[g_ - 2][0], fdma[g_ - 2][1])
                evg[g_] = k.inc(k.s.activation(fst[:, j % 2, :],
                                               psD[:, 2 + j % 2, :],
                                               AF.Identity), "se")
                k.wait(k.y, "se", evg[g_])
                nmF = "dS" if j % 2 == 0 else "dS2"
                fdma[g_] = (nmF, k.dinc(k.y.dma_start(
                    out=io["f_scr"][j, :, half * 512:half * 512 + 512],
                    in_=fst[:, j % 2, :]), nmF))
                if j + 2 < 32:
                    k.wait(k.g, "pe")
                    load_wo(j + 2)

    fT2 = ARE[:, 0:32768].rearrange("p (a b) -> p a b", a=8)
    rtru = ARE[:, 32768:49152].bitcast(F32)
    rt4 = rtru[:, 0:4096]
    ru4 = rtru[:, 4096:8192]
    dmpC = ARE[:, 49152:53248]
    dmpD = ARE[:, 53248:57344]
    k.join(k.y, ("pe", "se", "dS", "dS2", "ve"))
    k.join(k.s, ("dS", "dS2", "ve", "pe"))
    ldf = k.dinc(k.s.dma_start(out=fT2,
                               in_=io["f_scr"].rearrange("j p t -> (j p) t"),
                               transpose=True), "dT")
    for tt in range(8):
        k.wait(k.y, "dS")
        k.wait(k.y, "ve")
        k.dinc(k.y.dma_start(out=rt4,
                             in_=io["r_in"][128 + tt * 128:256 + tt * 128, :]),
               "dX")
        fview = fT2[:, tt, :]
        c0 = 48 + 4 * tt
        k.wait(k.v, "dT", ldf)
        k.wait(k.v, "dX")
        k.inc(k.v.tensor_scalar(dmpC, fview, 1.0, 0.0, op0=ALU.mult,
                                op1=ALU.add, accum_out=stats[:, c0:c0 + 1]), "ve")
        k.wait(k.s, "dT", ldf)
        k.wait(k.s, "ve")
        k.inc(k.s.activation(dmpD, fview, AF.Square,
                             accum_out=stats[:, c0 + 3:c0 + 4]), "se")
        mcol = stats[:, c0 + 1:c0 + 2]
        vcol = stats[:, c0 + 2:c0 + 3]
        k.wait(k.v, "se")
        k.wait(k.v, "ve")
        k.inc(k.v.tensor_scalar(mcol, stats[:, c0:c0 + 1], 1.0 / D, None,
                                op0=ALU.mult), "ve")
        k.wait(k.v, "ve")
        k.inc(k.v.tensor_tensor(vcol, mcol, mcol, op=ALU.mult), "ve")
        k.wait(k.v, "ve")
        k.inc(k.v.scalar_tensor_tensor(vcol, stats[:, c0 + 3:c0 + 4], 1.0 / D,
                                       vcol, op0=ALU.mult, op1=ALU.subtract),
              "ve")
        k.wait(k.s, "ve")
        k.inc(k.s.activation(vcol, vcol, AF.Ln, bias=epscol[:]), "se")
        k.wait(k.s, "se")
        k.inc(k.s.activation(vcol, vcol, AF.Exp, scale=-0.5), "se")
        k.wait(k.v, "se")
        k.inc(k.v.tensor_scalar(ru4, fview, mcol, vcol, op0=ALU.subtract,
                                op1=ALU.mult), "ve")
        k.wait(k.v, "ve")
        k.inc(k.v.tensor_tensor(ru4, ru4, rt4, op=ALU.add), "ve")
        k.wait(k.y, "ve")
        k.dinc(k.y.dma_start(out=io["r_out"][tt * 128:(tt + 1) * 128, :],
                             in_=ru4), "dS")
    k.wait(k.y, "dS")


def _build_mamba(groups=None):
    nc = bass.Bass("TRN2", target_bir_lowering=False)
    io = {}
    io["r_in"] = nc.dram_tensor("r_in", [TH, D], F32, kind="ExternalInput").ap()
    io["winP"] = nc.dram_tensor("winP", [D, 16704], F32, kind="ExternalInput").ap()
    io["woutP"] = nc.dram_tensor("woutP", [DSSM, D], F32, kind="ExternalInput").ap()
    io["conv_w"] = nc.dram_tensor("conv_w", [CONVD, 4], F32, kind="ExternalInput").ap()
    io["dt_bias"] = nc.dram_tensor("dt_bias", [64, 1], F32, kind="ExternalInput").ap()
    io["A_log"] = nc.dram_tensor("A_log", [64, 1], F32, kind="ExternalInput").ap()
    io["D_param"] = nc.dram_tensor("D_param", [1, 64], F32, kind="ExternalInput").ap()
    io["norm_w"] = nc.dram_tensor("norm_w", [1, DSSM], F32, kind="ExternalInput").ap()
    io["ident"] = nc.dram_tensor("ident", [128, 128], F32, kind="ExternalInput").ap()
    io["ltmask"] = nc.dram_tensor("ltmask", [128, 128], F32, kind="ExternalInput").ap()
    io["wmask"] = nc.dram_tensor("wmask", [8, 8], F32, kind="ExternalInput").ap()
    io["selmask"] = nc.dram_tensor("selmask", [8, 1], F32, kind="ExternalInput").ap()
    io["r_out"] = nc.dram_tensor("r_out", [T, D], F32, kind="ExternalOutput").ap()
    io["z_scr"] = nc.dram_tensor("z_scr", [64, 128, T], BF16).ap()
    io["xbc_scr"] = nc.dram_tensor("xbc_scr", [66, 128, TH], BF16).ap()
    io["ypre_scr"] = nc.dram_tensor("ypre_scr", [8, 128, DSSM], BF16).ap()
    io["f_scr"] = nc.dram_tensor("f_scr", [32, 128, T], BF16).ap()
    io["cum_scr"] = nc.dram_tensor("cum_scr", [64, T], F32).ap()
    io["pf_scr"] = nc.dram_tensor("pf_scr", [17, 64], F32).ap()
    io["w_scr"] = nc.dram_tensor("w_scr", [8, 64], F32).ap()
    io["st_in"] = nc.dram_tensor("st_in", [128, 64, 128], BF16).ap()
    io["st_out"] = nc.dram_tensor("st_out", [8, 128, 64 * 128], BF16).ap()
    io["ct_in"] = nc.dram_tensor("ct_in", [64], F32).ap()
    io["ct_out"] = nc.dram_tensor("ct_out", [8, 64], F32).ap()
    with ExitStack() as es:
        def sb(name, shape, dt):
            return es.enter_context(nc.sbuf_tensor(name, shape, dt))
        k = Kx(nc, es, SEMS)
        block = es.enter_context(nc.Block())

        @block.sync
        def _(sy):
            _emit_mamba(k, es, sb, nc, io, groups=groups)
    return nc


def _run_mamba(p, R):
    nc = _build_mamba()
    lt = np.tril(np.ones((128, 128), np.float32)).T
    maps = []
    for c in range(NC):
        rin = np.zeros((TH, D), np.float32)
        rin[128:] = R[c * T:(c + 1) * T]
        if c > 0:
            rin[125:128] = R[c * T - 3:c * T]
        wm = np.zeros((8, 8), np.float32)
        for cp in range(8):
            for cpp in range(cp + 1, c):
                wm[cpp, cp] = 1.0
        sel = np.zeros((8, 1), np.float32)
        sel[:c, 0] = 1.0
        maps.append(dict(
            r_in=rin, winP=p["winP"], woutP=p["woutP"], conv_w=p["conv_w"],
            dt_bias=np.ascontiguousarray(p["dt_bias"].reshape(64, 1)),
            A_log=np.ascontiguousarray(p["A_log"].reshape(64, 1)),
            D_param=p["D_param"], norm_w=p["norm_w"], ident=p["ident"],
            ltmask=np.ascontiguousarray(lt), wmask=wm, selmask=sel))
    res = run_bass_kernel_spmd(nc, maps, core_ids=list(range(NC)))
    return np.concatenate([res.results[c]["r_out"] for c in range(NC)], axis=0)





class _NCProxy:
    def __init__(self, nc, prefix):
        self._nc = nc
        self._p = prefix

    def sbuf_tensor(self, name, shape, dt):
        return self._nc.sbuf_tensor(self._p + name, shape, dt)

    def psum_tensor(self, name, shape, dt):
        return self._nc.psum_tensor(self._p + name, shape, dt)

    def __getattr__(self, a):
        return getattr(self._nc, a)


def _full_join(k):
    for eng in (k.t, k.v, k.s, k.g, k.y):
        k.join(eng, ("pe", "ve", "se", "ge", "dW", "dW2", "dV", "dX", "dX2",
                     "dG", "dG2", "dT", "dS", "dS2", "cc"))


def _emit_halo_exchange(k, es, sb, nc, io):
    zrows = sb("zrows", [128, D], F32)
    ohot = sb("ohot", [8, 1], F32)
    halv = sb("halv", [8, 3 * D], F32)
    hsel = sb("hsel", [1, 3 * D], F32)
    k.g.memset(zrows[:], 0.0)
    k.inc(k.g.engine_nop(), "ge")
    k.wait(k.y, "ge")
    k.dinc(k.y.dma_start(out=io["r2"][0:125, :], in_=zrows[0:125, :]), "dS")
    k.dinc(k.y.dma_start(out=io["ohot"] if False else ohot[:],
                         in_=io["onehot"]), "dX")
    k.wait(k.y, "dS")
    k.dinc(k.y.dma_start(out=io["hal_in"],
                         in_=io["r2"][1149:1152, :].rearrange("a b -> (a b)")),
           "dS")
    k.wait(k.g, "dS")
    ccx = k.inc(k.g.collective_compute(
        "AllGather", ALU.bypass, replica_groups=[list(range(NC))],
        ins=[io["hal_in"]], outs=[io["hal_out"].rearrange("c f -> (c f)")]), "cc")
    k.wait(k.g, "cc", ccx)
    k.dinc(k.g.dma_start(out=halv[:], in_=io["hal_out"]), "dG")
    with nc.psum_tensor("psH", [128, 512], F32) as psH:
        k.wait(k.t, "dG")
        k.wait(k.t, "dX")
        pes = []
        for i6 in range(24):
            mm = k.t.matmul(psH[0:1, :], ohot[:],
                            halv[:, i6 * 512:(i6 + 1) * 512],
                            start=True, stop=True)
            if i6 % 4 == 3 or i6 == 23:
                pass
            pes.append(k.inc(mm, "pe"))
            k.wait(k.v, "pe", pes[-1])
            k.inc(k.v.tensor_copy(hsel[:, i6 * 512:(i6 + 1) * 512], psH[0:1, :]),
                  "ve")
            k.wait(k.t, "ve")
        k.wait(k.y, "ve")
        k.dinc(k.y.dma_start(out=io["r2"][125:128, :].rearrange("a b -> (a b)"),
                             in_=hsel[:]), "dS")
    k.wait(k.y, "dS")


def _build_fused():
    nc = bass.Bass("TRN2", target_bir_lowering=False)
    io = {}

    def din(name, shape):
        io[name] = nc.dram_tensor(name, shape, F32, kind="ExternalInput").ap()

    din("xin", [XT, D])
    din("wqT", [D, D]); din("wkT", [D, DH]); din("wvT", [D, DH]); din("woT", [D, D])
    din("cosT", [128, XT]); din("sinrotT", [128, XT]); din("prot", [128, 128])
    din("ident", [128, 128]); din("maskneg", [128, NKT]); din("uw", [64, T])
    din("up", [16, 256, 1024]); din("down", [16, 1024, 256])
    din("winP", [D, 16704]); din("woutP", [DSSM, D]); din("conv_w", [CONVD, 4])
    din("dt_bias", [64, 1]); din("A_log", [64, 1]); din("D_param", [1, 64])
    din("norm_w", [1, DSSM]); din("ltmask", [128, 128]); din("wmask", [8, 8])
    din("selmask", [8, 1]); din("onehot", [8, 1])
    din("wbT", [D, D]); din("wrT", [D, 8]); din("laT", [D, 128])
    din("lbT", [128, D]); din("bias_sb", [128, 32])
    io["out"] = nc.dram_tensor("out", [T, D], F32, kind="ExternalOutput").ap()
    io["r1"] = nc.dram_tensor("r1", [T, D], F32).ap()
    io["r2"] = nc.dram_tensor("r2", [TH, D], F32).ap()
    io["r3"] = nc.dram_tensor("r3", [T, D], F32).ap()
    io["rs_scr"] = nc.dram_tensor("rs_scr", [2, 8, 4, 128], BF16).ap()
    io["z_scr"] = nc.dram_tensor("z_scr", [64, 128, T], BF16).ap()
    io["xbc_scr"] = nc.dram_tensor("xbc_scr", [66, 128, TH], BF16).ap()
    io["ypre_scr"] = nc.dram_tensor("ypre_scr", [8, 128, DSSM], BF16).ap()
    io["f_scr"] = nc.dram_tensor("f_scr", [32, 128, T], BF16).ap()
    io["cum_scr"] = nc.dram_tensor("cum_scr", [64, T], F32).ap()
    io["pf_scr"] = nc.dram_tensor("pf_scr", [17, 64], F32).ap()
    io["w_scr"] = nc.dram_tensor("w_scr", [8, 64], F32).ap()
    io["st_in"] = nc.dram_tensor("st_in", [128, 64, 128], BF16).ap()
    io["st_out"] = nc.dram_tensor("st_out", [8, 128, 64 * 128], BF16).ap()
    io["ct_in"] = nc.dram_tensor("ct_in", [64], F32).ap()
    io["ct_out"] = nc.dram_tensor("ct_out", [8, 64], F32).ap()
    io["hal_in"] = nc.dram_tensor("hal_in", [3 * D], F32).ap()
    io["hal_out"] = nc.dram_tensor("hal_out", [8, 3 * D], F32).ap()

    with ExitStack() as es:
        k = Kx(nc, es, SEMS)
        block = es.enter_context(nc.Block())

        @block.sync
        def _(sy):
            def run(tag, fn, sub_io):
                ncp = _NCProxy(nc, tag)
                with ExitStack() as es2:
                    def sb2(name, shape, dt):
                        return es2.enter_context(
                            nc.sbuf_tensor(tag + name, shape, dt))
                    fn(k, es2, sb2, ncp, sub_io)
                _full_join(k)

            run("A_", _emit_l0attn, dict(
                xin=io["xin"], wqT=io["wqT"], wkT=io["wkT"], wvT=io["wvT"],
                woT=io["woT"], cosT=io["cosT"], sinrotT=io["sinrotT"],
                prot=io["prot"], ident=io["ident"], maskneg=io["maskneg"],
                uw=io["uw"], r_out=io["r1"], rs_scr=io["rs_scr"]))
            run("B_", _emit_l0ffn, dict(
                r_in=io["r1"], up=io["up"], down=io["down"],
                r_out=io["r2"][128:TH, :]))
            run("H_", _emit_halo_exchange, dict(
                r2=io["r2"], onehot=io["onehot"], hal_in=io["hal_in"],
                hal_out=io["hal_out"]))
            run("M_", _emit_mamba, dict(
                r_in=io["r2"], winP=io["winP"], woutP=io["woutP"],
                conv_w=io["conv_w"], dt_bias=io["dt_bias"], A_log=io["A_log"],
                D_param=io["D_param"], norm_w=io["norm_w"], ident=io["ident"],
                ltmask=io["ltmask"], wmask=io["wmask"], selmask=io["selmask"],
                r_out=io["r3"], z_scr=io["z_scr"], xbc_scr=io["xbc_scr"],
                ypre_scr=io["ypre_scr"], f_scr=io["f_scr"],
                cum_scr=io["cum_scr"], pf_scr=io["pf_scr"], w_scr=io["w_scr"],
                st_in=io["st_in"], st_out=io["st_out"], ct_in=io["ct_in"],
                ct_out=io["ct_out"]))
            run("F_", _emit_l1ffn, dict(
                r_in=io["r3"], wbT=io["wbT"], wrT=io["wrT"], laT=io["laT"],
                lbT=io["lbT"], bias_sb=io["bias_sb"], ident=io["ident"],
                r_out=io["out"]))
    return nc


_FUSED_CACHE = {}


def _run_fused(p):
    nc = _build_fused()
    lt = np.tril(np.ones((128, 128), np.float32)).T
    bias128 = np.ascontiguousarray(p["bias_p"].reshape(32, 128).T)
    R = p["x_p"]
    maps = []
    for c in range(NC):
        g0 = c * T - HALO
        xin = np.zeros((XT, D), dtype=np.float32)
        lo = max(0, g0); hi = min(S, g0 + XT)
        xin[lo - g0: hi - g0] = R[lo:hi]
        wm = np.zeros((8, 8), np.float32)
        for cp in range(8):
            for cpp in range(cp + 1, c):
                wm[cpp, cp] = 1.0
        sel = np.zeros((8, 1), np.float32)
        sel[:c, 0] = 1.0
        oh = np.zeros((8, 1), np.float32)
        if c > 0:
            oh[c - 1, 0] = 1.0
        maps.append(dict(
            xin=xin, wqT=p["wqT"], wkT=p["wkT"], wvT=p["wvT"], woT=p["woT"],
            cosT=p["cosT"][c], sinrotT=p["sinrotT"][c], prot=p["P_rot"],
            ident=p["ident"], maskneg=p["maskneg"][c], uw=p["uw"][c],
            up=p["up"], down=p["down"],
            winP=p["winP"], woutP=p["woutP"], conv_w=p["conv_w"],
            dt_bias=np.ascontiguousarray(p["dt_bias"].reshape(64, 1)),
            A_log=np.ascontiguousarray(p["A_log"].reshape(64, 1)),
            D_param=p["D_param"], norm_w=p["norm_w"],
            ltmask=np.ascontiguousarray(lt), wmask=wm, selmask=sel, onehot=oh,
            wbT=p["wbT"], wrT=p["wrT"], laT=p["laT"], lbT=p["lbT"],
            bias_sb=bias128))
    res = run_bass_kernel_spmd(nc, maps, core_ids=list(range(NC)))
    return np.concatenate([res.results[c]["out"] for c in range(NC)], axis=0)


def kernel(**inputs):
    p = _prep(inputs)
    R = _run_fused(p)
    out = np.empty((S, D), dtype=np.float32)
    out[:, p["perm"]] = R
    return out.reshape(1, S, D)


def kernel_unfused(**inputs):
    p = _prep(inputs)
    R = p["x_p"]
    R = _run_l0attn(p, R)
    R = _run_l0ffn(p, R)
    R = _run_mamba(p, R)
    R = _run_l1ffn(p, R)
    out = np.empty((S, D), dtype=np.float32)
    out[:, p["perm"]] = R
    return out.reshape(1, S, D)


# version 04
# speedup vs baseline: 1.1819x; 1.1819x over previous
import numpy as np
import ml_dtypes
from contextlib import ExitStack
import concourse.bass as bass
import bass_rust
import concourse.mybir as mybir
from concourse.bass_utils import run_bass_kernel_spmd

F32 = mybir.dt.float32
BF16 = mybir.dt.bfloat16
AF = mybir.ActivationFunctionType
ALU = mybir.AluOpType
BF = ml_dtypes.bfloat16

NC = 8
S = 8192
D = 4096
T = S // NC
HALO = 256
XT = T + 2 * HALO
NKT = XT // 128
DH = 128
EPS_LN = 1e-5
SCALE0 = 1.0 / float(np.sqrt(DH))



def _prep(I):
    p = {}
    perm = np.asarray(I["L0_ffn_perm"]).astype(np.int64)
    p["perm"] = perm

    x = np.asarray(I["L0_attn_x"]).reshape(S, D)
    p["x_p"] = np.ascontiguousarray(x[:, perm])

    p["wqT"] = np.ascontiguousarray(np.asarray(I["L0_attn_Wq"]).T[perm, :])
    p["wkT"] = np.ascontiguousarray(np.asarray(I["L0_attn_Wk"]).T[perm, :])
    p["wvT"] = np.ascontiguousarray(np.asarray(I["L0_attn_Wv"]).T[perm, :])
    p["woT"] = np.ascontiguousarray(np.asarray(I["L0_attn_Wo"])[perm, :].T)

    base = 10000.0
    inv_freq = 1.0 / base ** (np.arange(0, DH, 2, dtype=np.float32) / DH)
    t_ = np.arange(S, dtype=np.float32)
    f = np.outer(t_, inv_freq)
    cos = np.zeros((S, DH), dtype=np.float32)
    sin = np.zeros((S, DH), dtype=np.float32)
    cos[:, 0::2] = np.cos(f); cos[:, 1::2] = np.cos(f)
    sin[:, 0::2] = np.sin(f); sin[:, 1::2] = np.sin(f)
    cosT = np.ascontiguousarray(cos.T)
    sinT = np.ascontiguousarray(sin.T)
    sinrotT = np.concatenate([sinT[64:], sinT[:64]], axis=0)

    P = np.zeros((128, 128), dtype=np.float32)
    for a in range(64):
        P[a + 64, a] = -1.0
        P[a, a + 64] = 1.0
    p["P_rot"] = P
    p["ident"] = np.eye(128, dtype=np.float32)

    p["up"] = np.ascontiguousarray(np.asarray(I["L0_ffn_up"]))
    p["down"] = np.ascontiguousarray(np.asarray(I["L0_ffn_down"]))

    p["winP"] = np.ascontiguousarray(np.asarray(I["L1_attn_W_in"])[perm, :])
    p["woutP"] = np.ascontiguousarray(np.asarray(I["L1_attn_W_out"])[:, perm])
    p["conv_w"] = np.ascontiguousarray(np.asarray(I["L1_attn_conv_w"]))
    p["dt_bias"] = np.ascontiguousarray(np.asarray(I["L1_attn_dt_bias"]).reshape(1, 64))
    p["A_log"] = np.ascontiguousarray(np.asarray(I["L1_attn_A_log"]).reshape(1, 64))
    p["D_param"] = np.ascontiguousarray(np.asarray(I["L1_attn_D_param"]).reshape(1, 64))
    p["norm_w"] = np.ascontiguousarray(np.asarray(I["L1_attn_norm_weight"]).reshape(1, 8192))

    p["wbT"] = np.ascontiguousarray(np.asarray(I["L1_ffn_Wb"]).T[perm, :][:, perm])
    p["bias_p"] = np.ascontiguousarray(np.asarray(I["L1_ffn_bias"])[perm].reshape(1, D))
    p["wrT"] = np.ascontiguousarray(np.asarray(I["L1_ffn_Wr"]).T[perm, :])
    A = np.asarray(I["L1_ffn_A"])
    p["laT"] = np.ascontiguousarray(A[:, :, perm].transpose(2, 0, 1).reshape(D, 128))
    Bm = np.asarray(I["L1_ffn_Bm"])
    p["lbT"] = np.ascontiguousarray(Bm[:, perm, :].transpose(0, 2, 1).reshape(128, D))

    def _pack(W, ob=128):
        C, O = W.shape
        return np.ascontiguousarray(
            W.reshape(C // 128, 128, O // ob, ob).transpose(2, 1, 0, 3))

    p["wqT_pk"] = _pack(p["wqT"])
    p["woT_pk"] = _pack(p["woT"])
    p["wbT_pk"] = _pack(p["wbT"])
    p["wkv_pk"] = np.ascontiguousarray(np.stack([
        p["wkT"].reshape(32, 128, 128).transpose(1, 0, 2),
        p["wvT"].reshape(32, 128, 128).transpose(1, 0, 2)]))
    winP_pad = np.concatenate(
        [p["winP"], np.zeros((D, 64), np.float32)], axis=1)
    p["winP_pk"] = _pack(winP_pad)
    p["woutP_pk"] = _pack(p["woutP"])
    p["up_pk"] = np.ascontiguousarray(
        p["up"].reshape(16, 2, 128, 1024).transpose(0, 2, 1, 3))
    p["down_pk"] = np.ascontiguousarray(
        p["down"].reshape(16, 8, 128, 256).transpose(0, 2, 1, 3))
    p["laT_pk"] = np.ascontiguousarray(
        p["laT"].reshape(32, 128, 128).transpose(1, 0, 2))
    p["wrT_pk"] = np.ascontiguousarray(
        p["wrT"].reshape(32, 128, 8).transpose(1, 0, 2))

    xs, cs, sr, mk, uwts = [], [], [], [], []
    for c in range(NC):
        g0 = c * T - HALO
        xin = np.zeros((XT, D), dtype=np.float32)
        lo = max(0, g0); hi = min(S, g0 + XT)
        xin[lo - g0: hi - g0] = p["x_p"][lo:hi]
        xs.append(xin)
        ct = np.zeros((128, XT), dtype=np.float32)
        st = np.zeros((128, XT), dtype=np.float32)
        ct[:, lo - g0: hi - g0] = cosT[:, lo:hi]
        st[:, lo - g0: hi - g0] = sinrotT[:, lo:hi]
        cs.append(ct); sr.append(st)
        m = np.zeros((128, NKT), dtype=np.float32)
        for kt in range(NKT):
            for r_ in range(128):
                g = g0 + kt * 128 + r_
                if not (0 <= g < S):
                    m[r_, kt] = -1e9
        mk.append(m)
        u = np.zeros((64, T), dtype=np.float32)
        for q in range(T):
            g = c * T + q
            kk = g // 256
            r = g - 256 * kk
            ua, ub = (0.0, 1.0) if kk == 0 else (1.0 - r / 255.0, r / 255.0)
            u[:32, q] = ua
            u[32:, q] = ub
        uwts.append(np.ascontiguousarray(u))
    p["xin"] = xs; p["cosT"] = cs; p["sinrotT"] = sr
    p["maskneg"] = mk; p["uw"] = uwts
    return p



class Kx:
    def __init__(self, nc, es, names):
        self.nc = nc
        self.t = nc.tensor
        self.v = nc.vector
        self.s = nc.scalar
        self.g = nc.gpsimd
        self.y = nc.sync
        self.sem = {}
        self.n = {}
        for nm in names:
            self.sem[nm] = es.enter_context(nc.semaphore(nm))
            self.n[nm] = 0

    def inc(self, instr, nm, k=1):
        instr.then_inc(self.sem[nm], k)
        self.n[nm] += k
        return self.n[nm]

    def wait(self, eng, nm, v=None):
        eng.wait_ge(self.sem[nm], self.n[nm] if v is None else v)

    def dinc(self, instr, nm):
        return self.inc(instr, nm, 16)

    def join(self, eng, names=("pe", "ve", "se", "dT", "dW", "dX", "dS")):
        for nm in names:
            if self.n[nm]:
                eng.wait_ge(self.sem[nm], self.n[nm])


SEMS = ["dW", "dW2", "dV", "dX", "dX2", "dG", "dG2", "dT", "dS", "dS2", "pe", "ve", "se", "ge", "cc"]


def load_w_cast(k, dst, src):
    return k.dinc(k.g.dma_start(out=dst, in_=src), "dW")



def emit_prelnT(k, rin_rows, xh, n_tiles, rt, xht, dumps, stats, scol0, epscol=None):
    dump1, dump2 = dumps
    for tt in list(range(n_tiles))[::-1]:
        k.wait(k.y, "ve")
        k.dinc(k.y.dma_start(out=rt[:], in_=rin_rows(tt)), "dX")
        c0 = scol0 + 4 * tt
        k.wait(k.v, "dX")
        k.inc(k.v.tensor_scalar(dump1, rt[:], 1.0, 0.0, op0=ALU.mult, op1=ALU.add,
                                accum_out=stats[:, c0:c0 + 1]), "ve")
        k.wait(k.s, "dX")
        k.inc(k.s.activation(dump2, rt[:], AF.Square,
                             accum_out=stats[:, c0 + 3:c0 + 4]), "se")
        mcol = stats[:, c0 + 1:c0 + 2]
        vcol = stats[:, c0 + 2:c0 + 3]
        k.wait(k.v, "se")
        k.wait(k.v, "ve")
        k.inc(k.v.tensor_scalar(mcol, stats[:, c0:c0 + 1], 1.0 / D, None, op0=ALU.mult), "ve")
        k.wait(k.v, "ve")
        k.inc(k.v.tensor_tensor(vcol, mcol, mcol, op=ALU.mult), "ve")
        k.wait(k.v, "ve")
        k.inc(k.v.scalar_tensor_tensor(vcol, stats[:, c0 + 3:c0 + 4], 1.0 / D, vcol,
                                       op0=ALU.mult, op1=ALU.subtract), "ve")
        k.wait(k.s, "ve")
        k.inc(k.s.activation(vcol, vcol, AF.Ln, bias=epscol[:]), "se")
        k.wait(k.s, "se")
        k.inc(k.s.activation(vcol, vcol, AF.Exp, scale=-0.5), "se")
        k.wait(k.v, "se")
        k.wait(k.v, "dT")
        k.inc(k.v.tensor_scalar(xht[:], rt[:], mcol, vcol,
                                op0=ALU.subtract, op1=ALU.mult), "ve")
        k.inc(k.v.engine_nop(), "ve")
        k.wait(k.s, "ve")
        k.dinc(k.s.dma_start(out=xh[:, :, tt * 128:(tt + 1) * 128],
                            in_=xht[:], transpose=True), "dT")


def emit_postln_residual(k, f_ch, fT, r_rows, rout_rows, n_tiles, rt, ru,
                        dumps, stats, scol0, epscol=None):
    dump1, dump2 = dumps
    nt = n_tiles
    k.wait(k.s, "se")
    k.wait(k.s, "ve")
    tdone = k.dinc(k.s.dma_start(out=fT,
                                in_=f_ch.rearrange("p a b -> p (a b)"),
                                transpose=True), "dT")
    for tt in range(nt):
        k.wait(k.y, "dS")
        k.wait(k.y, "ve")
        k.dinc(k.y.dma_start(out=rt[:], in_=r_rows(tt)), "dX")
        fview = fT[:, tt:32 * nt:nt, :]
        c0 = scol0 + 4 * tt
        k.wait(k.v, "dT", tdone)
        k.inc(k.v.tensor_scalar(dump1, fview, 1.0, 0.0, op0=ALU.mult, op1=ALU.add,
                                accum_out=stats[:, c0:c0 + 1]), "ve")
        k.wait(k.s, "dT", tdone)
        k.inc(k.s.activation(dump2, fview, AF.Square,
                             accum_out=stats[:, c0 + 3:c0 + 4]), "se")
        mcol = stats[:, c0 + 1:c0 + 2]
        vcol = stats[:, c0 + 2:c0 + 3]
        k.wait(k.v, "se")
        k.wait(k.v, "ve")
        k.inc(k.v.tensor_scalar(mcol, stats[:, c0:c0 + 1], 1.0 / D, None, op0=ALU.mult), "ve")
        k.wait(k.v, "ve")
        k.inc(k.v.tensor_tensor(vcol, mcol, mcol, op=ALU.mult), "ve")
        k.wait(k.v, "ve")
        k.inc(k.v.scalar_tensor_tensor(vcol, stats[:, c0 + 3:c0 + 4], 1.0 / D, vcol,
                                       op0=ALU.mult, op1=ALU.subtract), "ve")
        k.wait(k.s, "ve")
        k.inc(k.s.activation(vcol, vcol, AF.Ln, bias=epscol[:]), "se")
        k.wait(k.s, "se")
        k.inc(k.s.activation(vcol, vcol, AF.Exp, scale=-0.5), "se")
        k.wait(k.v, "se")
        k.wait(k.v, "dX")
        k.inc(k.v.tensor_scalar(ru[:], fview, mcol, vcol, op0=ALU.subtract,
                                op1=ALU.mult), "ve")
        k.wait(k.v, "ve")
        k.inc(k.v.tensor_tensor(ru[:], ru[:], rt[:], op=ALU.add), "ve")
        k.wait(k.y, "ve")
        k.dinc(k.y.dma_start(out=rout_rows(tt), in_=ru[:]), "dS")



def _emit_l0attn(k, es, sb, nc, io, dbg=False, dbg_stop=None):
    kr = sb("kr", [128, XT], BF16)
    vt = sb("vt", [128, NKT, 128], BF16)
    cos_sb = sb("cos_sb", [128, XT], BF16)
    sinr_sb = sb("sinr_sb", [128, XT], BF16)
    p_sb = sb("p_sb", [128, 128], BF16)
    id_bf = sb("id_bf", [128, 128], BF16)
    id_f32 = sb("id_f32", [128, 128], F32)
    mask_sb = sb("mask_sb", [128, NKT], F32)
    stats = sb("stats", [128, 96], F32)
    ones_sb = sb("ones_sb", [128, 1], BF16)
    epscol = sb("epscol", [128, 1], F32)

    load_w_cast(k, cos_sb[:], io["cosT"])
    load_w_cast(k, sinr_sb[:], io["sinrotT"])
    load_w_cast(k, p_sb[:], io["prot"])
    load_w_cast(k, id_bf[:], io["ident"])
    k.dinc(k.y.dma_start(out=id_f32[:], in_=io["ident"]), "dX")
    k.dinc(k.y.dma_start(out=mask_sb[:], in_=io["maskneg"]), "dX")
    k.g.memset(ones_sb[:], 1.0)
    k.g.memset(epscol[:], EPS_LN)
    ge_consts = k.inc(k.g.engine_nop(), "ge")
    dw_consts = k.n["dW"]

    dummyT = sb("dummyT", [128, 16], BF16)
    k.wait(k.s, "dW")
    k.dinc(k.s.dma_start(out=dummyT[:], in_=cos_sb[0:16, 0:128], transpose=True), "dT")
    k.wait(k.s, "ge")
    k.wait(k.v, "ge")
    k.wait(k.t, "ge")

    BIGA = sb("BIGA", [128, 32 * XT], BF16)
    BIGB = sb("BIGB", [128, 32 * T], BF16)
    q_sb = BIGB.rearrange("p (a b) -> p a b", a=32)

    if True:
        xh = BIGA.rearrange("p (a b) -> p a b", a=32)
        with nc.sbuf_tensor("rt1", [128, D], F32) as rt, \
             nc.sbuf_tensor("xht1", [128, D], BF16) as xht:
            dump1 = q_sb[:, 0:4, :].rearrange("p a b -> p (a b)")
            dump2 = q_sb[:, 4:8, :].rearrange("p a b -> p (a b)")
            emit_prelnT(k, lambda tt: io["xin"][tt * 128:(tt + 1) * 128, :],
                        xh, NKT, rt, xht, (dump1, dump2), stats, 0, epscol)

        with nc.sbuf_tensor("wbuf", [128, 2, 32, 128], BF16) as wbuf, \
             nc.sbuf_tensor("qraw", [128, 2, 512], BF16) as qraw, \
             nc.sbuf_tensor("qs2", [128, 2, 512], BF16) as qs2, \
             nc.sbuf_tensor("t1b", [128, 2, 512], BF16) as t1b, \
             nc.psum_tensor("psp", [128, 2, 512], F32) as psp, \
             nc.psum_tensor("psr", [128, 2, 512], F32) as psr, \
             nc.psum_tensor("psT", [128, 128], BF16) as psT:

            xh_done = k.n["dT"]
            units = [("q", j) for j in range(32)] + [("k", 0), ("v", 0)]
            wload = []

            def load_unit(uj):
                kind, j = units[uj]
                src = (io["wqT"][j] if kind == "q"
                       else (io["wkv"][0] if kind == "k" else io["wkv"][1]))
                nm = "dW" if uj % 2 == 0 else "dW2"
                wload.append((nm, k.dinc(k.g.dma_start(
                    out=wbuf[:, uj % 2], in_=src), nm)))

            k.join(k.g, ("ve", "dT"))
            load_unit(0)
            load_unit(1)
            k.wait(k.t, "dT", xh_done)
            k.wait(k.t, "ge", ge_consts)
            k.wait(k.t, "dW", dw_consts)

            evac_ve = {}
            fin_ve = {}
            proj_pe = {}
            hfc = 0
            for uj, (kind, j) in enumerate(units):
                k.wait(k.t, wload[uj][0], wload[uj][1])
                nhalf = 2 if kind == "q" else 3
                for hf in range(nhalf):
                    off = (HALO if kind == "q" else 0) + hf * 512
                    pp = psp[:, hfc % 2, :]
                    if hfc >= 2:
                        k.wait(k.t, "ve", evac_ve[hfc - 2])
                    for ci in range(32):
                        mm = k.t.matmul(pp, wbuf[:, uj % 2, ci, :],
                                        xh[:, ci, off:off + 512],
                                        start=(ci == 0), stop=(ci == 31))
                    pj = k.inc(mm, "pe")
                    proj_pe[uj] = pj
                    k.wait(k.v, "pe", pj)
                    evac_ve[hfc] = k.inc(k.v.tensor_copy(qraw[:, hfc % 2, :], pp), "ve")
                    k.wait(k.v, "ve")
                    if kind == "v":
                        for c4 in range(4):
                            kt = 4 * hf + c4
                            k.wait(k.t, "ve")
                            tp = k.inc(k.t.transpose(
                                psT[:], qraw[:, hfc % 2, c4 * 128:(c4 + 1) * 128],
                                id_bf[:]), "pe")
                            k.wait(k.v, "pe", tp)
                            k.inc(k.v.tensor_copy(vt[:, kt, :], psT[:]), "ve")
                        fin_ve[hfc] = k.n["ve"]
                        hfc += 1
                        continue
                    k.inc(k.v.tensor_tensor(qs2[:, hfc % 2, :], qraw[:, hfc % 2, :],
                                            sinr_sb[:, off:off + 512], op=ALU.mult),
                          "ve")
                    k.wait(k.t, "ve")
                    rmm = k.t.matmul(psr[:, hfc % 2, :], p_sb[:], qs2[:, hfc % 2, :],
                                     start=True, stop=True)
                    rot_pe = k.inc(rmm, "pe")
                    k.inc(k.v.tensor_tensor(t1b[:, hfc % 2, :], qraw[:, hfc % 2, :],
                                            cos_sb[:, off:off + 512], op=ALU.mult),
                          "ve")
                    k.wait(k.v, "pe", rot_pe)
                    k.wait(k.v, "ve")
                    dst = (q_sb[:, j, hf * 512:(hf + 1) * 512] if kind == "q"
                           else kr[:, off:off + 512])
                    fin_ve[hfc] = k.inc(
                        k.v.tensor_tensor(dst, t1b[:, hfc % 2, :], psr[:, hfc % 2, :],
                                          op=ALU.add), "ve")
                    hfc += 1
                if uj + 2 < len(units):
                    k.wait(k.g, "pe", proj_pe[uj])
                    load_unit(uj + 2)

    if dbg:
        k.wait(k.y, "dT")
        k.dinc(k.y.dma_start(out=io["dbg_xh"], in_=BIGA[:]), "dS")
        k.dinc(k.y.dma_start(out=io["dbg_stats"], in_=stats[:]), "dS")
        k.wait(k.y, "ve")
        k.wait(k.y, "dS")
        k.dinc(k.y.dma_start(out=io["dbg_q"], in_=BIGB[:]), "dS")
        k.dinc(k.y.dma_start(out=io["dbg_kr"], in_=kr[:]), "dS")
        k.dinc(k.y.dma_start(out=io["dbg_vt"], in_=vt.rearrange("p a b -> p (a b)")), "dS")
        k.wait(k.y, "dS")
    if dbg_stop == "xh":
        return

    o_ch = BIGA[:, 0:32 * T].rearrange("p (a b) -> p a b", a=32)
    avA = BIGA[:, 32 * T + 0:32 * T + 4096].rearrange("p (a b) -> p a b", a=32)
    avB = BIGA[:, 32 * T + 4096:32 * T + 8192].rearrange("p (a b) -> p a b", a=32)
    avTA = BIGA[:, 32 * T + 8192:32 * T + 12288].rearrange("p (a b) -> p a b", a=32)
    avTB = BIGA[:, 32 * T + 12288:32 * T + 16384].rearrange("p (a b) -> p a b", a=32)
    with nc.sbuf_tensor("uw_sb", [64, T], F32) as uw_sb, \
         nc.sbuf_tensor("sig", [128, 2, 1024], BF16) as sig, \
         nc.sbuf_tensor("oT", [128, 32, 128], BF16) as oT, \
         nc.sbuf_tensor("rsrow", [1, 8192], BF16) as rsrow, \
         nc.sbuf_tensor("rsT", [64, 128], F32) as rsT, \
         nc.sbuf_tensor("uTc", [128, 64], F32) as uTc, \
         nc.sbuf_tensor("tmp2", [128, 128], BF16) as tmp2, \
         nc.psum_tensor("sc", [128, 2, 1024], F32) as sc, \
         nc.psum_tensor("av", [128, 2, 512], F32) as av, \
         nc.psum_tensor("rsp", [1, 1024], F32) as rsp:

        k.join(k.y, ("ve", "se", "pe"))
        k.join(k.t, ("ve", "se"))
        k.dinc(k.y.dma_start(out=uw_sb[:], in_=io["uw"]), "dX")
        for qt in range(8):
            ka = qt & ~1
            for w in range(2):
                av_sb = avA if w == 0 else avB
                for hg in range(4):
                    for kt4 in range(4):
                        ktile = ka + 2 * w + kt4
                        k.wait(k.t, "se")
                        for half in range(2):
                            mm = k.t.matmul(
                                sc[:, kt4 % 2, half * 512:(half + 1) * 512],
                                kr[:, ktile * 128:(ktile + 1) * 128],
                                q_sb[:, 8 * hg + 4 * half: 8 * hg + 4 * half + 4,
                                     qt * 128:(qt + 1) * 128],
                                start=True, stop=True)
                        sc_pe = k.inc(mm, "pe")
                        k.wait(k.s, "pe")
                        sg = k.inc(k.s.activation(
                            sig[:, kt4 % 2, :], sc[:, kt4 % 2, :], AF.Sigmoid,
                            bias=mask_sb[:, ktile:ktile + 1], scale=SCALE0), "se")
                        k.wait(k.t, "se", sg)
                        for half in range(2):
                            k.t.matmul(rsp[:, half * 512:(half + 1) * 512],
                                       ones_sb[:],
                                       sig[:, kt4 % 2, half * 512:(half + 1) * 512],
                                       start=(kt4 == 0), stop=(kt4 == 3))
                        for half in range(2):
                            mm = k.t.matmul(
                                av[:, half, :], vt[:, ktile, :],
                                sig[:, kt4 % 2, half * 512:(half + 1) * 512],
                                start=(kt4 == 0), stop=(kt4 == 3))
                        k.inc(mm, "pe")
                    k.wait(k.s, "pe")
                    k.inc(k.s.activation(
                        rsrow[:, w * 4096 + hg * 1024: w * 4096 + (hg + 1) * 1024],
                        rsp[:], AF.Identity), "se")
                    for half in range(2):
                        k.inc(k.s.activation(
                            av_sb[:, 8 * hg + 4 * half: 8 * hg + 4 * half + 4, :],
                            av[:, half, :], AF.Identity), "se")
                    k.wait(k.t, "se")

            k.wait(k.y, "se")
            k.dinc(k.y.dma_start(out=io["rs_scr"].rearrange("w g h q -> (w g h q)"),
                                in_=rsrow[0:1, :]), "dS")
            k.wait(k.g, "dS")
            k.dinc(k.g.dma_start(out=rsT[:],
                                in_=io["rs_scr"].rearrange("w g h q -> (w g h) q")),
                  "dG")
            k.wait(k.v, "dG")
            k.inc(k.v.tensor_scalar(rsT[:], rsT[:], 1e-30, None, op0=ALU.add), "ve")
            k.wait(k.v, "ve")
            k.inc(k.v.reciprocal(rsT[:], rsT[:]), "ve")
            k.wait(k.v, "ve")
            k.wait(k.v, "dX")
            umul = k.inc(k.v.tensor_tensor(rsT[:], rsT[:],
                                           uw_sb[:, qt * 128:(qt + 1) * 128],
                                           op=ALU.mult), "ve")
            k.wait(k.t, "ve", umul)
            k.wait(k.t, "se")
            tp = k.inc(k.t.transpose(sc[:, 0, 0:64], rsT[:], id_f32[0:64, 0:64]), "pe")
            k.wait(k.v, "pe", tp)
            k.inc(k.v.tensor_copy(uTc[:], sc[:, 0, 0:64]), "ve")
            k.wait(k.t, "ve")

            k.wait(k.s, "se")
            k.dinc(k.s.dma_start(out=avTA, in_=avA.rearrange("p a b -> p (a b)"),
                                transpose=True), "dT")
            dtb = k.dinc(k.s.dma_start(out=avTB,
                                      in_=avB.rearrange("p a b -> p (a b)"),
                                      transpose=True), "dT")
            k.wait(k.v, "dT", dtb)
            k.wait(k.v, "ve")
            for h in range(32):
                k.wait(k.v, "ve")
                k.inc(k.v.tensor_scalar(tmp2[:], avTA[:, h, :], uTc[:, h:h + 1], None,
                                  op0=ALU.mult), "ve")
                k.wait(k.v, "ve")
                k.inc(k.v.scalar_tensor_tensor(
                    oT[:, h, :], avTB[:, h, :], uTc[:, 32 + h:33 + h],
                    tmp2[:], op0=ALU.mult, op1=ALU.add), "ve")
            k.wait(k.s, "ve")
            k.dinc(k.s.dma_start(out=o_ch[:, :, qt * 128:(qt + 1) * 128],
                                in_=oT[:].rearrange("p a b -> p (a b)"),
                                transpose=True), "dT")
            k.wait(k.t, "dT")
            k.wait(k.v, "dT")

    if dbg:
        k.wait(k.y, "dT")
        k.wait(k.y, "ve")
        k.dinc(k.y.dma_start(out=io["dbg_och"], in_=BIGA[:, 0:32 * T]), "dS")
        k.wait(k.y, "dS")

    f_ch = BIGB.rearrange("p (a b) -> p a b", a=32)
    with nc.sbuf_tensor("wbuf3", [128, 2, 32, 128], BF16) as wbuf, \
         nc.psum_tensor("psp3", [128, 2, 512], F32) as psp:
        wl = []

        def load_w3(j):
            nm = "dW" if j % 2 == 0 else "dW2"
            wl.append((nm, k.dinc(k.g.dma_start(
                out=wbuf[:, j % 2], in_=io["woT"][j]), nm)))

        k.join(k.g, ("ve", "se", "pe", "dT", "dS"))
        k.join(k.t, ("ve", "se"))
        load_w3(0)
        load_w3(1)
        k.wait(k.t, "dT")
        evac_se = {}
        for j in range(32):
            k.wait(k.t, wl[j][0], wl[j][1])
            for hf in range(2):
                u = 2 * j + hf
                pp = psp[:, u % 2, :]
                if u >= 2:
                    k.wait(k.t, "se", evac_se[u - 2])
                for ci in range(32):
                    mm = k.t.matmul(pp, wbuf[:, j % 2, ci, :],
                                    o_ch[:, ci, hf * 512:(hf + 1) * 512],
                                    start=(ci == 0), stop=(ci == 31))
                pej = k.inc(mm, "pe")
                k.wait(k.s, "pe", pej)
                evac_se[u] = k.inc(
                    k.s.activation(f_ch[:, j, hf * 512:(hf + 1) * 512], pp,
                                   AF.Identity), "se")
            if j + 2 < 32:
                k.wait(k.g, "pe")
                load_w3(j + 2)

    if dbg:
        k.wait(k.y, "se")
        k.dinc(k.y.dma_start(out=io["dbg_fch"], in_=BIGB[:]), "dS")
        k.wait(k.y, "dS")

    fT = BIGA[:, 0:32 * T].rearrange("p (a b) -> p a b", a=256)
    dmp1 = BIGA[:, 32 * T:32 * T + 4096]
    dmp2 = BIGA[:, 32 * T + 4096:32 * T + 8192]
    k.join(k.y, ("pe", "se"))
    with nc.sbuf_tensor("rt4", [128, D], F32) as rt4, \
         nc.sbuf_tensor("ru4", [128, D], F32) as ru4:
        emit_postln_residual(
            k, f_ch, fT,
            lambda tt: io["xin"][HALO + tt * 128: HALO + (tt + 1) * 128, :],
            lambda tt: io["r_out"][tt * 128:(tt + 1) * 128, :],
            8, rt4, ru4, (dmp1, dmp2), stats, 48, epscol)

    k.wait(k.y, "dS")


def _build_l0attn(dbg=False, dbg_stop=None):
    nc = bass.Bass("TRN2", target_bir_lowering=False)
    io = {}
    io["xin"] = nc.dram_tensor("xin", [XT, D], F32, kind="ExternalInput").ap()
    io["wqT"] = nc.dram_tensor("wqT", [32, 128, 32, 128], F32, kind="ExternalInput").ap()
    io["wkv"] = nc.dram_tensor("wkv", [2, 128, 32, 128], F32, kind="ExternalInput").ap()
    io["woT"] = nc.dram_tensor("woT", [32, 128, 32, 128], F32, kind="ExternalInput").ap()
    io["cosT"] = nc.dram_tensor("cosT", [128, XT], F32, kind="ExternalInput").ap()
    io["sinrotT"] = nc.dram_tensor("sinrotT", [128, XT], F32, kind="ExternalInput").ap()
    io["prot"] = nc.dram_tensor("prot", [128, 128], F32, kind="ExternalInput").ap()
    io["ident"] = nc.dram_tensor("ident", [128, 128], F32, kind="ExternalInput").ap()
    io["maskneg"] = nc.dram_tensor("maskneg", [128, NKT], F32, kind="ExternalInput").ap()
    io["uw"] = nc.dram_tensor("uw", [64, T], F32, kind="ExternalInput").ap()
    io["r_out"] = nc.dram_tensor("r_out", [T, D], F32, kind="ExternalOutput").ap()
    io["rs_scr"] = nc.dram_tensor("rs_scr", [2, 8, 4, 128], BF16).ap()
    if dbg:
        io["dbg_xh"] = nc.dram_tensor("dbg_xh", [128, 32 * XT], BF16, kind="ExternalOutput").ap()
        io["dbg_stats"] = nc.dram_tensor("dbg_stats", [128, 96], F32, kind="ExternalOutput").ap()
        io["dbg_q"] = nc.dram_tensor("dbg_q", [128, 32 * T], BF16, kind="ExternalOutput").ap()
        io["dbg_kr"] = nc.dram_tensor("dbg_kr", [128, XT], BF16, kind="ExternalOutput").ap()
        io["dbg_vt"] = nc.dram_tensor("dbg_vt", [128, NKT * 128], BF16, kind="ExternalOutput").ap()
        io["dbg_och"] = nc.dram_tensor("dbg_och", [128, 32 * T], BF16, kind="ExternalOutput").ap()
        io["dbg_fch"] = nc.dram_tensor("dbg_fch", [128, 32 * T], BF16, kind="ExternalOutput").ap()

    with ExitStack() as es:
        def sb(name, shape, dt):
            return es.enter_context(nc.sbuf_tensor(name, shape, dt))
        k = Kx(nc, es, SEMS)
        block = es.enter_context(nc.Block())

        @block.sync
        def _(sy):
            _emit_l0attn(k, es, sb, nc, io, dbg=dbg, dbg_stop=dbg_stop)
    return nc


def _maps_l0attn(p, R):
    maps = []
    for c in range(NC):
        g0 = c * T - HALO
        xin = np.zeros((XT, D), dtype=np.float32)
        lo = max(0, g0); hi = min(S, g0 + XT)
        xin[lo - g0: hi - g0] = R[lo:hi]
        maps.append(dict(
            xin=xin, wqT=p["wqT_pk"], wkv=p["wkv_pk"], woT=p["woT_pk"],
            cosT=p["cosT"][c], sinrotT=p["sinrotT"][c], prot=p["P_rot"],
            ident=p["ident"], maskneg=p["maskneg"][c], uw=p["uw"][c]))
    return maps


def _run_l0attn(p, R):
    nc = _build_l0attn()
    res = run_bass_kernel_spmd(nc, _maps_l0attn(p, R), core_ids=list(range(NC)))
    return np.concatenate([res.results[c]["r_out"] for c in range(NC)], axis=0)





def _emit_l0ffn(k, es, sb, nc, io):
    stats = sb("stats", [128, 96], F32)
    epscol = sb("epscol", [128, 1], F32)
    ARA = sb("ARA", [128, 32 * T], BF16)
    ARB = sb("ARB", [128, 32 * T], BF16)
    dmp = sb("dmp", [128, 2 * D], BF16)
    xh = ARA.rearrange("p (a b) -> p a b", a=32)
    f_ch = ARB.rearrange("p (a b) -> p a b", a=32)

    k.g.memset(epscol[:], EPS_LN)
    k.inc(k.g.engine_nop(), "ge")
    k.wait(k.s, "ge")
    k.wait(k.v, "ge")
    dummyT = sb("dummyT", [128, 16], BF16)
    dummyS = sb("dummyS", [16, 128], BF16)
    k.g.memset(dummyS[:], 0.0)
    ge_dummy = k.inc(k.g.engine_nop(), "ge")
    k.wait(k.s, "ge", ge_dummy)
    k.dinc(k.s.dma_start(out=dummyT[:], in_=dummyS[:], transpose=True), "dT")

    with nc.sbuf_tensor("rt1", [128, D], F32) as rt, \
         nc.sbuf_tensor("xht1", [128, D], BF16) as xht:
        emit_prelnT(k, lambda tt: io["r_in"][tt * 128:(tt + 1) * 128, :],
                    xh, 8, rt, xht, (dmp[:, 0:D], dmp[:, D:2 * D]), stats, 0,
                    epscol)

    k.join(k.g, ("ve", "dT", "se"))
    with nc.sbuf_tensor("wbufU", [128, 2, 2, 1024], BF16) as wbufU, \
         nc.sbuf_tensor("wbufD", [128, 2, 8, 256], BF16) as wbufD, \
         nc.sbuf_tensor("hid", [128, 8, 1024], BF16) as hid, \
         nc.psum_tensor("psu", [128, 2, 512], F32) as psu, \
         nc.psum_tensor("psd", [128, 2, 512], F32) as psd:

        wl = []

        def load_head(h):
            nm = "dW" if h % 2 == 0 else "dW2"
            k.dinc(k.g.dma_start(out=wbufU[:, h % 2], in_=io["up"][h]), nm)
            wl.append((nm, k.dinc(k.g.dma_start(
                out=wbufD[:, h % 2], in_=io["down"][h]), nm)))

        load_head(0)
        load_head(1)
        k.wait(k.t, "dT")
        k.wait(k.t, "ge", ge_dummy)
        up_pe = {}
        for h in range(16):
            k.wait(k.t, wl[h][0], wl[h][1])
            for mt in range(8):
                for half in range(2):
                    u = 2 * mt + half
                    if h > 0 or u >= 2:
                        k.wait(k.t, "se")
                    for d2 in range(2):
                        mm = k.t.matmul(psu[:, u % 2, :],
                                        wbufU[:, h % 2, d2, mt * 128:(mt + 1) * 128],
                                        xh[:, 2 * h + d2, half * 512:(half + 1) * 512],
                                        start=(d2 == 0), stop=(d2 == 1))
                    pj = k.inc(mm, "pe")
                    k.wait(k.s, "pe", pj)
                    k.inc(k.s.activation(hid[:, mt, half * 512:(half + 1) * 512],
                                         psu[:, u % 2, :], AF.Gelu), "se")
            for dt_ in range(2):
                for half in range(2):
                    u = 2 * dt_ + half
                    if h > 0 or u >= 2:
                        k.wait(k.t, "se")
                    k.wait(k.t, "se")
                    for mt in range(8):
                        mm = k.t.matmul(psd[:, u % 2, :],
                                        wbufD[:, h % 2, mt, dt_ * 128:(dt_ + 1) * 128],
                                        hid[:, mt, half * 512:(half + 1) * 512],
                                        start=(mt == 0), stop=(mt == 7))
                    pj = k.inc(mm, "pe")
                    up_pe[h] = pj
                    k.wait(k.s, "pe", pj)
                    k.inc(k.s.activation(
                        f_ch[:, 2 * h + dt_, half * 512:(half + 1) * 512],
                        psd[:, u % 2, :], AF.Identity), "se")
            if h + 2 < 16:
                k.wait(k.g, "pe", up_pe[h])
                load_head(h + 2)

    fT = ARA.rearrange("p (a b) -> p a b", a=256)
    k.join(k.y, ("pe", "se"))
    with nc.sbuf_tensor("rt4", [128, D], F32) as rt4, \
         nc.sbuf_tensor("ru4", [128, D], F32) as ru4:
        emit_postln_residual(
            k, f_ch, fT,
            lambda tt: io["r_in"][tt * 128:(tt + 1) * 128, :],
            lambda tt: io["r_out"][tt * 128:(tt + 1) * 128, :],
            8, rt4, ru4, (dmp[:, 0:D], dmp[:, D:2 * D]), stats, 48, epscol)
    k.wait(k.y, "dS")


def _build_l0ffn():
    nc = bass.Bass("TRN2", target_bir_lowering=False)
    io = {}
    io["r_in"] = nc.dram_tensor("r_in", [T, D], F32, kind="ExternalInput").ap()
    io["up"] = nc.dram_tensor("up", [16, 128, 2, 1024], F32, kind="ExternalInput").ap()
    io["down"] = nc.dram_tensor("down", [16, 128, 8, 256], F32, kind="ExternalInput").ap()
    io["r_out"] = nc.dram_tensor("r_out", [T, D], F32, kind="ExternalOutput").ap()
    with ExitStack() as es:
        def sb(name, shape, dt):
            return es.enter_context(nc.sbuf_tensor(name, shape, dt))
        k = Kx(nc, es, SEMS)
        block = es.enter_context(nc.Block())

        @block.sync
        def _(sy):
            _emit_l0ffn(k, es, sb, nc, io)
    return nc


def _run_l0ffn(p, R):
    nc = _build_l0ffn()
    maps = [dict(r_in=np.ascontiguousarray(R[c * T:(c + 1) * T]),
                 up=p["up_pk"], down=p["down_pk"]) for c in range(NC)]
    res = run_bass_kernel_spmd(nc, maps, core_ids=list(range(NC)))
    return np.concatenate([res.results[c]["r_out"] for c in range(NC)], axis=0)





def _emit_l1ffn(k, es, sb, nc, io):
    stats = sb("stats", [128, 96], F32)
    epscol = sb("epscol", [128, 1], F32)
    ARA = sb("ARA", [128, 32 * T], BF16)
    ARB = sb("ARB", [128, 32 * T], BF16)
    dmp = sb("dmp", [128, 2 * D], BF16)
    xh = ARA.rearrange("p (a b) -> p a b", a=32)
    f_ch = ARB.rearrange("p (a b) -> p a b", a=32)

    k.g.memset(epscol[:], EPS_LN)
    k.inc(k.g.engine_nop(), "ge")
    k.wait(k.s, "ge")
    k.wait(k.v, "ge")
    dummyT = sb("dummyT", [128, 16], BF16)
    dummyS = sb("dummyS", [16, 128], BF16)
    k.g.memset(dummyS[:], 0.0)
    ge_dummy = k.inc(k.g.engine_nop(), "ge")
    k.wait(k.s, "ge", ge_dummy)
    k.dinc(k.s.dma_start(out=dummyT[:], in_=dummyS[:], transpose=True), "dT")

    with nc.sbuf_tensor("rt1", [128, D], F32) as rt, \
         nc.sbuf_tensor("xht1", [128, D], BF16) as xht:
        emit_prelnT(k, lambda tt: io["r_in"][tt * 128:(tt + 1) * 128, :],
                    xh, 8, rt, xht, (dmp[:, 0:D], dmp[:, D:2 * D]), stats, 0,
                    epscol)

    k.join(k.g, ("ve", "dT", "se"))
    with nc.sbuf_tensor("wbuf", [128, 2, 32, 128], BF16) as wbuf, \
         nc.psum_tensor("ps2", [128, 2, 512], F32) as ps2:
        es2 = ExitStack()
        lg = es2.enter_context(nc.sbuf_tensor("lg", [8, T], F32))[:]
        wx = es2.enter_context(nc.sbuf_tensor("wx", [8, T], F32))[:]
        lgt_sb = es2.enter_context(nc.sbuf_tensor("lgt_sb", [128, 8, 26], F32))
        id8 = es2.enter_context(nc.sbuf_tensor("id8", [8, 8], F32))
        id_f32 = es2.enter_context(nc.sbuf_tensor("idf", [128, 128], F32))
        k.dinc(k.y.dma_start(out=id8[:], in_=io["ident"][0:8, 0:8]), "dS")
        k.dinc(k.y.dma_start(out=id_f32[:], in_=io["ident"]), "dS")
        wexp = es2.enter_context(nc.sbuf_tensor("wexp", [128, T], F32))
        g_sb = es2.enter_context(nc.sbuf_tensor("g_sb", [128, T], BF16))
        lbT = es2.enter_context(nc.sbuf_tensor("lbT_sb", [128, 32, 128], BF16))
        bias_sb = es2.enter_context(nc.sbuf_tensor("biasv", [128, 32], F32))
        k.dinc(k.g.dma_start(out=lbT[:],
                             in_=io["lbT"].rearrange("p (a b) -> p a b", a=32)), "dV")
        k.dinc(k.y.dma_start(out=bias_sb[:], in_=io["bias_sb"]), "dS")
        k.wait(k.t, "dS")
        with nc.sbuf_tensor("wr_sb", [128, 32, 8], BF16) as wr_sb:
            k.dinc(k.g.dma_start(out=wr_sb[:], in_=io["wrT"]), "dV")
            k.wait(k.t, "dV")
            k.wait(k.t, "dT")
            k.wait(k.t, "ge", ge_dummy)
            for half in range(2):
                for ci in range(32):
                    mm = k.t.matmul(ps2[0:8, half, :], wr_sb[:, ci, :],
                                    xh[:, ci, half * 512:(half + 1) * 512],
                                    start=(ci == 0), stop=(ci == 31))
                k.inc(mm, "pe")
                k.wait(k.v, "pe")
                k.inc(k.v.tensor_copy(lg[:, half * 512:(half + 1) * 512],
                                      ps2[0:8, half, :]), "ve")
            for c8 in range(8):
                k.wait(k.t, "ve")
                tp = k.inc(k.t.transpose(ps2[:, 0, 0:8],
                                         lg[:, c8 * 128:(c8 + 1) * 128],
                                         id8[0:8, 0:8]), "pe")
                k.wait(k.v, "pe", tp)
                k.inc(k.v.tensor_copy(lgt_sb[:, c8, 0:8], ps2[:, 0, 0:8]), "ve")
                sl = lgt_sb[:, c8, 0:8]
                m1c = lgt_sb[:, c8, 8:9]
                m2c = lgt_sb[:, c8, 9:10]
                k.wait(k.v, "ve")
                k.inc(k.v.tensor_reduce(m1c, sl, op=ALU.max,
                                        axis=mybir.AxisListType.X), "ve")
                k.wait(k.v, "ve")
                k.inc(k.v.tensor_scalar(lgt_sb[:, c8, 10:18], sl, m1c, None,
                                        op0=ALU.is_ge), "ve")
                k.wait(k.v, "ve")
                k.inc(k.v.scalar_tensor_tensor(lgt_sb[:, c8, 10:18],
                                               lgt_sb[:, c8, 10:18], -1e30, sl,
                                               op0=ALU.mult, op1=ALU.add), "ve")
                k.wait(k.v, "ve")
                k.inc(k.v.tensor_reduce(m2c, lgt_sb[:, c8, 10:18], op=ALU.max,
                                        axis=mybir.AxisListType.X), "ve")
                k.wait(k.v, "ve")
                k.inc(k.v.tensor_scalar(m1c, m1c, -1.0, None, op0=ALU.mult), "ve")
                k.wait(k.s, "ve")
                k.inc(k.s.activation(lgt_sb[:, c8, 10:18], sl, AF.Exp,
                                     bias=m1c), "se")
                k.wait(k.v, "se")
                k.inc(k.v.tensor_scalar(lgt_sb[:, c8, 18:26], sl, m2c, None,
                                        op0=ALU.is_ge), "ve")
                k.wait(k.v, "ve")
                k.inc(k.v.tensor_tensor(lgt_sb[:, c8, 10:18],
                                        lgt_sb[:, c8, 10:18],
                                        lgt_sb[:, c8, 18:26], op=ALU.mult), "ve")
                k.wait(k.v, "ve")
                k.inc(k.v.tensor_reduce(m2c, lgt_sb[:, c8, 10:18], op=ALU.add,
                                        axis=mybir.AxisListType.X), "ve")
                k.wait(k.v, "ve")
                k.inc(k.v.reciprocal(m2c, m2c), "ve")
                k.wait(k.v, "ve")
                k.inc(k.v.tensor_scalar(lgt_sb[:, c8, 10:18],
                                        lgt_sb[:, c8, 10:18], m2c, 2.0,
                                        op0=ALU.mult, op1=ALU.mult), "ve")
                k.wait(k.t, "ve")
                tp = k.inc(k.t.transpose(ps2[0:8, 1, 0:128],
                                         lgt_sb[:, c8, 10:18],
                                         id_f32[:]), "pe")
                k.wait(k.v, "pe", tp)
                k.inc(k.v.tensor_copy(wx[:, c8 * 128:(c8 + 1) * 128],
                                      ps2[0:8, 1, 0:128]), "ve")
                k.wait(k.t, "ve")
            k.g.memset(wexp[:], 0.0)
            k.inc(k.g.engine_nop(), "ge")
            k.wait(k.y, "ge")
            k.wait(k.y, "ve")
            k.wait(k.y, "dT")
            for qd in range(4):
                k.dinc(k.y.dma_start(out=wexp[32 * qd:32 * qd + 2, :],
                                     in_=wx[2 * qd:2 * qd + 2, :]), "dX")
            k.wait(k.v, "dX")
            mask = [i // 16 for i in range(32)]
            k.inc(k.v.stream_shuffle(wexp[:], wexp[:], mask), "ve")

            k.wait(k.g, "pe")
            k.dinc(k.g.dma_start(out=wbuf[:, 0, :, :], in_=io["laT"]), "dV")
            k.wait(k.t, "dV")
            for half in range(2):
                for ci in range(32):
                    mm = k.t.matmul(ps2[:, half, :], wbuf[:, 0, ci, :],
                                    xh[:, ci, half * 512:(half + 1) * 512],
                                    start=(ci == 0), stop=(ci == 31))
                k.inc(mm, "pe")
                k.wait(k.v, "pe")
                k.wait(k.v, "ve")
                k.inc(k.v.tensor_tensor(g_sb[:, half * 512:(half + 1) * 512],
                                        ps2[:, half, :],
                                        wexp[:, half * 512:(half + 1) * 512],
                                        op=ALU.mult), "ve")

        wl = []

        def load_wb(j):
            nm = "dW" if j % 2 == 0 else "dW2"
            wl.append((nm, k.dinc(k.g.dma_start(
                out=wbuf[:, j % 2], in_=io["wbT"][j]), nm)))

        k.wait(k.g, "ve")
        load_wb(0)
        load_wb(1)
        k.wait(k.s, "dS")
        evac_se = {}
        for j in range(32):
            k.wait(k.t, wl[j][0], wl[j][1])
            for hf in range(2):
                u = 2 * j + hf
                if u >= 2:
                    k.wait(k.t, "se", evac_se[u - 2])
                for ci in range(32):
                    k.t.matmul(ps2[:, u % 2, :], wbuf[:, j % 2, ci, :],
                               xh[:, ci, hf * 512:(hf + 1) * 512],
                               start=(ci == 0), stop=False)
                mm = k.t.matmul(ps2[:, u % 2, :], lbT[:, j, :],
                                g_sb[:, hf * 512:(hf + 1) * 512],
                                start=False, stop=True)
                pej = k.inc(mm, "pe")
                k.wait(k.s, "pe", pej)
                evac_se[u] = k.inc(
                    k.s.activation(f_ch[:, j, hf * 512:(hf + 1) * 512],
                                   ps2[:, u % 2, :], AF.Identity,
                                   bias=bias_sb[:, j:j + 1]), "se")
            if j + 2 < 32:
                k.wait(k.g, "pe", pej)
                load_wb(j + 2)
        es2.close()

    fT = ARA.rearrange("p (a b) -> p a b", a=256)
    k.join(k.y, ("pe", "se"))
    with nc.sbuf_tensor("rt4", [128, D], F32) as rt4, \
         nc.sbuf_tensor("ru4", [128, D], F32) as ru4:
        emit_postln_residual(
            k, f_ch, fT,
            lambda tt: io["r_in"][tt * 128:(tt + 1) * 128, :],
            lambda tt: io["r_out"][tt * 128:(tt + 1) * 128, :],
            8, rt4, ru4, (dmp[:, 0:D], dmp[:, D:2 * D]), stats, 48, epscol)
    k.wait(k.y, "dS")


def _build_l1ffn():
    nc = bass.Bass("TRN2", target_bir_lowering=False)
    io = {}
    io["r_in"] = nc.dram_tensor("r_in", [T, D], F32, kind="ExternalInput").ap()
    io["wbT"] = nc.dram_tensor("wbT", [32, 128, 32, 128], F32, kind="ExternalInput").ap()
    io["wrT"] = nc.dram_tensor("wrT", [128, 32, 8], F32, kind="ExternalInput").ap()
    io["laT"] = nc.dram_tensor("laT", [128, 32, 128], F32, kind="ExternalInput").ap()
    io["lbT"] = nc.dram_tensor("lbT", [128, D], F32, kind="ExternalInput").ap()
    io["bias_sb"] = nc.dram_tensor("bias_sb", [128, 32], F32, kind="ExternalInput").ap()
    io["ident"] = nc.dram_tensor("ident", [128, 128], F32, kind="ExternalInput").ap()
    io["r_out"] = nc.dram_tensor("r_out", [T, D], F32, kind="ExternalOutput").ap()
    with ExitStack() as es:
        def sb(name, shape, dt):
            return es.enter_context(nc.sbuf_tensor(name, shape, dt))
        k = Kx(nc, es, SEMS)
        block = es.enter_context(nc.Block())

        @block.sync
        def _(sy):
            _emit_l1ffn(k, es, sb, nc, io)
    return nc


def _run_l1ffn(p, R):
    nc = _build_l1ffn()
    bias128 = np.ascontiguousarray(p["bias_p"].reshape(32, 128).T)
    maps = [dict(r_in=np.ascontiguousarray(R[c * T:(c + 1) * T]),
                 wbT=p["wbT_pk"], wrT=p["wrT_pk"], laT=p["laT_pk"], lbT=p["lbT"],
                 bias_sb=bias128, ident=p["ident"]) for c in range(NC)]
    res = run_bass_kernel_spmd(nc, maps, core_ids=list(range(NC)))
    return np.concatenate([res.results[c]["r_out"] for c in range(NC)], axis=0)




TH = 1152
DSSM = 8192
NHM = 64
CONVD = 8448
EPS_RMS = 1e-5


def _emit_mamba(k, es, sb, nc, io, groups=None):
    if groups is None:
        groups = [list(range(NC))]
    W = len(groups[0])
    stats = sb("stats", [128, 96], F32)
    epscol = sb("epscol", [128, 1], F32)
    onecol = sb("onecol", [128, 1], F32)
    ones1 = sb("ones1", [1, 128], F32)
    id_f32 = sb("id_f32", [128, 128], F32)
    id_bf = sb("id_bf", [128, 128], BF16)
    ltmask = sb("ltmask_sb", [128, 128], BF16)
    dt_sp = sb("dt_sp", [64, T], F32)
    cum = sb("cum", [64, T], F32)
    ctp = sb("ctp", [64, 18], F32)
    dtT = sb("dtT", [128, 8, 64], F32)
    ncumT = sb("ncumT", [128, 8, 64], F32)
    ecumT = sb("ecumT", [128, 8, 64], F32)
    wdtT = sb("wdtT", [128, 8, 64], F32)
    decT = sb("decT", [128, 8, 64], F32)
    pfT = sb("pfT", [128, 8, 64], F32)
    h_sb = sb("h_sb", [128, 64, 128], BF16)
    C_full = sb("C_full", [128, 8, 128], BF16)
    nw_sb = sb("nw_sb", [128, DSSM], BF16)
    Dcol = sb("Dcol", [128, 64], F32)
    cw_sb = sb("cwv", [128, 66, 4], F32)
    dtb = sb("dtbv", [64, 1], F32)
    alg = sb("algv", [64, 1], F32)
    wmsk = sb("wmskv", [8, 8], F32)
    selc = sb("selcv", [8, 1], F32)
    gct = sb("gct", [8, 64], F32)
    wco = sb("wco", [128, 8, 64], F32)
    dummyT = sb("dummyT", [128, 16], BF16)
    dummyS = sb("dummyS", [16, 128], BF16)
    ARE = sb("ARE", [128, 66816], BF16)

    k.g.memset(epscol[:], EPS_RMS)
    k.g.memset(onecol[:], 1.0)
    k.g.memset(ones1[:], 1.0)
    k.g.memset(h_sb[:], 0.0)
    k.g.memset(dummyS[:], 0.0)
    ge_m0 = k.inc(k.g.engine_nop(), "ge")
    k.wait(k.s, "ge", ge_m0)
    k.wait(k.v, "ge", ge_m0)
    k.wait(k.t, "ge", ge_m0)
    k.dinc(k.s.dma_start(out=dummyT[:], in_=dummyS[:], transpose=True), "dT")
    k.dinc(k.y.dma_start(out=id_f32[:], in_=io["ident"]), "dX")
    k.dinc(k.g.dma_start(out=id_bf[:], in_=io["ident"]), "dV")
    k.dinc(k.g.dma_start(out=ltmask[:], in_=io["ltmask"]), "dV")
    k.dinc(k.y.dma_start(out=cw_sb[:],
                         in_=io["conv_w"].rearrange("(a p) t -> p a t", p=128)), "dX")
    k.dinc(k.y.dma_start(out=dtb[:], in_=io["dt_bias"]), "dX")
    k.dinc(k.y.dma_start(out=alg[:], in_=io["A_log"]), "dX")
    k.dinc(k.y.dma_start(out=wmsk[:], in_=io["wmask"]), "dX")
    k.dinc(k.y.dma_start(out=selc[:], in_=io["selmask"]), "dX")
    with nc.sbuf_tensor("drow", [1, 64], F32) as drow, \
         nc.sbuf_tensor("nwrow", [1, 4096], F32) as nwrow, \
         nc.psum_tensor("psb0", [128, 2, 512], F32) as psb0:
        k.dinc(k.y.dma_start(out=drow[:], in_=io["D_param"]), "dX")
        k.wait(k.s, "dX")
        k.inc(k.s.activation(alg[:], alg[:], AF.Exp), "se")
        k.wait(k.t, "dX")
        k.wait(k.t, "ge", ge_m0)
        tp = k.inc(k.t.matmul(psb0[:, 0, 0:64], ones1[:], drow[:],
                              start=True, stop=True), "pe")
        k.wait(k.v, "pe", tp)
        k.inc(k.v.tensor_copy(Dcol[:], psb0[:, 0, 0:64]), "ve")
        for hf_ in range(2):
            k.wait(k.y, "pe")
            ldn = k.dinc(k.y.dma_start(out=nwrow[:],
                                       in_=io["norm_w"][:, hf_ * 4096:
                                                        (hf_ + 1) * 4096]), "dX")
            k.wait(k.t, "dX", ldn)
            for c8 in range(8):
                k.wait(k.t, "ve")
                tp = k.inc(k.t.matmul(psb0[:, c8 % 2, :], ones1[:],
                                      nwrow[:, c8 * 512:(c8 + 1) * 512],
                                      start=True, stop=True), "pe")
                k.wait(k.v, "pe", tp)
                k.inc(k.v.tensor_copy(
                    nw_sb[:, hf_ * 4096 + c8 * 512:hf_ * 4096 + (c8 + 1) * 512],
                    psb0[:, c8 % 2, :]), "ve")

    xh = ARE[:, 0:36864].rearrange("p (a b) -> p a b", a=32)
    dmpA = ARE[:, 36864:40960]
    dmpB = ARE[:, 40960:45056]
    rtA = ARE[:, 45056:53248].bitcast(F32)
    xhtA = ARE[:, 53248:57344]
    emit_prelnT(k, lambda tt: io["r_in"][tt * 128:(tt + 1) * 128, :],
                xh, 9, rtA, xhtA, (dmpA, dmpB), stats, 0, epscol)

    k.join(k.g, ("ve", "dT", "se"))
    wbuf = ARE[:, 36864:45056].rearrange("p (x a b) -> p x a b", x=2, a=32)
    stg = ARE[:, 45056:46080].rearrange("p (x b) -> p x b", x=2)
    with nc.psum_tensor("psA", [128, 2, 512], F32) as psA:
        wl = {}

        def load_win(j):
            nm = "dW" if j % 2 == 0 else "dW2"
            wl[j] = (nm, k.dinc(k.g.dma_start(
                out=wbuf[:, j % 2], in_=io["winP"][j]), nm))

        load_win(0)
        load_win(1)
        k.wait(k.t, "dT")
        evac = {}
        sdma = {}
        segs = [(128, 512), (640, 512)]
        segx = [(0, 512), (512, 512), (1024, 128)]
        u = 0
        for j in range(131):
            k.wait(k.t, wl[j][0], wl[j][1])
            isz = j < 64
            isx = 64 <= j < 130
            w_ = 128 if j < 130 else 64
            for (off, ln) in (segx if isx else segs):
                if u >= 2:
                    k.wait(k.t, "se", evac[u - 2])
                for ci in range(32):
                    mm = k.t.matmul(psA[0:w_, u % 2, 0:ln],
                                    wbuf[:, j % 2, ci, 0:w_],
                                    xh[:, ci, off:off + ln],
                                    start=(ci == 0), stop=(ci == 31))
                pj = k.inc(mm, "pe")
                k.wait(k.s, "pe", pj)
                if j == 130:
                    evac[u] = k.inc(k.s.activation(
                        dt_sp[:, off - 128:off - 128 + ln],
                        psA[0:64, u % 2, 0:ln], AF.Identity), "se")
                else:
                    if u >= 2 and (u - 2) in sdma:
                        k.wait(k.s, sdma[u - 2][0], sdma[u - 2][1])
                    evac[u] = k.inc(k.s.activation(stg[:, u % 2, 0:ln],
                                                   psA[:, u % 2, 0:ln],
                                                   AF.Identity), "se")
                    k.wait(k.y, "se", evac[u])
                    if isz:
                        dst = io["z_scr"][j, :, off - 128:off - 128 + ln]
                    else:
                        dst = io["xbc_scr"][j - 64, :, off:off + ln]
                    nmS = "dS" if u % 2 == 0 else "dS2"
                    sdma[u] = (nmS, k.dinc(k.y.dma_start(
                        out=dst, in_=stg[:, u % 2, 0:ln]), nmS))
                u += 1
            if j + 2 < 131:
                k.wait(k.g, "pe")
                load_win(j + 2)

    k.join(k.v, ("se",))
    dtt = ARE[:, 36864:45056].bitcast(F32)[0:64, 0:1024]
    prow = ARE[:, 45056:47232].bitcast(F32)[0:1, 0:1088]
    k.inc(k.v.tensor_scalar(dt_sp[:], dt_sp[:], dtb[:], None, op0=ALU.add), "ve")
    k.wait(k.v, "ve")
    k.inc(k.v.tensor_scalar(dtt[:], dt_sp[:], -1.0, None, op0=ALU.mult), "ve")
    k.wait(k.v, "ve")
    k.inc(k.v.tensor_tensor(dtt[:], dtt[:], dt_sp[:], op=ALU.min), "ve")
    k.wait(k.s, "ve")
    k.inc(k.s.activation(dtt[:], dtt[:], AF.Exp), "se")
    k.wait(k.s, "se")
    k.inc(k.s.activation(dtt[:], dtt[:], AF.Ln, bias=onecol[0:64, :]), "se")
    k.wait(k.v, "se")
    k.inc(k.v.tensor_scalar(dt_sp[:], dt_sp[:], 0.0, None, op0=ALU.max), "ve")
    k.wait(k.v, "ve")
    k.inc(k.v.tensor_tensor(dt_sp[:], dt_sp[:], dtt[:], op=ALU.add), "ve")
    k.wait(k.v, "ve")
    k.inc(k.v.tensor_scalar(dtt[:], dt_sp[:], alg[:], -1.0, op0=ALU.mult,
                            op1=ALU.mult), "ve")
    for ck in range(8):
        k.wait(k.v, "ve")
        k.inc(k.v.tensor_tensor_scan(cum[:, ck * 128:(ck + 1) * 128],
                                     dtt[:, ck * 128:(ck + 1) * 128],
                                     dtt[:, ck * 128:(ck + 1) * 128],
                                     0.0, op0=ALU.add, op1=ALU.bypass), "ve")
        k.wait(k.v, "ve")
        k.inc(k.v.tensor_copy(ctp[:, ck:ck + 1],
                              cum[:, ck * 128 + 127:ck * 128 + 128]), "ve")
    k.g.memset(ctp[0:64, 8:9], 0.0)
    k.inc(k.g.engine_nop(), "ge")
    k.wait(k.v, "ge")
    for ck in range(1, 9):
        k.wait(k.v, "ve")
        k.inc(k.v.tensor_tensor(ctp[:, 8 + ck:9 + ck], ctp[:, 7 + ck:8 + ck],
                                ctp[:, ck - 1:ck], op=ALU.add), "ve")
    k.wait(k.y, "ve")
    k.wait(k.y, "dS2")
    k.dinc(k.y.dma_start(out=io["cum_scr"].rearrange("h t -> (h t)"), in_=cum[:]),
           "dS")
    with nc.psum_tensor("psT3", [128, 2, 512], F32) as psT3, \
         nc.sbuf_tensor("ctpT", [17, 64], F32) as ctpT:
        k.wait(k.t, "ve")
        tpx = k.inc(k.t.transpose(psT3[0:17, 0, 0:64], ctp[0:64, 0:17],
                                  id_f32[0:64, 0:64]), "pe")
        k.wait(k.v, "pe", tpx)
        k.inc(k.v.tensor_copy(ctpT[:], psT3[0:17, 0, 0:64]), "ve")
        k.wait(k.y, "ve")
        k.dinc(k.y.dma_start(out=io["pf_scr"].rearrange("c h -> (c h)"),
                             in_=ctpT[:]), "dS")
        k.wait(k.y, "dS")
        k.dinc(k.y.dma_start(out=prow[:],
                             in_=io["pf_scr"].rearrange("c h -> (c h)")), "dX")
        for ck in range(8):
            k.wait(k.t, "ve")
            k.wait(k.t, "se")
            tp = k.inc(k.t.transpose(psT3[:, 0, 0:64],
                                     dt_sp[:, ck * 128:(ck + 1) * 128],
                                     id_f32[0:64, 0:64]), "pe")
            k.wait(k.v, "pe", tp)
            k.inc(k.v.tensor_copy(dtT[:, ck, :], psT3[:, 0, 0:64]), "ve")
            k.wait(k.t, "ve")
            tp = k.inc(k.t.transpose(psT3[:, 1, 0:64],
                                     cum[:, ck * 128:(ck + 1) * 128],
                                     id_f32[0:64, 0:64]), "pe")
            k.wait(k.v, "pe", tp)
            k.inc(k.v.tensor_scalar(ncumT[:, ck, :], psT3[:, 1, 0:64], -1.0, None,
                                    op0=ALU.mult), "ve")
            k.wait(k.s, "pe", tp)
            k.wait(k.s, "ve")
            k.inc(k.s.activation(ecumT[:, ck, :], psT3[:, 1, 0:64], AF.Exp), "se")
            k.wait(k.v, "se")
            k.inc(k.v.tensor_scalar(dtt[:, ck * 128:(ck + 1) * 128],
                                    cum[:, ck * 128:(ck + 1) * 128],
                                    ctp[:, ck:ck + 1], None, op0=ALU.subtract),
                  "ve")
            k.wait(k.s, "ve")
            k.inc(k.s.activation(dtt[:, ck * 128:(ck + 1) * 128],
                                 dtt[:, ck * 128:(ck + 1) * 128], AF.Exp,
                                 scale=-1.0), "se")
            k.wait(k.v, "se")
            k.inc(k.v.tensor_tensor(dtt[:, ck * 128:(ck + 1) * 128],
                                    dtt[:, ck * 128:(ck + 1) * 128],
                                    dt_sp[:, ck * 128:(ck + 1) * 128],
                                    op=ALU.mult), "ve")
            k.wait(k.t, "ve")
            tp = k.inc(k.t.transpose(psT3[:, 0, 64:128],
                                     dtt[:, ck * 128:(ck + 1) * 128],
                                     id_f32[0:64, 0:64]), "pe")
            k.wait(k.v, "pe", tp)
            k.inc(k.v.tensor_copy(wdtT[:, ck, :], psT3[:, 0, 64:128]), "ve")
            k.wait(k.t, "dX")
            k.wait(k.t, "se")
            tp = k.inc(k.t.matmul(psT3[:, 1, 64:128], ones1[:],
                                  prow[:, ck * 64:(ck + 1) * 64],
                                  start=True, stop=True), "pe")
            k.wait(k.s, "pe", tp)
            k.inc(k.s.activation(decT[:, ck, :], psT3[:, 1, 64:128], AF.Exp), "se")
            k.wait(k.t, "se")
            tp = k.inc(k.t.matmul(psT3[:, 1, 64:128], ones1[:],
                                  prow[:, (8 + ck) * 64:(9 + ck) * 64],
                                  start=True, stop=True), "pe")
            k.wait(k.s, "pe", tp)
            k.inc(k.s.activation(pfT[:, ck, :], psT3[:, 1, 64:128], AF.Exp), "se")
            k.wait(k.v, "se")
            k.inc(k.v.tensor_tensor(pfT[:, ck, :], pfT[:, ck, :], ecumT[:, ck, :],
                                    op=ALU.mult), "ve")
            k.wait(k.t, "se")
            k.wait(k.t, "ve")
    k.wait(k.y, "ve")
    k.dinc(k.y.dma_start(out=io["ct_in"], in_=ctp[0:64, 16:17]), "dS")

    cnv_in = ARE[:, 0:8646].rearrange("p (a b) -> p a b", a=66)
    cnv_f = ARE[:, 8704:17152].rearrange("p (a b) -> p a b", a=66)
    cnv_t = ARE[:, 17152:25600].rearrange("p (a b) -> p a b", a=66)
    xbc_s = ARE[:, 25600:34048].rearrange("p (a b) -> p a b", a=66)
    xT_sb = ARE[:, 34048:42240].rearrange("p (a b) -> p a b", a=64)
    yA = ARE[:, 42240:50432].rearrange("p (a b) -> p a b", a=64)
    cumrow = ARE[:, 50432:66816].bitcast(F32)[0:1, 0:8192]
    with nc.sbuf_tensor("BT_sb", [128, 128], BF16) as BT_sb, \
         nc.sbuf_tensor("Bw_sb", [128, 2, 128], BF16) as Bw_sb, \
         nc.sbuf_tensor("cbt_m", [128, 128], BF16) as cbt_m, \
         nc.sbuf_tensor("E_sb", [128, 2, 128], BF16) as E_sb, \
         nc.sbuf_tensor("MT_sb", [128, 2, 128], BF16) as MT_sb, \
         nc.sbuf_tensor("u_tmp", [128, 2, 128], F32) as u_tmp, \
         nc.psum_tensor("psB", [128, 8, 512], F32) as psB:
        prev_spill = None
        for ck in range(8):
            k.wait(k.y, "ve")
            k.wait(k.y, "se")
            k.wait(k.y, "dS")
            k.wait(k.y, "dS2")
            ld = k.dinc(k.y.dma_start(
                out=cnv_in,
                in_=io["xbc_scr"].rearrange("j p t -> p j t")[:, :,
                                                              125 + ck * 128:
                                                              256 + ck * 128]),
                "dX")
            ldc = k.dinc(k.y.dma_start(
                out=cumrow[:],
                in_=io["cum_scr"].rearrange("h (c t) -> c h t", c=8)[ck]), "dX2")
            k.wait(k.v, "dX", ld)
            k.wait(k.v, "pe")
            k.inc(k.v.tensor_tensor(cnv_f, cnv_in[:, :, 0:128],
                                    cw_sb[:, :, 0].broadcast_to([128, 66, 128]),
                                    op=ALU.mult), "ve")
            for tap in range(1, 4):
                k.wait(k.v, "ve")
                k.inc(k.v.tensor_tensor(cnv_t, cnv_in[:, :, tap:tap + 128],
                                        cw_sb[:, :, tap]
                                        .broadcast_to([128, 66, 128]),
                                        op=ALU.mult), "ve")
                k.wait(k.v, "ve")
                k.inc(k.v.tensor_tensor(cnv_f, cnv_f, cnv_t, op=ALU.add), "ve")
            k.wait(k.s, "ve")
            k.wait(k.s, "pe")
            k.wait(k.s, "dT")
            sil = k.inc(k.s.activation(xbc_s, cnv_f, AF.Silu), "se")
            k.inc(k.s.activation(C_full[:, ck, :], cnv_f[:, 65, :], AF.Silu), "se")
            k.wait(k.s, "se", sil)
            xtd = k.dinc(k.s.dma_start(
                out=xT_sb, in_=xbc_s.rearrange("p a b -> p (a b)")[:, 0:8192],
                transpose=True), "dT")
            btd = k.dinc(k.s.dma_start(
                out=BT_sb[:], in_=xbc_s[:, 64, :], transpose=True), "dT")
            k.wait(k.t, "se", sil)
            k.wait(k.t, "ve")
            cbmm = k.inc(k.t.matmul(psB[:, 0, 0:128], xbc_s[:, 64, :],
                                    xbc_s[:, 65, :], start=True, stop=True), "pe")
            k.wait(k.v, "pe", cbmm)
            k.wait(k.v, "dV")
            k.inc(k.v.tensor_tensor(cbt_m[:], psB[:, 0, 0:128], ltmask[:],
                                    op=ALU.mult), "ve")
            k.wait(k.v, "dT", btd)
            k.wait(k.t, "dX2", ldc)
            k.wait(k.t, "dT", xtd + 16)
            k.wait(k.t, "ve")
            if prev_spill is not None:
                k.wait(k.v, "dS", prev_spill)

            gmm = {}; clamp_ve = {}; bw_ve = {}; u_ve = {}; ya_ve = {}
            hp_ve = {}; mi_pe = {}; mj_pe = {}; ms_pe = {}
            for h in range(65):
                b = h % 2
                if h < 64:
                    if h >= 2:
                        k.wait(k.t, "ve", clamp_ve[h - 2])
                    gmm[h] = k.inc(k.t.matmul(
                        psB[:, b, 0:128], ones1[:],
                        cumrow[:, h * 128:(h + 1) * 128],
                        start=True, stop=True), "pe")
                if h >= 1:
                    hp = h - 1
                    bp = hp % 2
                    k.wait(k.t, "ve", bw_ve[hp])
                    if hp >= 2:
                        k.wait(k.t, "ve", ya_ve[hp - 2])
                    mi_pe[hp] = k.inc(k.t.matmul(
                        psB[:, 2 + bp, 0:128], MT_sb[:, bp, :], xT_sb[:, hp, :],
                        start=True, stop=True), "pe")
                    if hp >= 2:
                        k.wait(k.t, "ve", hp_ve[hp - 2])
                    mj_pe[hp] = k.inc(k.t.matmul(
                        psB[:, 4 + bp, 0:128], C_full[:, ck, :], h_sb[:, hp, :],
                        start=True, stop=True), "pe")
                    ms_pe[hp] = k.inc(k.t.matmul(
                        psB[:, 6 + bp, 0:128], Bw_sb[:, bp, :], xT_sb[:, hp, :],
                        start=True, stop=True), "pe")
                if h < 64:
                    k.wait(k.v, "pe", gmm[h])
                    clamp_ve[h] = k.inc(k.v.tensor_scalar(
                        E_sb[:, b, :], psB[:, b, 0:128], ncumT[:, ck, h:h + 1],
                        0.0, op0=ALU.add, op1=ALU.min), "ve")
                    k.wait(k.s, "ve", clamp_ve[h])
                    k.inc(k.s.activation(E_sb[:, b, :], E_sb[:, b, :], AF.Exp),
                          "se")
                    k.wait(k.v, "se")
                    k.inc(k.v.scalar_tensor_tensor(
                        MT_sb[:, b, :], E_sb[:, b, :], dtT[:, ck, h:h + 1],
                        cbt_m[:], op0=ALU.mult, op1=ALU.mult), "ve")
                    bw_ve[h] = k.inc(k.v.tensor_scalar(
                        Bw_sb[:, b, :], BT_sb[:], wdtT[:, ck, h:h + 1], None,
                        op0=ALU.mult), "ve")
                if h >= 1:
                    hp = h - 1
                    bp = hp % 2
                    k.wait(k.v, "pe", mi_pe[hp])
                    u_ve[hp] = k.inc(k.v.scalar_tensor_tensor(
                        u_tmp[:, bp, :], xT_sb[:, hp, :], Dcol[:, hp:hp + 1],
                        psB[:, 2 + bp, 0:128], op0=ALU.mult, op1=ALU.add), "ve")
                    k.wait(k.v, "pe", mj_pe[hp])
                    k.wait(k.v, "ve", u_ve[hp])
                    ya_ve[hp] = k.inc(k.v.scalar_tensor_tensor(
                        yA[:, hp, :], psB[:, 4 + bp, 0:128],
                        ecumT[:, ck, hp:hp + 1], u_tmp[:, bp, :],
                        op0=ALU.mult, op1=ALU.add), "ve")
                    k.wait(k.v, "pe", ms_pe[hp])
                    hp_ve[hp] = k.inc(k.v.scalar_tensor_tensor(
                        h_sb[:, hp, :], h_sb[:, hp, :], decT[:, ck, hp:hp + 1],
                        psB[:, 6 + bp, 0:128], op0=ALU.mult, op1=ALU.add), "ve")
            k.wait(k.y, "ve")
            prev_spill = k.dinc(k.y.dma_start(
                out=io["ypre_scr"][ck],
                in_=yA.rearrange("p a b -> p (a b)")), "dS")
        k.wait(k.v, "dS")

    k.wait(k.y, "ve")
    k.wait(k.y, "dS")
    k.wait(k.y, "dS2")
    k.dinc(k.y.dma_start(out=io["st_in"].rearrange("n h pp -> n (h pp)"),
                         in_=h_sb.rearrange("p a b -> p (a b)")), "dS")
    k.wait(k.g, "dS")
    cc1 = k.inc(k.g.collective_compute(
        "AllGather", ALU.bypass, replica_groups=groups,
        ins=[io["st_in"].rearrange("n h pp -> n (h pp)")],
        outs=[io["st_out"][0:W].rearrange("c n hp -> (c n) hp")]), "cc")
    cc2 = k.inc(k.g.collective_compute(
        "AllGather", ALU.bypass, replica_groups=groups,
        ins=[io["ct_in"]], outs=[io["ct_out"][0:W].rearrange("c h -> (c h)")]), "cc")
    k.wait(k.g, "cc", cc2)
    k.g.memset(gct[:], 0.0)
    k.inc(k.g.engine_nop(), "ge")
    k.wait(k.g, "ge")
    k.dinc(k.g.dma_start(out=gct[0:W, :], in_=io["ct_out"][0:W]), "dV")
    Hf = ARE[:, 34048:50432].rearrange("p (a b c) -> p a b c", a=2, b=64)
    with nc.psum_tensor("psC", [128, 2, 512], F32) as psC:
        k.wait(k.t, "dV")
        mm = k.inc(k.t.matmul(psC[0:8, 0, 0:64], wmsk[:], gct[:],
                              start=True, stop=True), "pe")
        k.wait(k.s, "pe", mm)
        k.inc(k.s.activation(gct[:], psC[0:8, 0, 0:64], AF.Exp), "se")
        k.wait(k.v, "se")
        k.inc(k.v.tensor_scalar(gct[:], gct[:], selc[:], None, op0=ALU.mult), "ve")
        k.wait(k.y, "ve")
        k.dinc(k.y.dma_start(out=io["w_scr"].rearrange("c h -> (c h)"), in_=gct[:]),
               "dS")
        Sg = ARE[:, 0:16384].rearrange("p (x a b) -> p x a b", x=2, a=64)
        with nc.sbuf_tensor("wrow", [1, 512], F32) as wrow:
            k.wait(k.y, "dS")
            k.dinc(k.y.dma_start(out=wrow[:],
                                 in_=io["w_scr"].rearrange("c h -> (c h)")), "dX")
            k.wait(k.t, "dX")
            for cc_ in range(8):
                k.wait(k.t, "ve")
                k.wait(k.t, "se")
                mm = k.inc(k.t.matmul(psC[:, cc_ % 2, 0:64], ones1[:],
                                      wrow[:, cc_ * 64:(cc_ + 1) * 64],
                                      start=True, stop=True), "pe")
                k.wait(k.v, "pe", mm)
                k.inc(k.v.tensor_copy(wco[:, cc_, :], psC[:, cc_ % 2, 0:64]), "ve")
            k.g.memset(Hf[:, 0], 0.0)
            k.inc(k.g.engine_nop(), "ge")
            k.wait(k.g, "cc", cc1)
            lds = {}

            def load_sg(cc_):
                nm = "dG" if cc_ % 2 == 0 else "dG2"
                lds[cc_] = (nm, k.dinc(k.g.dma_start(
                    out=Sg[:, cc_ % 2].rearrange("p a b -> p (a b)"),
                    in_=io["st_out"][cc_]), nm))

            load_sg(0)
            if W > 1:
                load_sg(1)
            for cc_ in range(W):
                k.wait(k.v, lds[cc_][0], lds[cc_][1])
                k.wait(k.v, "ge")
                for h in range(64):
                    k.wait(k.v, "ve")
                    k.inc(k.v.scalar_tensor_tensor(
                        Hf[:, (cc_ + 1) % 2, h, :], Sg[:, cc_ % 2, h, :],
                        wco[:, cc_, h:h + 1], Hf[:, cc_ % 2, h, :],
                        op0=ALU.mult, op1=ALU.add), "ve")
                if cc_ + 2 < W:
                    k.wait(k.g, "ve")
                    load_sg(cc_ + 2)
    Hfin = Hf[:, W % 2]

    yF = ARE[:, 0:32768].rearrange("p (a b) -> p a b", a=64)
    zT = ARE[:, 50432:58624]
    yB2 = ARE[:, 58624:66816].rearrange("p (a b) -> p a b", a=64)
    wbuf3 = ARE[:, 50432:66816].rearrange("p (x a b) -> p x a b", x=2, a=64)
    with nc.sbuf_tensor("fst", [128, 2, 512], BF16) as fst, \
         nc.psum_tensor("psD", [128, 4, 512], F32) as psD:
        evg = {}
        fdma = {}
        for half in range(2):
            for ck4 in range(4):
                ck = half * 4 + ck4
                k.wait(k.y, "pe")
                k.wait(k.y, "ve")
                k.wait(k.y, "se")
                ldy = k.dinc(k.y.dma_start(
                    out=yF[:, :, ck4 * 128:(ck4 + 1) * 128],
                    in_=io["ypre_scr"][ck].rearrange("p (a b) -> p a b", a=64)),
                    "dX")
                k.wait(k.s, "pe")
                k.wait(k.s, "dT")
                ldz = k.dinc(k.s.dma_start(
                    out=zT.rearrange("p f -> p f"),
                    in_=io["z_scr"].rearrange("j p t -> (j p) t")
                    [:, ck * 128:(ck + 1) * 128], transpose=True), "dT")
                k.wait(k.v, "dX", ldy)
                stt_c = {}
                for h in range(64):
                    if h < 2:
                        k.wait(k.t, "ve")
                    else:
                        k.wait(k.t, "ve", stt_c[h - 2])
                    mm = k.inc(k.t.matmul(psD[:, h % 2, 0:128], C_full[:, ck, :],
                                          Hfin[:, h, :], start=True, stop=True),
                               "pe")
                    k.wait(k.v, "pe", mm)
                    stt_c[h] = k.inc(k.v.scalar_tensor_tensor(
                        yB2[:, h, :], psD[:, h % 2, 0:128], pfT[:, ck, h:h + 1],
                        yF[:, h, ck4 * 128:(ck4 + 1) * 128],
                        op0=ALU.mult, op1=ALU.add), "ve")
                scr = yF[:, :, ck4 * 128:(ck4 + 1) * 128]
                k.wait(k.s, "dT", ldz)
                k.wait(k.s, "ve")
                sil = k.inc(k.s.activation(scr, zT.rearrange("p (a b) -> p a b",
                                                             a=64), AF.Silu),
                            "se")
                k.wait(k.v, "se", sil)
                k.inc(k.v.tensor_tensor(zT.rearrange("p (a b) -> p a b", a=64),
                                        yB2, scr, op=ALU.mult), "ve")
                k.wait(k.s, "ve")
                k.inc(k.s.activation(scr, zT.rearrange("p (a b) -> p a b", a=64),
                                     AF.Square, accum_out=stats[:, 80:81]), "se")
                k.wait(k.v, "se")
                k.wait(k.v, "ve")
                k.inc(k.v.tensor_scalar(stats[:, 81:82], stats[:, 80:81],
                                        1.0 / DSSM, None, op0=ALU.mult), "ve")
                k.wait(k.s, "ve")
                k.inc(k.s.activation(stats[:, 81:82], stats[:, 81:82], AF.Ln,
                                     bias=epscol[:]), "se")
                k.wait(k.s, "se")
                k.inc(k.s.activation(stats[:, 81:82], stats[:, 81:82], AF.Exp,
                                     scale=-0.5), "se")
                k.wait(k.v, "se")
                k.inc(k.v.scalar_tensor_tensor(
                    zT.rearrange("p (a b) -> p a b", a=64),
                    zT.rearrange("p (a b) -> p a b", a=64),
                    stats[:, 81:82], nw_sb.rearrange("p (a b) -> p a b", a=64),
                    op0=ALU.mult, op1=ALU.mult), "ve")
                k.wait(k.s, "ve")
                k.dinc(k.s.dma_start(out=yF[:, :, ck4 * 128:(ck4 + 1) * 128],
                                     in_=zT, transpose=True), "dT")
                k.wait(k.v, "dT")
                k.wait(k.t, "dT")
            wl3 = {}

            def load_wo(j):
                nm = "dW" if j % 2 == 0 else "dW2"
                wl3[j] = (nm, k.dinc(k.g.dma_start(
                    out=wbuf3[:, j % 2], in_=io["woutP"][j]), nm))

            k.join(k.g, ("pe", "ve", "se", "dT"))
            load_wo(0)
            load_wo(1)
            for j in range(32):
                g_ = half * 32 + j
                k.wait(k.t, wl3[j][0], wl3[j][1])
                if g_ >= 2:
                    k.wait(k.t, "se", evg[g_ - 2])
                for ci in range(64):
                    mm = k.t.matmul(psD[:, 2 + j % 2, :],
                                    wbuf3[:, j % 2, ci, :], yF[:, ci, :],
                                    start=(ci == 0), stop=(ci == 63))
                pj = k.inc(mm, "pe")
                k.wait(k.s, "pe", pj)
                if g_ >= 2:
                    k.wait(k.s, fdma[g_ - 2][0], fdma[g_ - 2][1])
                evg[g_] = k.inc(k.s.activation(fst[:, j % 2, :],
                                               psD[:, 2 + j % 2, :],
                                               AF.Identity), "se")
                k.wait(k.y, "se", evg[g_])
                nmF = "dS" if j % 2 == 0 else "dS2"
                fdma[g_] = (nmF, k.dinc(k.y.dma_start(
                    out=io["f_scr"][j, :, half * 512:half * 512 + 512],
                    in_=fst[:, j % 2, :]), nmF))
                if j + 2 < 32:
                    k.wait(k.g, "pe")
                    load_wo(j + 2)

    fT2 = ARE[:, 0:32768].rearrange("p (a b) -> p a b", a=8)
    rtru = ARE[:, 32768:49152].bitcast(F32)
    rt4 = rtru[:, 0:4096]
    ru4 = rtru[:, 4096:8192]
    dmpC = ARE[:, 49152:53248]
    dmpD = ARE[:, 53248:57344]
    k.join(k.y, ("pe", "se", "dS", "dS2", "ve"))
    k.join(k.s, ("dS", "dS2", "ve", "pe"))
    ldf = k.dinc(k.s.dma_start(out=fT2,
                               in_=io["f_scr"].rearrange("j p t -> (j p) t"),
                               transpose=True), "dT")
    for tt in range(8):
        k.wait(k.y, "dS")
        k.wait(k.y, "ve")
        k.dinc(k.y.dma_start(out=rt4,
                             in_=io["r_in"][128 + tt * 128:256 + tt * 128, :]),
               "dX")
        fview = fT2[:, tt, :]
        c0 = 48 + 4 * tt
        k.wait(k.v, "dT", ldf)
        k.wait(k.v, "dX")
        k.inc(k.v.tensor_scalar(dmpC, fview, 1.0, 0.0, op0=ALU.mult,
                                op1=ALU.add, accum_out=stats[:, c0:c0 + 1]), "ve")
        k.wait(k.s, "dT", ldf)
        k.wait(k.s, "ve")
        k.inc(k.s.activation(dmpD, fview, AF.Square,
                             accum_out=stats[:, c0 + 3:c0 + 4]), "se")
        mcol = stats[:, c0 + 1:c0 + 2]
        vcol = stats[:, c0 + 2:c0 + 3]
        k.wait(k.v, "se")
        k.wait(k.v, "ve")
        k.inc(k.v.tensor_scalar(mcol, stats[:, c0:c0 + 1], 1.0 / D, None,
                                op0=ALU.mult), "ve")
        k.wait(k.v, "ve")
        k.inc(k.v.tensor_tensor(vcol, mcol, mcol, op=ALU.mult), "ve")
        k.wait(k.v, "ve")
        k.inc(k.v.scalar_tensor_tensor(vcol, stats[:, c0 + 3:c0 + 4], 1.0 / D,
                                       vcol, op0=ALU.mult, op1=ALU.subtract),
              "ve")
        k.wait(k.s, "ve")
        k.inc(k.s.activation(vcol, vcol, AF.Ln, bias=epscol[:]), "se")
        k.wait(k.s, "se")
        k.inc(k.s.activation(vcol, vcol, AF.Exp, scale=-0.5), "se")
        k.wait(k.v, "se")
        k.inc(k.v.tensor_scalar(ru4, fview, mcol, vcol, op0=ALU.subtract,
                                op1=ALU.mult), "ve")
        k.wait(k.v, "ve")
        k.inc(k.v.tensor_tensor(ru4, ru4, rt4, op=ALU.add), "ve")
        k.wait(k.y, "ve")
        k.dinc(k.y.dma_start(out=io["r_out"][tt * 128:(tt + 1) * 128, :],
                             in_=ru4), "dS")
    k.wait(k.y, "dS")


def _build_mamba(groups=None):
    nc = bass.Bass("TRN2", target_bir_lowering=False)
    io = {}
    io["r_in"] = nc.dram_tensor("r_in", [TH, D], F32, kind="ExternalInput").ap()
    io["winP"] = nc.dram_tensor("winP", [131, 128, 32, 128], F32, kind="ExternalInput").ap()
    io["woutP"] = nc.dram_tensor("woutP", [32, 128, 64, 128], F32, kind="ExternalInput").ap()
    io["conv_w"] = nc.dram_tensor("conv_w", [CONVD, 4], F32, kind="ExternalInput").ap()
    io["dt_bias"] = nc.dram_tensor("dt_bias", [64, 1], F32, kind="ExternalInput").ap()
    io["A_log"] = nc.dram_tensor("A_log", [64, 1], F32, kind="ExternalInput").ap()
    io["D_param"] = nc.dram_tensor("D_param", [1, 64], F32, kind="ExternalInput").ap()
    io["norm_w"] = nc.dram_tensor("norm_w", [1, DSSM], F32, kind="ExternalInput").ap()
    io["ident"] = nc.dram_tensor("ident", [128, 128], F32, kind="ExternalInput").ap()
    io["ltmask"] = nc.dram_tensor("ltmask", [128, 128], F32, kind="ExternalInput").ap()
    io["wmask"] = nc.dram_tensor("wmask", [8, 8], F32, kind="ExternalInput").ap()
    io["selmask"] = nc.dram_tensor("selmask", [8, 1], F32, kind="ExternalInput").ap()
    io["r_out"] = nc.dram_tensor("r_out", [T, D], F32, kind="ExternalOutput").ap()
    io["z_scr"] = nc.dram_tensor("z_scr", [64, 128, T], BF16).ap()
    io["xbc_scr"] = nc.dram_tensor("xbc_scr", [66, 128, TH], BF16).ap()
    io["ypre_scr"] = nc.dram_tensor("ypre_scr", [8, 128, DSSM], BF16).ap()
    io["f_scr"] = nc.dram_tensor("f_scr", [32, 128, T], BF16).ap()
    io["cum_scr"] = nc.dram_tensor("cum_scr", [64, T], F32).ap()
    io["pf_scr"] = nc.dram_tensor("pf_scr", [17, 64], F32).ap()
    io["w_scr"] = nc.dram_tensor("w_scr", [8, 64], F32).ap()
    io["st_in"] = nc.dram_tensor("st_in", [128, 64, 128], BF16).ap()
    io["st_out"] = nc.dram_tensor("st_out", [8, 128, 64 * 128], BF16).ap()
    io["ct_in"] = nc.dram_tensor("ct_in", [64], F32).ap()
    io["ct_out"] = nc.dram_tensor("ct_out", [8, 64], F32).ap()
    with ExitStack() as es:
        def sb(name, shape, dt):
            return es.enter_context(nc.sbuf_tensor(name, shape, dt))
        k = Kx(nc, es, SEMS)
        block = es.enter_context(nc.Block())

        @block.sync
        def _(sy):
            _emit_mamba(k, es, sb, nc, io, groups=groups)
    return nc


def _run_mamba(p, R):
    nc = _build_mamba()
    lt = np.tril(np.ones((128, 128), np.float32)).T
    maps = []
    for c in range(NC):
        rin = np.zeros((TH, D), np.float32)
        rin[128:] = R[c * T:(c + 1) * T]
        if c > 0:
            rin[125:128] = R[c * T - 3:c * T]
        wm = np.zeros((8, 8), np.float32)
        for cp in range(8):
            for cpp in range(cp + 1, c):
                wm[cpp, cp] = 1.0
        sel = np.zeros((8, 1), np.float32)
        sel[:c, 0] = 1.0
        maps.append(dict(
            r_in=rin, winP=p["winP_pk"], woutP=p["woutP_pk"], conv_w=p["conv_w"],
            dt_bias=np.ascontiguousarray(p["dt_bias"].reshape(64, 1)),
            A_log=np.ascontiguousarray(p["A_log"].reshape(64, 1)),
            D_param=p["D_param"], norm_w=p["norm_w"], ident=p["ident"],
            ltmask=np.ascontiguousarray(lt), wmask=wm, selmask=sel))
    res = run_bass_kernel_spmd(nc, maps, core_ids=list(range(NC)))
    return np.concatenate([res.results[c]["r_out"] for c in range(NC)], axis=0)





class _NCProxy:
    def __init__(self, nc, prefix):
        self._nc = nc
        self._p = prefix

    def sbuf_tensor(self, name, shape, dt):
        return self._nc.sbuf_tensor(self._p + name, shape, dt)

    def psum_tensor(self, name, shape, dt):
        return self._nc.psum_tensor(self._p + name, shape, dt)

    def __getattr__(self, a):
        return getattr(self._nc, a)


def _full_join(k):
    for eng in (k.t, k.v, k.s, k.g, k.y):
        k.join(eng, ("pe", "ve", "se", "ge", "dW", "dW2", "dV", "dX", "dX2",
                     "dG", "dG2", "dT", "dS", "dS2", "cc"))


def _emit_halo_exchange(k, es, sb, nc, io):
    zrows = sb("zrows", [128, D], F32)
    ohot = sb("ohot", [8, 1], F32)
    halv = sb("halv", [8, 3 * D], F32)
    hsel = sb("hsel", [1, 3 * D], F32)
    k.g.memset(zrows[:], 0.0)
    k.inc(k.g.engine_nop(), "ge")
    k.wait(k.y, "ge")
    k.dinc(k.y.dma_start(out=io["r2"][0:125, :], in_=zrows[0:125, :]), "dS")
    k.dinc(k.y.dma_start(out=io["ohot"] if False else ohot[:],
                         in_=io["onehot"]), "dX")
    k.wait(k.y, "dS")
    k.dinc(k.y.dma_start(out=io["hal_in"],
                         in_=io["r2"][1149:1152, :].rearrange("a b -> (a b)")),
           "dS")
    k.wait(k.g, "dS")
    ccx = k.inc(k.g.collective_compute(
        "AllGather", ALU.bypass, replica_groups=[list(range(NC))],
        ins=[io["hal_in"]], outs=[io["hal_out"].rearrange("c f -> (c f)")]), "cc")
    k.wait(k.g, "cc", ccx)
    k.dinc(k.g.dma_start(out=halv[:], in_=io["hal_out"]), "dG")
    with nc.psum_tensor("psH", [128, 512], F32) as psH:
        k.wait(k.t, "dG")
        k.wait(k.t, "dX")
        pes = []
        for i6 in range(24):
            mm = k.t.matmul(psH[0:1, :], ohot[:],
                            halv[:, i6 * 512:(i6 + 1) * 512],
                            start=True, stop=True)
            if i6 % 4 == 3 or i6 == 23:
                pass
            pes.append(k.inc(mm, "pe"))
            k.wait(k.v, "pe", pes[-1])
            k.inc(k.v.tensor_copy(hsel[:, i6 * 512:(i6 + 1) * 512], psH[0:1, :]),
                  "ve")
            k.wait(k.t, "ve")
        k.wait(k.y, "ve")
        k.dinc(k.y.dma_start(out=io["r2"][125:128, :].rearrange("a b -> (a b)"),
                             in_=hsel[:]), "dS")
    k.wait(k.y, "dS")


def _build_fused():
    nc = bass.Bass("TRN2", target_bir_lowering=False)
    io = {}

    def din(name, shape):
        io[name] = nc.dram_tensor(name, shape, F32, kind="ExternalInput").ap()

    din("xin", [XT, D])
    din("wqT", [32, 128, 32, 128]); din("wkv", [2, 128, 32, 128])
    din("woT", [32, 128, 32, 128])
    din("cosT", [128, XT]); din("sinrotT", [128, XT]); din("prot", [128, 128])
    din("ident", [128, 128]); din("maskneg", [128, NKT]); din("uw", [64, T])
    din("up", [16, 128, 2, 1024]); din("down", [16, 128, 8, 256])
    din("winP", [131, 128, 32, 128]); din("woutP", [32, 128, 64, 128])
    din("conv_w", [CONVD, 4])
    din("dt_bias", [64, 1]); din("A_log", [64, 1]); din("D_param", [1, 64])
    din("norm_w", [1, DSSM]); din("ltmask", [128, 128]); din("wmask", [8, 8])
    din("selmask", [8, 1]); din("onehot", [8, 1])
    din("wbT", [32, 128, 32, 128]); din("wrT", [128, 32, 8])
    din("laT", [128, 32, 128]); din("lbT", [128, D]); din("bias_sb", [128, 32])
    io["out"] = nc.dram_tensor("out", [T, D], F32, kind="ExternalOutput").ap()
    io["r1"] = nc.dram_tensor("r1", [T, D], F32).ap()
    io["r2"] = nc.dram_tensor("r2", [TH, D], F32).ap()
    io["r3"] = nc.dram_tensor("r3", [T, D], F32).ap()
    io["rs_scr"] = nc.dram_tensor("rs_scr", [2, 8, 4, 128], BF16).ap()
    io["z_scr"] = nc.dram_tensor("z_scr", [64, 128, T], BF16).ap()
    io["xbc_scr"] = nc.dram_tensor("xbc_scr", [66, 128, TH], BF16).ap()
    io["ypre_scr"] = nc.dram_tensor("ypre_scr", [8, 128, DSSM], BF16).ap()
    io["f_scr"] = nc.dram_tensor("f_scr", [32, 128, T], BF16).ap()
    io["cum_scr"] = nc.dram_tensor("cum_scr", [64, T], F32).ap()
    io["pf_scr"] = nc.dram_tensor("pf_scr", [17, 64], F32).ap()
    io["w_scr"] = nc.dram_tensor("w_scr", [8, 64], F32).ap()
    io["st_in"] = nc.dram_tensor("st_in", [128, 64, 128], BF16).ap()
    io["st_out"] = nc.dram_tensor("st_out", [8, 128, 64 * 128], BF16).ap()
    io["ct_in"] = nc.dram_tensor("ct_in", [64], F32).ap()
    io["ct_out"] = nc.dram_tensor("ct_out", [8, 64], F32).ap()
    io["hal_in"] = nc.dram_tensor("hal_in", [3 * D], F32).ap()
    io["hal_out"] = nc.dram_tensor("hal_out", [8, 3 * D], F32).ap()

    with ExitStack() as es:
        k = Kx(nc, es, SEMS)
        block = es.enter_context(nc.Block())

        @block.sync
        def _(sy):
            def run(tag, fn, sub_io):
                ncp = _NCProxy(nc, tag)
                with ExitStack() as es2:
                    def sb2(name, shape, dt):
                        return es2.enter_context(
                            nc.sbuf_tensor(tag + name, shape, dt))
                    fn(k, es2, sb2, ncp, sub_io)
                _full_join(k)

            run("A_", _emit_l0attn, dict(
                xin=io["xin"], wqT=io["wqT"], wkv=io["wkv"],
                woT=io["woT"], cosT=io["cosT"], sinrotT=io["sinrotT"],
                prot=io["prot"], ident=io["ident"], maskneg=io["maskneg"],
                uw=io["uw"], r_out=io["r1"], rs_scr=io["rs_scr"]))
            run("B_", _emit_l0ffn, dict(
                r_in=io["r1"], up=io["up"], down=io["down"],
                r_out=io["r2"][128:TH, :]))
            run("H_", _emit_halo_exchange, dict(
                r2=io["r2"], onehot=io["onehot"], hal_in=io["hal_in"],
                hal_out=io["hal_out"]))
            run("M_", _emit_mamba, dict(
                r_in=io["r2"], winP=io["winP"], woutP=io["woutP"],
                conv_w=io["conv_w"], dt_bias=io["dt_bias"], A_log=io["A_log"],
                D_param=io["D_param"], norm_w=io["norm_w"], ident=io["ident"],
                ltmask=io["ltmask"], wmask=io["wmask"], selmask=io["selmask"],
                r_out=io["r3"], z_scr=io["z_scr"], xbc_scr=io["xbc_scr"],
                ypre_scr=io["ypre_scr"], f_scr=io["f_scr"],
                cum_scr=io["cum_scr"], pf_scr=io["pf_scr"], w_scr=io["w_scr"],
                st_in=io["st_in"], st_out=io["st_out"], ct_in=io["ct_in"],
                ct_out=io["ct_out"]))
            run("F_", _emit_l1ffn, dict(
                r_in=io["r3"], wbT=io["wbT"], wrT=io["wrT"], laT=io["laT"],
                lbT=io["lbT"], bias_sb=io["bias_sb"], ident=io["ident"],
                r_out=io["out"]))
    return nc


_FUSED_CACHE = {}


def _run_fused(p):
    nc = _build_fused()
    lt = np.tril(np.ones((128, 128), np.float32)).T
    bias128 = np.ascontiguousarray(p["bias_p"].reshape(32, 128).T)
    R = p["x_p"]
    maps = []
    for c in range(NC):
        g0 = c * T - HALO
        xin = np.zeros((XT, D), dtype=np.float32)
        lo = max(0, g0); hi = min(S, g0 + XT)
        xin[lo - g0: hi - g0] = R[lo:hi]
        wm = np.zeros((8, 8), np.float32)
        for cp in range(8):
            for cpp in range(cp + 1, c):
                wm[cpp, cp] = 1.0
        sel = np.zeros((8, 1), np.float32)
        sel[:c, 0] = 1.0
        oh = np.zeros((8, 1), np.float32)
        if c > 0:
            oh[c - 1, 0] = 1.0
        maps.append(dict(
            xin=xin, wqT=p["wqT_pk"], wkv=p["wkv_pk"], woT=p["woT_pk"],
            cosT=p["cosT"][c], sinrotT=p["sinrotT"][c], prot=p["P_rot"],
            ident=p["ident"], maskneg=p["maskneg"][c], uw=p["uw"][c],
            up=p["up_pk"], down=p["down_pk"],
            winP=p["winP_pk"], woutP=p["woutP_pk"], conv_w=p["conv_w"],
            dt_bias=np.ascontiguousarray(p["dt_bias"].reshape(64, 1)),
            A_log=np.ascontiguousarray(p["A_log"].reshape(64, 1)),
            D_param=p["D_param"], norm_w=p["norm_w"],
            ltmask=np.ascontiguousarray(lt), wmask=wm, selmask=sel, onehot=oh,
            wbT=p["wbT_pk"], wrT=p["wrT_pk"], laT=p["laT_pk"], lbT=p["lbT"],
            bias_sb=bias128))
    res = run_bass_kernel_spmd(nc, maps, core_ids=list(range(NC)))
    return np.concatenate([res.results[c]["out"] for c in range(NC)], axis=0)


def kernel(**inputs):
    p = _prep(inputs)
    R = _run_fused(p)
    out = np.empty((S, D), dtype=np.float32)
    out[:, p["perm"]] = R
    return out.reshape(1, S, D)


def kernel_unfused(**inputs):
    p = _prep(inputs)
    R = p["x_p"]
    R = _run_l0attn(p, R)
    R = _run_l0ffn(p, R)
    R = _run_mamba(p, R)
    R = _run_l1ffn(p, R)
    out = np.empty((S, D), dtype=np.float32)
    out[:, p["perm"]] = R
    return out.reshape(1, S, D)


# version 05
# speedup vs baseline: 1.1971x; 1.0129x over previous
import numpy as np
import ml_dtypes
from contextlib import ExitStack
import concourse.bass as bass
import bass_rust
import concourse.mybir as mybir
from concourse.bass_utils import run_bass_kernel_spmd

F32 = mybir.dt.float32
BF16 = mybir.dt.bfloat16
AF = mybir.ActivationFunctionType
ALU = mybir.AluOpType
BF = ml_dtypes.bfloat16

NC = 8
S = 8192
D = 4096
T = S // NC
HALO = 256
XT = T + 2 * HALO
NKT = XT // 128
DH = 128
EPS_LN = 1e-5
SCALE0 = 1.0 / float(np.sqrt(DH))



def _prep(I):
    p = {}
    perm = np.asarray(I["L0_ffn_perm"]).astype(np.int64)
    p["perm"] = perm

    x = np.asarray(I["L0_attn_x"]).reshape(S, D)
    p["x_p"] = np.ascontiguousarray(x[:, perm])

    p["wqT"] = np.ascontiguousarray(np.asarray(I["L0_attn_Wq"]).T[perm, :])
    p["wkT"] = np.ascontiguousarray(np.asarray(I["L0_attn_Wk"]).T[perm, :])
    p["wvT"] = np.ascontiguousarray(np.asarray(I["L0_attn_Wv"]).T[perm, :])
    p["woT"] = np.ascontiguousarray(np.asarray(I["L0_attn_Wo"])[perm, :].T)

    base = 10000.0
    inv_freq = 1.0 / base ** (np.arange(0, DH, 2, dtype=np.float32) / DH)
    t_ = np.arange(S, dtype=np.float32)
    f = np.outer(t_, inv_freq)
    cos = np.zeros((S, DH), dtype=np.float32)
    sin = np.zeros((S, DH), dtype=np.float32)
    cos[:, 0::2] = np.cos(f); cos[:, 1::2] = np.cos(f)
    sin[:, 0::2] = np.sin(f); sin[:, 1::2] = np.sin(f)
    cosT = np.ascontiguousarray(cos.T)
    sinT = np.ascontiguousarray(sin.T)
    sinrotT = np.concatenate([sinT[64:], sinT[:64]], axis=0)

    P = np.zeros((128, 128), dtype=np.float32)
    for a in range(64):
        P[a + 64, a] = -1.0
        P[a, a + 64] = 1.0
    p["P_rot"] = P
    p["ident"] = np.eye(128, dtype=np.float32)

    p["up"] = np.ascontiguousarray(np.asarray(I["L0_ffn_up"]))
    p["down"] = np.ascontiguousarray(np.asarray(I["L0_ffn_down"]))

    p["winP"] = np.ascontiguousarray(np.asarray(I["L1_attn_W_in"])[perm, :])
    p["woutP"] = np.ascontiguousarray(np.asarray(I["L1_attn_W_out"])[:, perm])
    p["conv_w"] = np.ascontiguousarray(np.asarray(I["L1_attn_conv_w"]))
    p["dt_bias"] = np.ascontiguousarray(np.asarray(I["L1_attn_dt_bias"]).reshape(1, 64))
    p["A_log"] = np.ascontiguousarray(np.asarray(I["L1_attn_A_log"]).reshape(1, 64))
    p["D_param"] = np.ascontiguousarray(np.asarray(I["L1_attn_D_param"]).reshape(1, 64))
    p["norm_w"] = np.ascontiguousarray(np.asarray(I["L1_attn_norm_weight"]).reshape(1, 8192))

    p["wbT"] = np.ascontiguousarray(np.asarray(I["L1_ffn_Wb"]).T[perm, :][:, perm])
    p["bias_p"] = np.ascontiguousarray(np.asarray(I["L1_ffn_bias"])[perm].reshape(1, D))
    p["wrT"] = np.ascontiguousarray(np.asarray(I["L1_ffn_Wr"]).T[perm, :])
    A = np.asarray(I["L1_ffn_A"])
    p["laT"] = np.ascontiguousarray(A[:, :, perm].transpose(2, 0, 1).reshape(D, 128))
    Bm = np.asarray(I["L1_ffn_Bm"])
    p["lbT"] = np.ascontiguousarray(Bm[:, perm, :].transpose(0, 2, 1).reshape(128, D))

    def _pack(W, ob=128):
        C, O = W.shape
        return np.ascontiguousarray(
            W.reshape(C // 128, 128, O // ob, ob).transpose(2, 1, 0, 3))

    p["wqT_pk"] = _pack(p["wqT"])
    p["woT_pk"] = _pack(p["woT"])
    p["wbT_pk"] = _pack(p["wbT"])
    p["wkv_pk"] = np.ascontiguousarray(np.stack([
        p["wkT"].reshape(32, 128, 128).transpose(1, 0, 2),
        p["wvT"].reshape(32, 128, 128).transpose(1, 0, 2)]))
    winP_pad = np.concatenate(
        [p["winP"], np.zeros((D, 64), np.float32)], axis=1)
    p["winP_pk"] = _pack(winP_pad)
    p["woutP_pk"] = _pack(p["woutP"])
    p["up_pk"] = np.ascontiguousarray(
        p["up"].reshape(16, 2, 128, 1024).transpose(0, 2, 1, 3))
    p["down_pk"] = np.ascontiguousarray(
        p["down"].reshape(16, 8, 128, 256).transpose(0, 2, 1, 3))
    p["laT_pk"] = np.ascontiguousarray(
        p["laT"].reshape(32, 128, 128).transpose(1, 0, 2))
    p["wrT_pk"] = np.ascontiguousarray(
        p["wrT"].reshape(32, 128, 8).transpose(1, 0, 2))

    xs, cs, sr, mk, uwts = [], [], [], [], []
    for c in range(NC):
        g0 = c * T - HALO
        xin = np.zeros((XT, D), dtype=np.float32)
        lo = max(0, g0); hi = min(S, g0 + XT)
        xin[lo - g0: hi - g0] = p["x_p"][lo:hi]
        xs.append(xin)
        ct = np.zeros((128, XT), dtype=np.float32)
        st = np.zeros((128, XT), dtype=np.float32)
        ct[:, lo - g0: hi - g0] = cosT[:, lo:hi]
        st[:, lo - g0: hi - g0] = sinrotT[:, lo:hi]
        cs.append(ct); sr.append(st)
        m = np.zeros((128, NKT), dtype=np.float32)
        for kt in range(NKT):
            for r_ in range(128):
                g = g0 + kt * 128 + r_
                if not (0 <= g < S):
                    m[r_, kt] = -1e9
        mk.append(m)
        u = np.zeros((64, T), dtype=np.float32)
        for q in range(T):
            g = c * T + q
            kk = g // 256
            r = g - 256 * kk
            ua, ub = (0.0, 1.0) if kk == 0 else (1.0 - r / 255.0, r / 255.0)
            u[:32, q] = ua
            u[32:, q] = ub
        uwts.append(np.ascontiguousarray(u))
    p["xin"] = xs; p["cosT"] = cs; p["sinrotT"] = sr
    p["maskneg"] = mk; p["uw"] = uwts
    return p



class Kx:
    def __init__(self, nc, es, names):
        self.nc = nc
        self.t = nc.tensor
        self.v = nc.vector
        self.s = nc.scalar
        self.g = nc.gpsimd
        self.y = nc.sync
        self.sem = {}
        self.n = {}
        for nm in names:
            self.sem[nm] = es.enter_context(nc.semaphore(nm))
            self.n[nm] = 0

    def inc(self, instr, nm, k=1):
        instr.then_inc(self.sem[nm], k)
        self.n[nm] += k
        return self.n[nm]

    def wait(self, eng, nm, v=None):
        eng.wait_ge(self.sem[nm], self.n[nm] if v is None else v)

    def dinc(self, instr, nm):
        return self.inc(instr, nm, 16)

    def join(self, eng, names=("pe", "ve", "se", "dT", "dW", "dX", "dS")):
        for nm in names:
            if self.n[nm]:
                eng.wait_ge(self.sem[nm], self.n[nm])


SEMS = ["dW", "dW2", "dV", "dX", "dX2", "dG", "dG2", "dT", "dT2", "dS", "dS2", "pe", "ve", "se", "ge", "cc"]


def load_w_cast(k, dst, src):
    return k.dinc(k.g.dma_start(out=dst, in_=src), "dW")



def emit_prelnT(k, rin_rows, xh, n_tiles, rts, xhts, stats, scol0, epscol=None):
    ld = {}; sq = {}; rstd = {}; ap = {}; tp = {}
    for i in range(n_tiles):
        b = i % 2
        c0 = scol0 + 4 * i
        nmx = "dX" if b == 0 else "dX2"
        nmt = "dT" if b == 0 else "dT2"
        if i >= 2:
            k.wait(k.y, "ve", ap[i - 2])
            k.wait(k.y, nmx, ld[i - 2])
        else:
            k.wait(k.y, nmx)
        ld[i] = k.dinc(k.y.dma_start(out=rts[b][:], in_=rin_rows(i)), nmx)
        k.wait(k.v, nmx, ld[i])
        su = k.inc(k.v.tensor_reduce(stats[:, c0:c0 + 1], rts[b][:],
                                     op=ALU.add, axis=mybir.AxisListType.X),
                   "ve")
        k.wait(k.s, nmx, ld[i])
        if i >= 2:
            k.wait(k.s, nmt, tp[i - 2])
        sq[i] = k.inc(k.s.activation(xhts[b][:], rts[b][:], AF.Square,
                                     accum_out=stats[:, c0 + 3:c0 + 4]), "se")
        mcol = stats[:, c0 + 1:c0 + 2]
        vcol = stats[:, c0 + 2:c0 + 3]
        k.wait(k.v, "se", sq[i])
        k.wait(k.v, "ve", su)
        k.inc(k.v.tensor_scalar(mcol, stats[:, c0:c0 + 1], 1.0 / D, None,
                                op0=ALU.mult), "ve")
        k.wait(k.v, "ve")
        k.inc(k.v.tensor_tensor(vcol, mcol, mcol, op=ALU.mult), "ve")
        k.wait(k.v, "ve")
        vstt = k.inc(k.v.scalar_tensor_tensor(vcol, stats[:, c0 + 3:c0 + 4],
                                              1.0 / D, vcol,
                                              op0=ALU.mult, op1=ALU.subtract),
                     "ve")
        k.wait(k.s, "ve", vstt)
        k.inc(k.s.activation(vcol, vcol, AF.Ln, bias=epscol[:]), "se")
        k.wait(k.s, "se")
        rstd[i] = k.inc(k.s.activation(vcol, vcol, AF.Exp, scale=-0.5), "se")
        k.wait(k.v, "se", rstd[i])
        ap[i] = k.inc(k.v.tensor_scalar(xhts[b][:], rts[b][:], mcol, vcol,
                                        op0=ALU.subtract, op1=ALU.mult), "ve")
        k.wait(k.s, "ve", ap[i])
        tp[i] = k.dinc(k.s.dma_start(out=xh[:, :, i * 128:(i + 1) * 128],
                                     in_=xhts[b][:], transpose=True), nmt)


def emit_postln_residual(k, fview_fn, tdone, r_rows, rout_rows, n_tiles,
                         rts, rus, stats, scol0, epscol=None):
    ld = {}; sq = {}; rstd = {}; addv = {}; st = {}
    for i in range(n_tiles):
        b = i % 2
        c0 = scol0 + 4 * i
        nmx = "dX" if b == 0 else "dX2"
        nms = "dS" if b == 0 else "dS2"
        if i == 1:
            k.wait(k.y, "dT", tdone)
        if i >= 2:
            k.wait(k.y, "ve", addv[i - 2])
            k.wait(k.y, nmx, ld[i - 2])
        else:
            k.wait(k.y, nmx)
        ld[i] = k.dinc(k.y.dma_start(out=rts[b][:], in_=r_rows(i)), nmx)
        fv = fview_fn(i)
        ax = (mybir.AxisListType.XY if len(fv.shape) == 3
              else mybir.AxisListType.X)
        k.wait(k.v, "dT", tdone)
        su = k.inc(k.v.tensor_reduce(stats[:, c0:c0 + 1], fv, op=ALU.add,
                                     axis=ax), "ve")
        k.wait(k.s, "dT", tdone)
        if i >= 2:
            k.wait(k.s, nms, st[i - 2])
        sq[i] = k.inc(k.s.activation(rus[b][:], fv, AF.Square,
                                     accum_out=stats[:, c0 + 3:c0 + 4]), "se")
        mcol = stats[:, c0 + 1:c0 + 2]
        vcol = stats[:, c0 + 2:c0 + 3]
        k.wait(k.v, "se", sq[i])
        k.wait(k.v, "ve", su)
        k.inc(k.v.tensor_scalar(mcol, stats[:, c0:c0 + 1], 1.0 / D, None,
                                op0=ALU.mult), "ve")
        k.wait(k.v, "ve")
        k.inc(k.v.tensor_tensor(vcol, mcol, mcol, op=ALU.mult), "ve")
        k.wait(k.v, "ve")
        vstt = k.inc(k.v.scalar_tensor_tensor(vcol, stats[:, c0 + 3:c0 + 4],
                                              1.0 / D, vcol,
                                              op0=ALU.mult, op1=ALU.subtract),
                     "ve")
        k.wait(k.s, "ve", vstt)
        k.inc(k.s.activation(vcol, vcol, AF.Ln, bias=epscol[:]), "se")
        k.wait(k.s, "se")
        rstd[i] = k.inc(k.s.activation(vcol, vcol, AF.Exp, scale=-0.5), "se")
        k.wait(k.v, "se", rstd[i])
        k.inc(k.v.tensor_scalar(rus[b][:], fv, mcol, vcol,
                                op0=ALU.subtract, op1=ALU.mult), "ve")
        k.wait(k.v, nmx, ld[i])
        k.wait(k.v, "ve")
        addv[i] = k.inc(k.v.tensor_tensor(rus[b][:], rus[b][:], rts[b][:],
                                          op=ALU.add), "ve")
        k.wait(k.y, "ve", addv[i])
        if i >= 2:
            k.wait(k.y, nms, st[i - 2])
        else:
            k.wait(k.y, nms)
        st[i] = k.dinc(k.y.dma_start(out=rout_rows(i), in_=rus[b][:]), nms)



def _emit_l0attn(k, es, sb, nc, io, dbg=False, dbg_stop=None):
    kr = sb("kr", [128, XT], BF16)
    vt = sb("vt", [128, NKT, 128], BF16)
    cos_sb = sb("cos_sb", [128, XT], BF16)
    sinr_sb = sb("sinr_sb", [128, XT], BF16)
    p_sb = sb("p_sb", [128, 128], BF16)
    id_bf = sb("id_bf", [128, 128], BF16)
    id_f32 = sb("id_f32", [128, 128], F32)
    mask_sb = sb("mask_sb", [128, NKT], F32)
    stats = sb("stats", [128, 96], F32)
    ones_sb = sb("ones_sb", [128, 1], BF16)
    epscol = sb("epscol", [128, 1], F32)

    load_w_cast(k, cos_sb[:], io["cosT"])
    load_w_cast(k, sinr_sb[:], io["sinrotT"])
    load_w_cast(k, p_sb[:], io["prot"])
    load_w_cast(k, id_bf[:], io["ident"])
    k.dinc(k.y.dma_start(out=id_f32[:], in_=io["ident"]), "dX")
    k.dinc(k.y.dma_start(out=mask_sb[:], in_=io["maskneg"]), "dX")
    k.g.memset(ones_sb[:], 1.0)
    k.g.memset(epscol[:], EPS_LN)
    ge_consts = k.inc(k.g.engine_nop(), "ge")
    dw_consts = k.n["dW"]

    dummyT = sb("dummyT", [128, 16], BF16)
    k.wait(k.s, "dW")
    k.dinc(k.s.dma_start(out=dummyT[:], in_=cos_sb[0:16, 0:128], transpose=True), "dT")
    k.wait(k.s, "ge")
    k.wait(k.v, "ge")
    k.wait(k.t, "ge")

    BIGA = sb("BIGA", [128, 32 * XT], BF16)
    BIGB = sb("BIGB", [128, 32 * T], BF16)
    q_sb = BIGB.rearrange("p (a b) -> p a b", a=32)

    if True:
        xh = BIGA.rearrange("p (a b) -> p a b", a=32)
        rt_a = BIGB[:, 0:8192].bitcast(F32)
        rt_b = BIGB[:, 8192:16384].bitcast(F32)
        xht_a = BIGB[:, 16384:20480]
        xht_b = BIGB[:, 20480:24576]
        emit_prelnT(k, lambda tt: io["xin"][tt * 128:(tt + 1) * 128, :],
                    xh, NKT, (rt_a, rt_b), (xht_a, xht_b), stats, 0, epscol)

        with nc.sbuf_tensor("wbuf", [128, 2, 32, 128], BF16) as wbuf, \
             nc.sbuf_tensor("qraw", [128, 2, 512], BF16) as qraw, \
             nc.sbuf_tensor("qs2", [128, 2, 512], BF16) as qs2, \
             nc.sbuf_tensor("t1b", [128, 2, 512], BF16) as t1b, \
             nc.psum_tensor("psp", [128, 2, 512], F32) as psp, \
             nc.psum_tensor("psr", [128, 2, 512], F32) as psr, \
             nc.psum_tensor("psT", [128, 128], BF16) as psT:

            xh_done = k.n["dT"]
            xh_done2 = k.n["dT2"]
            units = [("q", j) for j in range(32)] + [("k", 0), ("v", 0)]
            wload = []

            def load_unit(uj):
                kind, j = units[uj]
                src = (io["wqT"][j] if kind == "q"
                       else (io["wkv"][0] if kind == "k" else io["wkv"][1]))
                nm = "dW" if uj % 2 == 0 else "dW2"
                wload.append((nm, k.dinc(k.g.dma_start(
                    out=wbuf[:, uj % 2], in_=src), nm)))

            k.join(k.g, ("ve", "dT", "dT2"))
            load_unit(0)
            load_unit(1)
            k.wait(k.t, "dT", xh_done)
            k.wait(k.t, "dT2", xh_done2)
            k.wait(k.t, "ge", ge_consts)
            k.wait(k.t, "dW", dw_consts)

            evac_ve = {}
            fin_ve = {}
            proj_pe = {}
            hfc = 0
            for uj, (kind, j) in enumerate(units):
                k.wait(k.t, wload[uj][0], wload[uj][1])
                nhalf = 2 if kind == "q" else 3
                for hf in range(nhalf):
                    off = (HALO if kind == "q" else 0) + hf * 512
                    pp = psp[:, hfc % 2, :]
                    if hfc >= 2:
                        k.wait(k.t, "ve", evac_ve[hfc - 2])
                    for ci in range(32):
                        mm = k.t.matmul(pp, wbuf[:, uj % 2, ci, :],
                                        xh[:, ci, off:off + 512],
                                        start=(ci == 0), stop=(ci == 31))
                    pj = k.inc(mm, "pe")
                    proj_pe[uj] = pj
                    k.wait(k.v, "pe", pj)
                    evac_ve[hfc] = k.inc(k.v.tensor_copy(qraw[:, hfc % 2, :], pp), "ve")
                    k.wait(k.v, "ve")
                    if kind == "v":
                        for c4 in range(4):
                            kt = 4 * hf + c4
                            k.wait(k.t, "ve")
                            tp = k.inc(k.t.transpose(
                                psT[:], qraw[:, hfc % 2, c4 * 128:(c4 + 1) * 128],
                                id_bf[:]), "pe")
                            k.wait(k.v, "pe", tp)
                            k.inc(k.v.tensor_copy(vt[:, kt, :], psT[:]), "ve")
                        fin_ve[hfc] = k.n["ve"]
                        hfc += 1
                        continue
                    k.inc(k.v.tensor_tensor(qs2[:, hfc % 2, :], qraw[:, hfc % 2, :],
                                            sinr_sb[:, off:off + 512], op=ALU.mult),
                          "ve")
                    k.wait(k.t, "ve")
                    rmm = k.t.matmul(psr[:, hfc % 2, :], p_sb[:], qs2[:, hfc % 2, :],
                                     start=True, stop=True)
                    rot_pe = k.inc(rmm, "pe")
                    k.inc(k.v.tensor_tensor(t1b[:, hfc % 2, :], qraw[:, hfc % 2, :],
                                            cos_sb[:, off:off + 512], op=ALU.mult),
                          "ve")
                    k.wait(k.v, "pe", rot_pe)
                    k.wait(k.v, "ve")
                    dst = (q_sb[:, j, hf * 512:(hf + 1) * 512] if kind == "q"
                           else kr[:, off:off + 512])
                    fin_ve[hfc] = k.inc(
                        k.v.tensor_tensor(dst, t1b[:, hfc % 2, :], psr[:, hfc % 2, :],
                                          op=ALU.add), "ve")
                    hfc += 1
                if uj + 2 < len(units):
                    k.wait(k.g, "pe", proj_pe[uj])
                    load_unit(uj + 2)

    if dbg:
        k.wait(k.y, "dT")
        k.dinc(k.y.dma_start(out=io["dbg_xh"], in_=BIGA[:]), "dS")
        k.dinc(k.y.dma_start(out=io["dbg_stats"], in_=stats[:]), "dS")
        k.wait(k.y, "ve")
        k.wait(k.y, "dS")
        k.dinc(k.y.dma_start(out=io["dbg_q"], in_=BIGB[:]), "dS")
        k.dinc(k.y.dma_start(out=io["dbg_kr"], in_=kr[:]), "dS")
        k.dinc(k.y.dma_start(out=io["dbg_vt"], in_=vt.rearrange("p a b -> p (a b)")), "dS")
        k.wait(k.y, "dS")
    if dbg_stop == "xh":
        return

    o_ch = BIGA[:, 0:32 * T].rearrange("p (a b) -> p a b", a=32)
    avA = BIGA[:, 32 * T + 0:32 * T + 4096].rearrange("p (a b) -> p a b", a=32)
    avB = BIGA[:, 32 * T + 4096:32 * T + 8192].rearrange("p (a b) -> p a b", a=32)
    avTA = BIGA[:, 32 * T + 8192:32 * T + 12288].rearrange("p (a b) -> p a b", a=32)
    avTB = BIGA[:, 32 * T + 12288:32 * T + 16384].rearrange("p (a b) -> p a b", a=32)
    with nc.sbuf_tensor("uw_sb", [64, T], F32) as uw_sb, \
         nc.sbuf_tensor("sig", [128, 2, 1024], BF16) as sig, \
         nc.sbuf_tensor("oT", [128, 32, 128], BF16) as oT, \
         nc.sbuf_tensor("rsrow", [1, 8192], BF16) as rsrow, \
         nc.sbuf_tensor("rsT", [64, 128], F32) as rsT, \
         nc.sbuf_tensor("uTc", [128, 64], F32) as uTc, \
         nc.sbuf_tensor("tmp2", [128, 128], BF16) as tmp2, \
         nc.psum_tensor("sc", [128, 2, 1024], F32) as sc, \
         nc.psum_tensor("av", [128, 2, 512], F32) as av, \
         nc.psum_tensor("rsp", [1, 1024], F32) as rsp:

        k.join(k.y, ("ve", "se", "pe"))
        k.join(k.t, ("ve", "se"))
        k.dinc(k.y.dma_start(out=uw_sb[:], in_=io["uw"]), "dX")
        for qt in range(8):
            ka = qt & ~1
            for w in range(2):
                av_sb = avA if w == 0 else avB
                for hg in range(4):
                    for kt4 in range(4):
                        ktile = ka + 2 * w + kt4
                        k.wait(k.t, "se")
                        for half in range(2):
                            mm = k.t.matmul(
                                sc[:, kt4 % 2, half * 512:(half + 1) * 512],
                                kr[:, ktile * 128:(ktile + 1) * 128],
                                q_sb[:, 8 * hg + 4 * half: 8 * hg + 4 * half + 4,
                                     qt * 128:(qt + 1) * 128],
                                start=True, stop=True)
                        sc_pe = k.inc(mm, "pe")
                        k.wait(k.s, "pe")
                        sg = k.inc(k.s.activation(
                            sig[:, kt4 % 2, :], sc[:, kt4 % 2, :], AF.Sigmoid,
                            bias=mask_sb[:, ktile:ktile + 1], scale=SCALE0), "se")
                        k.wait(k.t, "se", sg)
                        for half in range(2):
                            k.t.matmul(rsp[:, half * 512:(half + 1) * 512],
                                       ones_sb[:],
                                       sig[:, kt4 % 2, half * 512:(half + 1) * 512],
                                       start=(kt4 == 0), stop=(kt4 == 3))
                        for half in range(2):
                            mm = k.t.matmul(
                                av[:, half, :], vt[:, ktile, :],
                                sig[:, kt4 % 2, half * 512:(half + 1) * 512],
                                start=(kt4 == 0), stop=(kt4 == 3))
                        k.inc(mm, "pe")
                    k.wait(k.s, "pe")
                    k.inc(k.s.activation(
                        rsrow[:, w * 4096 + hg * 1024: w * 4096 + (hg + 1) * 1024],
                        rsp[:], AF.Identity), "se")
                    for half in range(2):
                        k.inc(k.s.activation(
                            av_sb[:, 8 * hg + 4 * half: 8 * hg + 4 * half + 4, :],
                            av[:, half, :], AF.Identity), "se")
                    k.wait(k.t, "se")

            k.wait(k.y, "se")
            k.dinc(k.y.dma_start(out=io["rs_scr"].rearrange("w g h q -> (w g h q)"),
                                in_=rsrow[0:1, :]), "dS")
            k.wait(k.g, "dS")
            k.dinc(k.g.dma_start(out=rsT[:],
                                in_=io["rs_scr"].rearrange("w g h q -> (w g h) q")),
                  "dG")
            k.wait(k.v, "dG")
            k.inc(k.v.tensor_scalar(rsT[:], rsT[:], 1e-30, None, op0=ALU.add), "ve")
            k.wait(k.v, "ve")
            k.inc(k.v.reciprocal(rsT[:], rsT[:]), "ve")
            k.wait(k.v, "ve")
            k.wait(k.v, "dX")
            umul = k.inc(k.v.tensor_tensor(rsT[:], rsT[:],
                                           uw_sb[:, qt * 128:(qt + 1) * 128],
                                           op=ALU.mult), "ve")
            k.wait(k.t, "ve", umul)
            k.wait(k.t, "se")
            tp = k.inc(k.t.transpose(sc[:, 0, 0:64], rsT[:], id_f32[0:64, 0:64]), "pe")
            k.wait(k.v, "pe", tp)
            k.inc(k.v.tensor_copy(uTc[:], sc[:, 0, 0:64]), "ve")
            k.wait(k.t, "ve")

            k.wait(k.s, "se")
            k.dinc(k.s.dma_start(out=avTA, in_=avA.rearrange("p a b -> p (a b)"),
                                transpose=True), "dT")
            dtb = k.dinc(k.s.dma_start(out=avTB,
                                      in_=avB.rearrange("p a b -> p (a b)"),
                                      transpose=True), "dT")
            k.wait(k.v, "dT", dtb)
            k.wait(k.v, "ve")
            for h in range(32):
                k.wait(k.v, "ve")
                k.inc(k.v.tensor_scalar(tmp2[:], avTA[:, h, :], uTc[:, h:h + 1], None,
                                  op0=ALU.mult), "ve")
                k.wait(k.v, "ve")
                k.inc(k.v.scalar_tensor_tensor(
                    oT[:, h, :], avTB[:, h, :], uTc[:, 32 + h:33 + h],
                    tmp2[:], op0=ALU.mult, op1=ALU.add), "ve")
            k.wait(k.s, "ve")
            k.dinc(k.s.dma_start(out=o_ch[:, :, qt * 128:(qt + 1) * 128],
                                in_=oT[:].rearrange("p a b -> p (a b)"),
                                transpose=True), "dT")
            k.wait(k.t, "dT")
            k.wait(k.v, "dT")

    if dbg:
        k.wait(k.y, "dT")
        k.wait(k.y, "ve")
        k.dinc(k.y.dma_start(out=io["dbg_och"], in_=BIGA[:, 0:32 * T]), "dS")
        k.wait(k.y, "dS")

    f_ch = BIGB.rearrange("p (a b) -> p a b", a=32)
    with nc.sbuf_tensor("wbuf3", [128, 2, 32, 128], BF16) as wbuf, \
         nc.psum_tensor("psp3", [128, 2, 512], F32) as psp:
        wl = []

        def load_w3(j):
            nm = "dW" if j % 2 == 0 else "dW2"
            wl.append((nm, k.dinc(k.g.dma_start(
                out=wbuf[:, j % 2], in_=io["woT"][j]), nm)))

        k.join(k.g, ("ve", "se", "pe", "dT", "dS"))
        k.join(k.t, ("ve", "se"))
        load_w3(0)
        load_w3(1)
        k.wait(k.t, "dT")
        evac_se = {}
        for j in range(32):
            k.wait(k.t, wl[j][0], wl[j][1])
            for hf in range(2):
                u = 2 * j + hf
                pp = psp[:, u % 2, :]
                if u >= 2:
                    k.wait(k.t, "se", evac_se[u - 2])
                for ci in range(32):
                    mm = k.t.matmul(pp, wbuf[:, j % 2, ci, :],
                                    o_ch[:, ci, hf * 512:(hf + 1) * 512],
                                    start=(ci == 0), stop=(ci == 31))
                pej = k.inc(mm, "pe")
                k.wait(k.s, "pe", pej)
                evac_se[u] = k.inc(
                    k.s.activation(f_ch[:, j, hf * 512:(hf + 1) * 512], pp,
                                   AF.Identity), "se")
            if j + 2 < 32:
                k.wait(k.g, "pe")
                load_w3(j + 2)

    if dbg:
        k.wait(k.y, "se")
        k.dinc(k.y.dma_start(out=io["dbg_fch"], in_=BIGB[:]), "dS")
        k.wait(k.y, "dS")

    fT = BIGA[:, 0:32 * T].rearrange("p (a b) -> p a b", a=256)
    rt4_a = BIGA[:, 32768:40960].bitcast(F32)
    rt4_b = BIGA[:, 40960:49152].bitcast(F32)
    ru4_a = BIGB[:, 0:8192].bitcast(F32)
    ru4_b = BIGB[:, 8192:16384].bitcast(F32)
    k.join(k.y, ("pe", "se", "ve", "dT", "dT2"))
    k.wait(k.s, "se")
    k.wait(k.s, "ve")
    tdone = k.dinc(k.s.dma_start(out=fT,
                                 in_=f_ch.rearrange("p a b -> p (a b)"),
                                 transpose=True), "dT")
    emit_postln_residual(
        k, lambda tt: fT[:, tt:256:8, :], tdone,
        lambda tt: io["xin"][HALO + tt * 128: HALO + (tt + 1) * 128, :],
        lambda tt: io["r_out"][tt * 128:(tt + 1) * 128, :],
        8, (rt4_a, rt4_b), (ru4_a, ru4_b), stats, 48, epscol)

    k.wait(k.y, "dS")
    k.wait(k.y, "dS2")


def _build_l0attn(dbg=False, dbg_stop=None):
    nc = bass.Bass("TRN2", target_bir_lowering=False)
    io = {}
    io["xin"] = nc.dram_tensor("xin", [XT, D], F32, kind="ExternalInput").ap()
    io["wqT"] = nc.dram_tensor("wqT", [32, 128, 32, 128], F32, kind="ExternalInput").ap()
    io["wkv"] = nc.dram_tensor("wkv", [2, 128, 32, 128], F32, kind="ExternalInput").ap()
    io["woT"] = nc.dram_tensor("woT", [32, 128, 32, 128], F32, kind="ExternalInput").ap()
    io["cosT"] = nc.dram_tensor("cosT", [128, XT], F32, kind="ExternalInput").ap()
    io["sinrotT"] = nc.dram_tensor("sinrotT", [128, XT], F32, kind="ExternalInput").ap()
    io["prot"] = nc.dram_tensor("prot", [128, 128], F32, kind="ExternalInput").ap()
    io["ident"] = nc.dram_tensor("ident", [128, 128], F32, kind="ExternalInput").ap()
    io["maskneg"] = nc.dram_tensor("maskneg", [128, NKT], F32, kind="ExternalInput").ap()
    io["uw"] = nc.dram_tensor("uw", [64, T], F32, kind="ExternalInput").ap()
    io["r_out"] = nc.dram_tensor("r_out", [T, D], F32, kind="ExternalOutput").ap()
    io["rs_scr"] = nc.dram_tensor("rs_scr", [2, 8, 4, 128], BF16).ap()
    if dbg:
        io["dbg_xh"] = nc.dram_tensor("dbg_xh", [128, 32 * XT], BF16, kind="ExternalOutput").ap()
        io["dbg_stats"] = nc.dram_tensor("dbg_stats", [128, 96], F32, kind="ExternalOutput").ap()
        io["dbg_q"] = nc.dram_tensor("dbg_q", [128, 32 * T], BF16, kind="ExternalOutput").ap()
        io["dbg_kr"] = nc.dram_tensor("dbg_kr", [128, XT], BF16, kind="ExternalOutput").ap()
        io["dbg_vt"] = nc.dram_tensor("dbg_vt", [128, NKT * 128], BF16, kind="ExternalOutput").ap()
        io["dbg_och"] = nc.dram_tensor("dbg_och", [128, 32 * T], BF16, kind="ExternalOutput").ap()
        io["dbg_fch"] = nc.dram_tensor("dbg_fch", [128, 32 * T], BF16, kind="ExternalOutput").ap()

    with ExitStack() as es:
        def sb(name, shape, dt):
            return es.enter_context(nc.sbuf_tensor(name, shape, dt))
        k = Kx(nc, es, SEMS)
        block = es.enter_context(nc.Block())

        @block.sync
        def _(sy):
            _emit_l0attn(k, es, sb, nc, io, dbg=dbg, dbg_stop=dbg_stop)
    return nc


def _maps_l0attn(p, R):
    maps = []
    for c in range(NC):
        g0 = c * T - HALO
        xin = np.zeros((XT, D), dtype=np.float32)
        lo = max(0, g0); hi = min(S, g0 + XT)
        xin[lo - g0: hi - g0] = R[lo:hi]
        maps.append(dict(
            xin=xin, wqT=p["wqT_pk"], wkv=p["wkv_pk"], woT=p["woT_pk"],
            cosT=p["cosT"][c], sinrotT=p["sinrotT"][c], prot=p["P_rot"],
            ident=p["ident"], maskneg=p["maskneg"][c], uw=p["uw"][c]))
    return maps


def _run_l0attn(p, R):
    nc = _build_l0attn()
    res = run_bass_kernel_spmd(nc, _maps_l0attn(p, R), core_ids=list(range(NC)))
    return np.concatenate([res.results[c]["r_out"] for c in range(NC)], axis=0)





def _emit_l0ffn(k, es, sb, nc, io):
    stats = sb("stats", [128, 96], F32)
    epscol = sb("epscol", [128, 1], F32)
    ARA = sb("ARA", [128, 32 * T], BF16)
    ARB = sb("ARB", [128, 32 * T], BF16)
    dmp = sb("dmp", [128, 2 * D], BF16)
    xh = ARA.rearrange("p (a b) -> p a b", a=32)
    f_ch = ARB.rearrange("p (a b) -> p a b", a=32)

    k.g.memset(epscol[:], EPS_LN)
    k.inc(k.g.engine_nop(), "ge")
    k.wait(k.s, "ge")
    k.wait(k.v, "ge")
    dummyT = sb("dummyT", [128, 16], BF16)
    dummyS = sb("dummyS", [16, 128], BF16)
    k.g.memset(dummyS[:], 0.0)
    ge_dummy = k.inc(k.g.engine_nop(), "ge")
    k.wait(k.s, "ge", ge_dummy)
    k.dinc(k.s.dma_start(out=dummyT[:], in_=dummyS[:], transpose=True), "dT")

    rt_a = dmp[:].bitcast(F32)
    rt_b = ARB[:, 0:8192].bitcast(F32)
    xht_a = ARB[:, 8192:12288]
    xht_b = ARB[:, 12288:16384]
    emit_prelnT(k, lambda tt: io["r_in"][tt * 128:(tt + 1) * 128, :],
                xh, 8, (rt_a, rt_b), (xht_a, xht_b), stats, 0, epscol)

    k.join(k.g, ("ve", "dT", "dT2", "se"))
    with nc.sbuf_tensor("wbufU", [128, 2, 2, 1024], BF16) as wbufU, \
         nc.sbuf_tensor("wbufD", [128, 2, 8, 256], BF16) as wbufD, \
         nc.sbuf_tensor("hid", [128, 8, 1024], BF16) as hid, \
         nc.psum_tensor("psu", [128, 2, 512], F32) as psu, \
         nc.psum_tensor("psd", [128, 2, 512], F32) as psd:

        wl = []

        def load_head(h):
            nm = "dW" if h % 2 == 0 else "dW2"
            k.dinc(k.g.dma_start(out=wbufU[:, h % 2], in_=io["up"][h]), nm)
            wl.append((nm, k.dinc(k.g.dma_start(
                out=wbufD[:, h % 2], in_=io["down"][h]), nm)))

        load_head(0)
        load_head(1)
        k.wait(k.t, "dT")
        k.wait(k.t, "ge", ge_dummy)
        up_pe = {}
        for h in range(16):
            k.wait(k.t, wl[h][0], wl[h][1])
            for mt in range(8):
                for half in range(2):
                    u = 2 * mt + half
                    if h > 0 or u >= 2:
                        k.wait(k.t, "se")
                    for d2 in range(2):
                        mm = k.t.matmul(psu[:, u % 2, :],
                                        wbufU[:, h % 2, d2, mt * 128:(mt + 1) * 128],
                                        xh[:, 2 * h + d2, half * 512:(half + 1) * 512],
                                        start=(d2 == 0), stop=(d2 == 1))
                    pj = k.inc(mm, "pe")
                    k.wait(k.s, "pe", pj)
                    k.inc(k.s.activation(hid[:, mt, half * 512:(half + 1) * 512],
                                         psu[:, u % 2, :], AF.Gelu), "se")
            for dt_ in range(2):
                for half in range(2):
                    u = 2 * dt_ + half
                    if h > 0 or u >= 2:
                        k.wait(k.t, "se")
                    k.wait(k.t, "se")
                    for mt in range(8):
                        mm = k.t.matmul(psd[:, u % 2, :],
                                        wbufD[:, h % 2, mt, dt_ * 128:(dt_ + 1) * 128],
                                        hid[:, mt, half * 512:(half + 1) * 512],
                                        start=(mt == 0), stop=(mt == 7))
                    pj = k.inc(mm, "pe")
                    up_pe[h] = pj
                    k.wait(k.s, "pe", pj)
                    k.inc(k.s.activation(
                        f_ch[:, 2 * h + dt_, half * 512:(half + 1) * 512],
                        psd[:, u % 2, :], AF.Identity), "se")
            if h + 2 < 16:
                k.wait(k.g, "pe", up_pe[h])
                load_head(h + 2)

    fT = ARA.rearrange("p (a b) -> p a b", a=256)
    rt4_a = dmp[:].bitcast(F32)
    rt4_b = ARB[:, 0:8192].bitcast(F32)
    ru4_a = ARB[:, 8192:16384].bitcast(F32)
    ru4_b = ARB[:, 16384:24576].bitcast(F32)
    k.join(k.y, ("pe", "se", "ve", "dT", "dT2"))
    k.wait(k.s, "se")
    k.wait(k.s, "ve")
    tdone = k.dinc(k.s.dma_start(out=fT,
                                 in_=f_ch.rearrange("p a b -> p (a b)"),
                                 transpose=True), "dT")
    emit_postln_residual(
        k, lambda tt: fT[:, tt:256:8, :], tdone,
        lambda tt: io["r_in"][tt * 128:(tt + 1) * 128, :],
        lambda tt: io["r_out"][tt * 128:(tt + 1) * 128, :],
        8, (rt4_a, rt4_b), (ru4_a, ru4_b), stats, 48, epscol)
    k.wait(k.y, "dS")
    k.wait(k.y, "dS2")


def _build_l0ffn():
    nc = bass.Bass("TRN2", target_bir_lowering=False)
    io = {}
    io["r_in"] = nc.dram_tensor("r_in", [T, D], F32, kind="ExternalInput").ap()
    io["up"] = nc.dram_tensor("up", [16, 128, 2, 1024], F32, kind="ExternalInput").ap()
    io["down"] = nc.dram_tensor("down", [16, 128, 8, 256], F32, kind="ExternalInput").ap()
    io["r_out"] = nc.dram_tensor("r_out", [T, D], F32, kind="ExternalOutput").ap()
    with ExitStack() as es:
        def sb(name, shape, dt):
            return es.enter_context(nc.sbuf_tensor(name, shape, dt))
        k = Kx(nc, es, SEMS)
        block = es.enter_context(nc.Block())

        @block.sync
        def _(sy):
            _emit_l0ffn(k, es, sb, nc, io)
    return nc


def _run_l0ffn(p, R):
    nc = _build_l0ffn()
    maps = [dict(r_in=np.ascontiguousarray(R[c * T:(c + 1) * T]),
                 up=p["up_pk"], down=p["down_pk"]) for c in range(NC)]
    res = run_bass_kernel_spmd(nc, maps, core_ids=list(range(NC)))
    return np.concatenate([res.results[c]["r_out"] for c in range(NC)], axis=0)





def _emit_l1ffn(k, es, sb, nc, io):
    stats = sb("stats", [128, 96], F32)
    epscol = sb("epscol", [128, 1], F32)
    ARA = sb("ARA", [128, 32 * T], BF16)
    ARB = sb("ARB", [128, 32 * T], BF16)
    dmp = sb("dmp", [128, 2 * D], BF16)
    xh = ARA.rearrange("p (a b) -> p a b", a=32)
    f_ch = ARB.rearrange("p (a b) -> p a b", a=32)

    k.g.memset(epscol[:], EPS_LN)
    k.inc(k.g.engine_nop(), "ge")
    k.wait(k.s, "ge")
    k.wait(k.v, "ge")
    dummyT = sb("dummyT", [128, 16], BF16)
    dummyS = sb("dummyS", [16, 128], BF16)
    k.g.memset(dummyS[:], 0.0)
    ge_dummy = k.inc(k.g.engine_nop(), "ge")
    k.wait(k.s, "ge", ge_dummy)
    k.dinc(k.s.dma_start(out=dummyT[:], in_=dummyS[:], transpose=True), "dT")

    rt_a = dmp[:].bitcast(F32)
    rt_b = ARB[:, 0:8192].bitcast(F32)
    xht_a = ARB[:, 8192:12288]
    xht_b = ARB[:, 12288:16384]
    emit_prelnT(k, lambda tt: io["r_in"][tt * 128:(tt + 1) * 128, :],
                xh, 8, (rt_a, rt_b), (xht_a, xht_b), stats, 0, epscol)

    k.join(k.g, ("ve", "dT", "dT2", "se"))
    with nc.sbuf_tensor("wbuf", [128, 2, 32, 128], BF16) as wbuf, \
         nc.psum_tensor("ps2", [128, 2, 512], F32) as ps2:
        es2 = ExitStack()
        lg = es2.enter_context(nc.sbuf_tensor("lg", [8, T], F32))[:]
        wx = es2.enter_context(nc.sbuf_tensor("wx", [8, T], F32))[:]
        lgt_sb = es2.enter_context(nc.sbuf_tensor("lgt_sb", [128, 8, 26], F32))
        id8 = es2.enter_context(nc.sbuf_tensor("id8", [8, 8], F32))
        id_f32 = es2.enter_context(nc.sbuf_tensor("idf", [128, 128], F32))
        k.dinc(k.y.dma_start(out=id8[:], in_=io["ident"][0:8, 0:8]), "dS")
        k.dinc(k.y.dma_start(out=id_f32[:], in_=io["ident"]), "dS")
        wexp = es2.enter_context(nc.sbuf_tensor("wexp", [128, T], F32))
        g_sb = es2.enter_context(nc.sbuf_tensor("g_sb", [128, T], BF16))
        lbT = es2.enter_context(nc.sbuf_tensor("lbT_sb", [128, 32, 128], BF16))
        bias_sb = es2.enter_context(nc.sbuf_tensor("biasv", [128, 32], F32))
        k.dinc(k.g.dma_start(out=lbT[:],
                             in_=io["lbT"].rearrange("p (a b) -> p a b", a=32)), "dV")
        k.dinc(k.y.dma_start(out=bias_sb[:], in_=io["bias_sb"]), "dS")
        k.wait(k.t, "dS")
        with nc.sbuf_tensor("wr_sb", [128, 32, 8], BF16) as wr_sb:
            k.dinc(k.g.dma_start(out=wr_sb[:], in_=io["wrT"]), "dV")
            k.wait(k.t, "dV")
            k.wait(k.t, "dT")
            k.wait(k.t, "ge", ge_dummy)
            for half in range(2):
                for ci in range(32):
                    mm = k.t.matmul(ps2[0:8, half, :], wr_sb[:, ci, :],
                                    xh[:, ci, half * 512:(half + 1) * 512],
                                    start=(ci == 0), stop=(ci == 31))
                k.inc(mm, "pe")
                k.wait(k.v, "pe")
                k.inc(k.v.tensor_copy(lg[:, half * 512:(half + 1) * 512],
                                      ps2[0:8, half, :]), "ve")
            for c8 in range(8):
                k.wait(k.t, "ve")
                tp = k.inc(k.t.transpose(ps2[:, 0, 0:8],
                                         lg[:, c8 * 128:(c8 + 1) * 128],
                                         id8[0:8, 0:8]), "pe")
                k.wait(k.v, "pe", tp)
                k.inc(k.v.tensor_copy(lgt_sb[:, c8, 0:8], ps2[:, 0, 0:8]), "ve")
                sl = lgt_sb[:, c8, 0:8]
                m1c = lgt_sb[:, c8, 8:9]
                m2c = lgt_sb[:, c8, 9:10]
                k.wait(k.v, "ve")
                k.inc(k.v.tensor_reduce(m1c, sl, op=ALU.max,
                                        axis=mybir.AxisListType.X), "ve")
                k.wait(k.v, "ve")
                k.inc(k.v.tensor_scalar(lgt_sb[:, c8, 10:18], sl, m1c, None,
                                        op0=ALU.is_ge), "ve")
                k.wait(k.v, "ve")
                k.inc(k.v.scalar_tensor_tensor(lgt_sb[:, c8, 10:18],
                                               lgt_sb[:, c8, 10:18], -1e30, sl,
                                               op0=ALU.mult, op1=ALU.add), "ve")
                k.wait(k.v, "ve")
                k.inc(k.v.tensor_reduce(m2c, lgt_sb[:, c8, 10:18], op=ALU.max,
                                        axis=mybir.AxisListType.X), "ve")
                k.wait(k.v, "ve")
                k.inc(k.v.tensor_scalar(m1c, m1c, -1.0, None, op0=ALU.mult), "ve")
                k.wait(k.s, "ve")
                k.inc(k.s.activation(lgt_sb[:, c8, 10:18], sl, AF.Exp,
                                     bias=m1c), "se")
                k.wait(k.v, "se")
                k.inc(k.v.tensor_scalar(lgt_sb[:, c8, 18:26], sl, m2c, None,
                                        op0=ALU.is_ge), "ve")
                k.wait(k.v, "ve")
                k.inc(k.v.tensor_tensor(lgt_sb[:, c8, 10:18],
                                        lgt_sb[:, c8, 10:18],
                                        lgt_sb[:, c8, 18:26], op=ALU.mult), "ve")
                k.wait(k.v, "ve")
                k.inc(k.v.tensor_reduce(m2c, lgt_sb[:, c8, 10:18], op=ALU.add,
                                        axis=mybir.AxisListType.X), "ve")
                k.wait(k.v, "ve")
                k.inc(k.v.reciprocal(m2c, m2c), "ve")
                k.wait(k.v, "ve")
                k.inc(k.v.tensor_scalar(lgt_sb[:, c8, 10:18],
                                        lgt_sb[:, c8, 10:18], m2c, 2.0,
                                        op0=ALU.mult, op1=ALU.mult), "ve")
                k.wait(k.t, "ve")
                tp = k.inc(k.t.transpose(ps2[0:8, 1, 0:128],
                                         lgt_sb[:, c8, 10:18],
                                         id_f32[:]), "pe")
                k.wait(k.v, "pe", tp)
                k.inc(k.v.tensor_copy(wx[:, c8 * 128:(c8 + 1) * 128],
                                      ps2[0:8, 1, 0:128]), "ve")
                k.wait(k.t, "ve")
            k.g.memset(wexp[:], 0.0)
            k.inc(k.g.engine_nop(), "ge")
            k.wait(k.y, "ge")
            k.wait(k.y, "ve")
            k.wait(k.y, "dT")
            for qd in range(4):
                k.dinc(k.y.dma_start(out=wexp[32 * qd:32 * qd + 2, :],
                                     in_=wx[2 * qd:2 * qd + 2, :]), "dX")
            k.wait(k.v, "dX")
            mask = [i // 16 for i in range(32)]
            k.inc(k.v.stream_shuffle(wexp[:], wexp[:], mask), "ve")

            k.wait(k.g, "pe")
            k.dinc(k.g.dma_start(out=wbuf[:, 0, :, :], in_=io["laT"]), "dV")
            k.wait(k.t, "dV")
            for half in range(2):
                for ci in range(32):
                    mm = k.t.matmul(ps2[:, half, :], wbuf[:, 0, ci, :],
                                    xh[:, ci, half * 512:(half + 1) * 512],
                                    start=(ci == 0), stop=(ci == 31))
                k.inc(mm, "pe")
                k.wait(k.v, "pe")
                k.wait(k.v, "ve")
                k.inc(k.v.tensor_tensor(g_sb[:, half * 512:(half + 1) * 512],
                                        ps2[:, half, :],
                                        wexp[:, half * 512:(half + 1) * 512],
                                        op=ALU.mult), "ve")

        wl = []

        def load_wb(j):
            nm = "dW" if j % 2 == 0 else "dW2"
            wl.append((nm, k.dinc(k.g.dma_start(
                out=wbuf[:, j % 2], in_=io["wbT"][j]), nm)))

        k.wait(k.g, "ve")
        load_wb(0)
        load_wb(1)
        k.wait(k.s, "dS")
        evac_se = {}
        for j in range(32):
            k.wait(k.t, wl[j][0], wl[j][1])
            for hf in range(2):
                u = 2 * j + hf
                if u >= 2:
                    k.wait(k.t, "se", evac_se[u - 2])
                for ci in range(32):
                    k.t.matmul(ps2[:, u % 2, :], wbuf[:, j % 2, ci, :],
                               xh[:, ci, hf * 512:(hf + 1) * 512],
                               start=(ci == 0), stop=False)
                mm = k.t.matmul(ps2[:, u % 2, :], lbT[:, j, :],
                                g_sb[:, hf * 512:(hf + 1) * 512],
                                start=False, stop=True)
                pej = k.inc(mm, "pe")
                k.wait(k.s, "pe", pej)
                evac_se[u] = k.inc(
                    k.s.activation(f_ch[:, j, hf * 512:(hf + 1) * 512],
                                   ps2[:, u % 2, :], AF.Identity,
                                   bias=bias_sb[:, j:j + 1]), "se")
            if j + 2 < 32:
                k.wait(k.g, "pe", pej)
                load_wb(j + 2)
        es2.close()

    fT = ARA.rearrange("p (a b) -> p a b", a=256)
    rt4_a = dmp[:].bitcast(F32)
    rt4_b = ARB[:, 0:8192].bitcast(F32)
    ru4_a = ARB[:, 8192:16384].bitcast(F32)
    ru4_b = ARB[:, 16384:24576].bitcast(F32)
    k.join(k.y, ("pe", "se", "ve", "dT", "dT2"))
    k.wait(k.s, "se")
    k.wait(k.s, "ve")
    tdone = k.dinc(k.s.dma_start(out=fT,
                                 in_=f_ch.rearrange("p a b -> p (a b)"),
                                 transpose=True), "dT")
    emit_postln_residual(
        k, lambda tt: fT[:, tt:256:8, :], tdone,
        lambda tt: io["r_in"][tt * 128:(tt + 1) * 128, :],
        lambda tt: io["r_out"][tt * 128:(tt + 1) * 128, :],
        8, (rt4_a, rt4_b), (ru4_a, ru4_b), stats, 48, epscol)
    k.wait(k.y, "dS")
    k.wait(k.y, "dS2")


def _build_l1ffn():
    nc = bass.Bass("TRN2", target_bir_lowering=False)
    io = {}
    io["r_in"] = nc.dram_tensor("r_in", [T, D], F32, kind="ExternalInput").ap()
    io["wbT"] = nc.dram_tensor("wbT", [32, 128, 32, 128], F32, kind="ExternalInput").ap()
    io["wrT"] = nc.dram_tensor("wrT", [128, 32, 8], F32, kind="ExternalInput").ap()
    io["laT"] = nc.dram_tensor("laT", [128, 32, 128], F32, kind="ExternalInput").ap()
    io["lbT"] = nc.dram_tensor("lbT", [128, D], F32, kind="ExternalInput").ap()
    io["bias_sb"] = nc.dram_tensor("bias_sb", [128, 32], F32, kind="ExternalInput").ap()
    io["ident"] = nc.dram_tensor("ident", [128, 128], F32, kind="ExternalInput").ap()
    io["r_out"] = nc.dram_tensor("r_out", [T, D], F32, kind="ExternalOutput").ap()
    with ExitStack() as es:
        def sb(name, shape, dt):
            return es.enter_context(nc.sbuf_tensor(name, shape, dt))
        k = Kx(nc, es, SEMS)
        block = es.enter_context(nc.Block())

        @block.sync
        def _(sy):
            _emit_l1ffn(k, es, sb, nc, io)
    return nc


def _run_l1ffn(p, R):
    nc = _build_l1ffn()
    bias128 = np.ascontiguousarray(p["bias_p"].reshape(32, 128).T)
    maps = [dict(r_in=np.ascontiguousarray(R[c * T:(c + 1) * T]),
                 wbT=p["wbT_pk"], wrT=p["wrT_pk"], laT=p["laT_pk"], lbT=p["lbT"],
                 bias_sb=bias128, ident=p["ident"]) for c in range(NC)]
    res = run_bass_kernel_spmd(nc, maps, core_ids=list(range(NC)))
    return np.concatenate([res.results[c]["r_out"] for c in range(NC)], axis=0)




TH = 1152
DSSM = 8192
NHM = 64
CONVD = 8448
EPS_RMS = 1e-5


def _emit_mamba(k, es, sb, nc, io, groups=None):
    if groups is None:
        groups = [list(range(NC))]
    W = len(groups[0])
    stats = sb("stats", [128, 96], F32)
    epscol = sb("epscol", [128, 1], F32)
    onecol = sb("onecol", [128, 1], F32)
    ones1 = sb("ones1", [1, 128], F32)
    id_f32 = sb("id_f32", [128, 128], F32)
    id_bf = sb("id_bf", [128, 128], BF16)
    ltmask = sb("ltmask_sb", [128, 128], BF16)
    dt_sp = sb("dt_sp", [64, T], F32)
    cum = sb("cum", [64, T], F32)
    ctp = sb("ctp", [64, 18], F32)
    dtT = sb("dtT", [128, 8, 64], F32)
    ncumT = sb("ncumT", [128, 8, 64], F32)
    ecumT = sb("ecumT", [128, 8, 64], F32)
    wdtT = sb("wdtT", [128, 8, 64], F32)
    decT = sb("decT", [128, 8, 64], F32)
    pfT = sb("pfT", [128, 8, 64], F32)
    h_sb = sb("h_sb", [128, 64, 128], BF16)
    C_full = sb("C_full", [128, 8, 128], BF16)
    nw_sb = sb("nw_sb", [128, DSSM], BF16)
    Dcol = sb("Dcol", [128, 64], F32)
    cw_sb = sb("cwv", [128, 66, 4], F32)
    dtb = sb("dtbv", [64, 1], F32)
    alg = sb("algv", [64, 1], F32)
    wmsk = sb("wmskv", [8, 8], F32)
    selc = sb("selcv", [8, 1], F32)
    gct = sb("gct", [8, 64], F32)
    wco = sb("wco", [128, 8, 64], F32)
    dummyT = sb("dummyT", [128, 16], BF16)
    dummyS = sb("dummyS", [16, 128], BF16)
    ARE = sb("ARE", [128, 66816], BF16)

    k.g.memset(epscol[:], EPS_RMS)
    k.g.memset(onecol[:], 1.0)
    k.g.memset(ones1[:], 1.0)
    k.g.memset(h_sb[:], 0.0)
    k.g.memset(dummyS[:], 0.0)
    ge_m0 = k.inc(k.g.engine_nop(), "ge")
    k.wait(k.s, "ge", ge_m0)
    k.wait(k.v, "ge", ge_m0)
    k.wait(k.t, "ge", ge_m0)
    k.dinc(k.s.dma_start(out=dummyT[:], in_=dummyS[:], transpose=True), "dT")
    k.dinc(k.y.dma_start(out=id_f32[:], in_=io["ident"]), "dX")
    k.dinc(k.g.dma_start(out=id_bf[:], in_=io["ident"]), "dV")
    k.dinc(k.g.dma_start(out=ltmask[:], in_=io["ltmask"]), "dV")
    k.dinc(k.y.dma_start(out=cw_sb[:],
                         in_=io["conv_w"].rearrange("(a p) t -> p a t", p=128)), "dX")
    k.dinc(k.y.dma_start(out=dtb[:], in_=io["dt_bias"]), "dX")
    k.dinc(k.y.dma_start(out=alg[:], in_=io["A_log"]), "dX")
    k.dinc(k.y.dma_start(out=wmsk[:], in_=io["wmask"]), "dX")
    k.dinc(k.y.dma_start(out=selc[:], in_=io["selmask"]), "dX")
    with nc.sbuf_tensor("drow", [1, 64], F32) as drow, \
         nc.sbuf_tensor("nwrow", [1, 4096], F32) as nwrow, \
         nc.psum_tensor("psb0", [128, 2, 512], F32) as psb0:
        k.dinc(k.y.dma_start(out=drow[:], in_=io["D_param"]), "dX")
        k.wait(k.s, "dX")
        k.inc(k.s.activation(alg[:], alg[:], AF.Exp), "se")
        k.wait(k.t, "dX")
        k.wait(k.t, "ge", ge_m0)
        tp = k.inc(k.t.matmul(psb0[:, 0, 0:64], ones1[:], drow[:],
                              start=True, stop=True), "pe")
        k.wait(k.v, "pe", tp)
        k.inc(k.v.tensor_copy(Dcol[:], psb0[:, 0, 0:64]), "ve")
        for hf_ in range(2):
            k.wait(k.y, "pe")
            ldn = k.dinc(k.y.dma_start(out=nwrow[:],
                                       in_=io["norm_w"][:, hf_ * 4096:
                                                        (hf_ + 1) * 4096]), "dX")
            k.wait(k.t, "dX", ldn)
            for c8 in range(8):
                k.wait(k.t, "ve")
                tp = k.inc(k.t.matmul(psb0[:, c8 % 2, :], ones1[:],
                                      nwrow[:, c8 * 512:(c8 + 1) * 512],
                                      start=True, stop=True), "pe")
                k.wait(k.v, "pe", tp)
                k.inc(k.v.tensor_copy(
                    nw_sb[:, hf_ * 4096 + c8 * 512:hf_ * 4096 + (c8 + 1) * 512],
                    psb0[:, c8 % 2, :]), "ve")

    xh = ARE[:, 0:36864].rearrange("p (a b) -> p a b", a=32)
    rtA = ARE[:, 45056:53248].bitcast(F32)
    rtB = ARE[:, 36864:45056].bitcast(F32)
    xhtA = ARE[:, 53248:57344]
    xhtB = ARE[:, 57344:61440]
    emit_prelnT(k, lambda tt: io["r_in"][tt * 128:(tt + 1) * 128, :],
                xh, 9, (rtA, rtB), (xhtA, xhtB), stats, 0, epscol)

    k.join(k.g, ("ve", "dT", "dT2", "se"))
    wbuf = ARE[:, 36864:45056].rearrange("p (x a b) -> p x a b", x=2, a=32)
    stg = ARE[:, 45056:46080].rearrange("p (x b) -> p x b", x=2)
    with nc.psum_tensor("psA", [128, 2, 512], F32) as psA:
        wl = {}

        def load_win(j):
            nm = "dW" if j % 2 == 0 else "dW2"
            wl[j] = (nm, k.dinc(k.g.dma_start(
                out=wbuf[:, j % 2], in_=io["winP"][j]), nm))

        load_win(0)
        load_win(1)
        k.wait(k.t, "dT")
        evac = {}
        sdma = {}
        segs = [(128, 512), (640, 512)]
        segx = [(0, 512), (512, 512), (1024, 128)]
        u = 0
        for j in range(131):
            k.wait(k.t, wl[j][0], wl[j][1])
            isz = j < 64
            isx = 64 <= j < 130
            w_ = 128 if j < 130 else 64
            for (off, ln) in (segx if isx else segs):
                if u >= 2:
                    k.wait(k.t, "se", evac[u - 2])
                for ci in range(32):
                    mm = k.t.matmul(psA[0:w_, u % 2, 0:ln],
                                    wbuf[:, j % 2, ci, 0:w_],
                                    xh[:, ci, off:off + ln],
                                    start=(ci == 0), stop=(ci == 31))
                pj = k.inc(mm, "pe")
                k.wait(k.s, "pe", pj)
                if j == 130:
                    evac[u] = k.inc(k.s.activation(
                        dt_sp[:, off - 128:off - 128 + ln],
                        psA[0:64, u % 2, 0:ln], AF.Identity), "se")
                else:
                    if u >= 2 and (u - 2) in sdma:
                        k.wait(k.s, sdma[u - 2][0], sdma[u - 2][1])
                    evac[u] = k.inc(k.s.activation(stg[:, u % 2, 0:ln],
                                                   psA[:, u % 2, 0:ln],
                                                   AF.Identity), "se")
                    k.wait(k.y, "se", evac[u])
                    if isz:
                        dst = io["z_scr"][j, :, off - 128:off - 128 + ln]
                    else:
                        dst = io["xbc_scr"][j - 64, :, off:off + ln]
                    nmS = "dS" if u % 2 == 0 else "dS2"
                    sdma[u] = (nmS, k.dinc(k.y.dma_start(
                        out=dst, in_=stg[:, u % 2, 0:ln]), nmS))
                u += 1
            if j + 2 < 131:
                k.wait(k.g, "pe")
                load_win(j + 2)

    k.join(k.v, ("se",))
    dtt = ARE[:, 36864:45056].bitcast(F32)[0:64, 0:1024]
    prow = ARE[:, 45056:47232].bitcast(F32)[0:1, 0:1088]
    k.inc(k.v.tensor_scalar(dt_sp[:], dt_sp[:], dtb[:], None, op0=ALU.add), "ve")
    k.wait(k.v, "ve")
    k.inc(k.v.tensor_scalar(dtt[:], dt_sp[:], -1.0, None, op0=ALU.mult), "ve")
    k.wait(k.v, "ve")
    k.inc(k.v.tensor_tensor(dtt[:], dtt[:], dt_sp[:], op=ALU.min), "ve")
    k.wait(k.s, "ve")
    k.inc(k.s.activation(dtt[:], dtt[:], AF.Exp), "se")
    k.wait(k.s, "se")
    k.inc(k.s.activation(dtt[:], dtt[:], AF.Ln, bias=onecol[0:64, :]), "se")
    k.wait(k.v, "se")
    k.inc(k.v.tensor_scalar(dt_sp[:], dt_sp[:], 0.0, None, op0=ALU.max), "ve")
    k.wait(k.v, "ve")
    k.inc(k.v.tensor_tensor(dt_sp[:], dt_sp[:], dtt[:], op=ALU.add), "ve")
    k.wait(k.v, "ve")
    k.inc(k.v.tensor_scalar(dtt[:], dt_sp[:], alg[:], -1.0, op0=ALU.mult,
                            op1=ALU.mult), "ve")
    for ck in range(8):
        k.wait(k.v, "ve")
        k.inc(k.v.tensor_tensor_scan(cum[:, ck * 128:(ck + 1) * 128],
                                     dtt[:, ck * 128:(ck + 1) * 128],
                                     dtt[:, ck * 128:(ck + 1) * 128],
                                     0.0, op0=ALU.add, op1=ALU.bypass), "ve")
        k.wait(k.v, "ve")
        k.inc(k.v.tensor_copy(ctp[:, ck:ck + 1],
                              cum[:, ck * 128 + 127:ck * 128 + 128]), "ve")
    k.g.memset(ctp[0:64, 8:9], 0.0)
    k.inc(k.g.engine_nop(), "ge")
    k.wait(k.v, "ge")
    for ck in range(1, 9):
        k.wait(k.v, "ve")
        k.inc(k.v.tensor_tensor(ctp[:, 8 + ck:9 + ck], ctp[:, 7 + ck:8 + ck],
                                ctp[:, ck - 1:ck], op=ALU.add), "ve")
    k.wait(k.y, "ve")
    k.wait(k.y, "dS2")
    k.dinc(k.y.dma_start(out=io["cum_scr"].rearrange("h t -> (h t)"), in_=cum[:]),
           "dS")
    with nc.psum_tensor("psT3", [128, 2, 512], F32) as psT3, \
         nc.sbuf_tensor("ctpT", [17, 64], F32) as ctpT:
        k.wait(k.t, "ve")
        tpx = k.inc(k.t.transpose(psT3[0:17, 0, 0:64], ctp[0:64, 0:17],
                                  id_f32[0:64, 0:64]), "pe")
        k.wait(k.v, "pe", tpx)
        k.inc(k.v.tensor_copy(ctpT[:], psT3[0:17, 0, 0:64]), "ve")
        k.wait(k.y, "ve")
        k.dinc(k.y.dma_start(out=io["pf_scr"].rearrange("c h -> (c h)"),
                             in_=ctpT[:]), "dS")
        k.wait(k.y, "dS")
        k.dinc(k.y.dma_start(out=prow[:],
                             in_=io["pf_scr"].rearrange("c h -> (c h)")), "dX")
        for ck in range(8):
            k.wait(k.t, "ve")
            k.wait(k.t, "se")
            tp = k.inc(k.t.transpose(psT3[:, 0, 0:64],
                                     dt_sp[:, ck * 128:(ck + 1) * 128],
                                     id_f32[0:64, 0:64]), "pe")
            k.wait(k.v, "pe", tp)
            k.inc(k.v.tensor_copy(dtT[:, ck, :], psT3[:, 0, 0:64]), "ve")
            k.wait(k.t, "ve")
            tp = k.inc(k.t.transpose(psT3[:, 1, 0:64],
                                     cum[:, ck * 128:(ck + 1) * 128],
                                     id_f32[0:64, 0:64]), "pe")
            k.wait(k.v, "pe", tp)
            k.inc(k.v.tensor_scalar(ncumT[:, ck, :], psT3[:, 1, 0:64], -1.0, None,
                                    op0=ALU.mult), "ve")
            k.wait(k.s, "pe", tp)
            k.wait(k.s, "ve")
            k.inc(k.s.activation(ecumT[:, ck, :], psT3[:, 1, 0:64], AF.Exp), "se")
            k.wait(k.v, "se")
            k.inc(k.v.tensor_scalar(dtt[:, ck * 128:(ck + 1) * 128],
                                    cum[:, ck * 128:(ck + 1) * 128],
                                    ctp[:, ck:ck + 1], None, op0=ALU.subtract),
                  "ve")
            k.wait(k.s, "ve")
            k.inc(k.s.activation(dtt[:, ck * 128:(ck + 1) * 128],
                                 dtt[:, ck * 128:(ck + 1) * 128], AF.Exp,
                                 scale=-1.0), "se")
            k.wait(k.v, "se")
            k.inc(k.v.tensor_tensor(dtt[:, ck * 128:(ck + 1) * 128],
                                    dtt[:, ck * 128:(ck + 1) * 128],
                                    dt_sp[:, ck * 128:(ck + 1) * 128],
                                    op=ALU.mult), "ve")
            k.wait(k.t, "ve")
            tp = k.inc(k.t.transpose(psT3[:, 0, 64:128],
                                     dtt[:, ck * 128:(ck + 1) * 128],
                                     id_f32[0:64, 0:64]), "pe")
            k.wait(k.v, "pe", tp)
            k.inc(k.v.tensor_copy(wdtT[:, ck, :], psT3[:, 0, 64:128]), "ve")
            k.wait(k.t, "dX")
            k.wait(k.t, "se")
            tp = k.inc(k.t.matmul(psT3[:, 1, 64:128], ones1[:],
                                  prow[:, ck * 64:(ck + 1) * 64],
                                  start=True, stop=True), "pe")
            k.wait(k.s, "pe", tp)
            k.inc(k.s.activation(decT[:, ck, :], psT3[:, 1, 64:128], AF.Exp), "se")
            k.wait(k.t, "se")
            tp = k.inc(k.t.matmul(psT3[:, 1, 64:128], ones1[:],
                                  prow[:, (8 + ck) * 64:(9 + ck) * 64],
                                  start=True, stop=True), "pe")
            k.wait(k.s, "pe", tp)
            k.inc(k.s.activation(pfT[:, ck, :], psT3[:, 1, 64:128], AF.Exp), "se")
            k.wait(k.v, "se")
            k.inc(k.v.tensor_tensor(pfT[:, ck, :], pfT[:, ck, :], ecumT[:, ck, :],
                                    op=ALU.mult), "ve")
            k.wait(k.t, "se")
            k.wait(k.t, "ve")
    k.wait(k.y, "ve")
    k.dinc(k.y.dma_start(out=io["ct_in"], in_=ctp[0:64, 16:17]), "dS")

    cnv_in = ARE[:, 0:8646].rearrange("p (a b) -> p a b", a=66)
    cnv_f = ARE[:, 8704:17152].rearrange("p (a b) -> p a b", a=66)
    cnv_t = ARE[:, 17152:25600].rearrange("p (a b) -> p a b", a=66)
    xbc_s = ARE[:, 25600:34048].rearrange("p (a b) -> p a b", a=66)
    xT_sb = ARE[:, 34048:42240].rearrange("p (a b) -> p a b", a=64)
    yA = ARE[:, 42240:50432].rearrange("p (a b) -> p a b", a=64)
    cumrow = ARE[:, 50432:66816].bitcast(F32)[0:1, 0:8192]
    with nc.sbuf_tensor("BT_sb", [128, 128], BF16) as BT_sb, \
         nc.sbuf_tensor("Bw_sb", [128, 2, 128], BF16) as Bw_sb, \
         nc.sbuf_tensor("cbt_m", [128, 128], BF16) as cbt_m, \
         nc.sbuf_tensor("E_sb", [128, 2, 128], BF16) as E_sb, \
         nc.sbuf_tensor("MT_sb", [128, 2, 128], BF16) as MT_sb, \
         nc.sbuf_tensor("u_tmp", [128, 2, 128], F32) as u_tmp, \
         nc.psum_tensor("psB", [128, 8, 512], F32) as psB:
        prev_spill = None
        for ck in range(8):
            k.wait(k.y, "ve")
            k.wait(k.y, "se")
            k.wait(k.y, "dS")
            k.wait(k.y, "dS2")
            ld = k.dinc(k.y.dma_start(
                out=cnv_in,
                in_=io["xbc_scr"].rearrange("j p t -> p j t")[:, :,
                                                              125 + ck * 128:
                                                              256 + ck * 128]),
                "dX")
            ldc = k.dinc(k.y.dma_start(
                out=cumrow[:],
                in_=io["cum_scr"].rearrange("h (c t) -> c h t", c=8)[ck]), "dX2")
            k.wait(k.v, "dX", ld)
            k.wait(k.v, "pe")
            k.inc(k.v.tensor_tensor(cnv_f, cnv_in[:, :, 0:128],
                                    cw_sb[:, :, 0].broadcast_to([128, 66, 128]),
                                    op=ALU.mult), "ve")
            for tap in range(1, 4):
                k.wait(k.v, "ve")
                k.inc(k.v.tensor_tensor(cnv_t, cnv_in[:, :, tap:tap + 128],
                                        cw_sb[:, :, tap]
                                        .broadcast_to([128, 66, 128]),
                                        op=ALU.mult), "ve")
                k.wait(k.v, "ve")
                k.inc(k.v.tensor_tensor(cnv_f, cnv_f, cnv_t, op=ALU.add), "ve")
            k.wait(k.s, "ve")
            k.wait(k.s, "pe")
            k.wait(k.s, "dT")
            sil = k.inc(k.s.activation(xbc_s, cnv_f, AF.Silu), "se")
            k.inc(k.s.activation(C_full[:, ck, :], cnv_f[:, 65, :], AF.Silu), "se")
            k.wait(k.s, "se", sil)
            xtd = k.dinc(k.s.dma_start(
                out=xT_sb, in_=xbc_s.rearrange("p a b -> p (a b)")[:, 0:8192],
                transpose=True), "dT")
            btd = k.dinc(k.s.dma_start(
                out=BT_sb[:], in_=xbc_s[:, 64, :], transpose=True), "dT")
            k.wait(k.t, "se", sil)
            k.wait(k.t, "ve")
            cbmm = k.inc(k.t.matmul(psB[:, 0, 0:128], xbc_s[:, 64, :],
                                    xbc_s[:, 65, :], start=True, stop=True), "pe")
            k.wait(k.v, "pe", cbmm)
            k.wait(k.v, "dV")
            k.inc(k.v.tensor_tensor(cbt_m[:], psB[:, 0, 0:128], ltmask[:],
                                    op=ALU.mult), "ve")
            k.wait(k.v, "dT", btd)
            k.wait(k.t, "dX2", ldc)
            k.wait(k.t, "dT", xtd + 16)
            k.wait(k.t, "ve")
            if prev_spill is not None:
                k.wait(k.v, "dS", prev_spill)

            gmm = {}; clamp_ve = {}; bw_ve = {}; u_ve = {}; ya_ve = {}
            hp_ve = {}; mi_pe = {}; mj_pe = {}; ms_pe = {}
            for h in range(65):
                b = h % 2
                if h < 64:
                    if h >= 2:
                        k.wait(k.t, "ve", clamp_ve[h - 2])
                    gmm[h] = k.inc(k.t.matmul(
                        psB[:, b, 0:128], ones1[:],
                        cumrow[:, h * 128:(h + 1) * 128],
                        start=True, stop=True), "pe")
                if h >= 1:
                    hp = h - 1
                    bp = hp % 2
                    k.wait(k.t, "ve", bw_ve[hp])
                    if hp >= 2:
                        k.wait(k.t, "ve", ya_ve[hp - 2])
                    mi_pe[hp] = k.inc(k.t.matmul(
                        psB[:, 2 + bp, 0:128], MT_sb[:, bp, :], xT_sb[:, hp, :],
                        start=True, stop=True), "pe")
                    if hp >= 2:
                        k.wait(k.t, "ve", hp_ve[hp - 2])
                    mj_pe[hp] = k.inc(k.t.matmul(
                        psB[:, 4 + bp, 0:128], C_full[:, ck, :], h_sb[:, hp, :],
                        start=True, stop=True), "pe")
                    ms_pe[hp] = k.inc(k.t.matmul(
                        psB[:, 6 + bp, 0:128], Bw_sb[:, bp, :], xT_sb[:, hp, :],
                        start=True, stop=True), "pe")
                if h < 64:
                    k.wait(k.v, "pe", gmm[h])
                    clamp_ve[h] = k.inc(k.v.tensor_scalar(
                        E_sb[:, b, :], psB[:, b, 0:128], ncumT[:, ck, h:h + 1],
                        0.0, op0=ALU.add, op1=ALU.min), "ve")
                    k.wait(k.s, "ve", clamp_ve[h])
                    k.inc(k.s.activation(E_sb[:, b, :], E_sb[:, b, :], AF.Exp),
                          "se")
                    k.wait(k.v, "se")
                    k.inc(k.v.scalar_tensor_tensor(
                        MT_sb[:, b, :], E_sb[:, b, :], dtT[:, ck, h:h + 1],
                        cbt_m[:], op0=ALU.mult, op1=ALU.mult), "ve")
                    bw_ve[h] = k.inc(k.v.tensor_scalar(
                        Bw_sb[:, b, :], BT_sb[:], wdtT[:, ck, h:h + 1], None,
                        op0=ALU.mult), "ve")
                if h >= 1:
                    hp = h - 1
                    bp = hp % 2
                    k.wait(k.v, "pe", mi_pe[hp])
                    u_ve[hp] = k.inc(k.v.scalar_tensor_tensor(
                        u_tmp[:, bp, :], xT_sb[:, hp, :], Dcol[:, hp:hp + 1],
                        psB[:, 2 + bp, 0:128], op0=ALU.mult, op1=ALU.add), "ve")
                    k.wait(k.v, "pe", mj_pe[hp])
                    k.wait(k.v, "ve", u_ve[hp])
                    ya_ve[hp] = k.inc(k.v.scalar_tensor_tensor(
                        yA[:, hp, :], psB[:, 4 + bp, 0:128],
                        ecumT[:, ck, hp:hp + 1], u_tmp[:, bp, :],
                        op0=ALU.mult, op1=ALU.add), "ve")
                    k.wait(k.v, "pe", ms_pe[hp])
                    hp_ve[hp] = k.inc(k.v.scalar_tensor_tensor(
                        h_sb[:, hp, :], h_sb[:, hp, :], decT[:, ck, hp:hp + 1],
                        psB[:, 6 + bp, 0:128], op0=ALU.mult, op1=ALU.add), "ve")
            k.wait(k.y, "ve")
            prev_spill = k.dinc(k.y.dma_start(
                out=io["ypre_scr"][ck],
                in_=yA.rearrange("p a b -> p (a b)")), "dS")
        k.wait(k.v, "dS")

    k.wait(k.y, "ve")
    k.wait(k.y, "dS")
    k.wait(k.y, "dS2")
    k.dinc(k.y.dma_start(out=io["st_in"].rearrange("n h pp -> n (h pp)"),
                         in_=h_sb.rearrange("p a b -> p (a b)")), "dS")
    k.wait(k.g, "dS")
    cc1 = k.inc(k.g.collective_compute(
        "AllGather", ALU.bypass, replica_groups=groups,
        ins=[io["st_in"].rearrange("n h pp -> n (h pp)")],
        outs=[io["st_out"][0:W].rearrange("c n hp -> (c n) hp")]), "cc")
    cc2 = k.inc(k.g.collective_compute(
        "AllGather", ALU.bypass, replica_groups=groups,
        ins=[io["ct_in"]], outs=[io["ct_out"][0:W].rearrange("c h -> (c h)")]), "cc")
    k.wait(k.g, "cc", cc2)
    k.g.memset(gct[:], 0.0)
    k.inc(k.g.engine_nop(), "ge")
    k.wait(k.g, "ge")
    k.dinc(k.g.dma_start(out=gct[0:W, :], in_=io["ct_out"][0:W]), "dV")
    Hf = ARE[:, 34048:50432].rearrange("p (a b c) -> p a b c", a=2, b=64)
    with nc.psum_tensor("psC", [128, 2, 512], F32) as psC:
        k.wait(k.t, "dV")
        mm = k.inc(k.t.matmul(psC[0:8, 0, 0:64], wmsk[:], gct[:],
                              start=True, stop=True), "pe")
        k.wait(k.s, "pe", mm)
        k.inc(k.s.activation(gct[:], psC[0:8, 0, 0:64], AF.Exp), "se")
        k.wait(k.v, "se")
        k.inc(k.v.tensor_scalar(gct[:], gct[:], selc[:], None, op0=ALU.mult), "ve")
        k.wait(k.y, "ve")
        k.dinc(k.y.dma_start(out=io["w_scr"].rearrange("c h -> (c h)"), in_=gct[:]),
               "dS")
        Sg = ARE[:, 0:16384].rearrange("p (x a b) -> p x a b", x=2, a=64)
        with nc.sbuf_tensor("wrow", [1, 512], F32) as wrow:
            k.wait(k.y, "dS")
            k.dinc(k.y.dma_start(out=wrow[:],
                                 in_=io["w_scr"].rearrange("c h -> (c h)")), "dX")
            k.wait(k.t, "dX")
            for cc_ in range(8):
                k.wait(k.t, "ve")
                k.wait(k.t, "se")
                mm = k.inc(k.t.matmul(psC[:, cc_ % 2, 0:64], ones1[:],
                                      wrow[:, cc_ * 64:(cc_ + 1) * 64],
                                      start=True, stop=True), "pe")
                k.wait(k.v, "pe", mm)
                k.inc(k.v.tensor_copy(wco[:, cc_, :], psC[:, cc_ % 2, 0:64]), "ve")
            k.g.memset(Hf[:, 0], 0.0)
            k.inc(k.g.engine_nop(), "ge")
            k.wait(k.g, "cc", cc1)
            lds = {}

            def load_sg(cc_):
                nm = "dG" if cc_ % 2 == 0 else "dG2"
                lds[cc_] = (nm, k.dinc(k.g.dma_start(
                    out=Sg[:, cc_ % 2].rearrange("p a b -> p (a b)"),
                    in_=io["st_out"][cc_]), nm))

            load_sg(0)
            if W > 1:
                load_sg(1)
            for cc_ in range(W):
                k.wait(k.v, lds[cc_][0], lds[cc_][1])
                k.wait(k.v, "ge")
                for h in range(64):
                    k.wait(k.v, "ve")
                    k.inc(k.v.scalar_tensor_tensor(
                        Hf[:, (cc_ + 1) % 2, h, :], Sg[:, cc_ % 2, h, :],
                        wco[:, cc_, h:h + 1], Hf[:, cc_ % 2, h, :],
                        op0=ALU.mult, op1=ALU.add), "ve")
                if cc_ + 2 < W:
                    k.wait(k.g, "ve")
                    load_sg(cc_ + 2)
    Hfin = Hf[:, W % 2]

    yF = ARE[:, 0:32768].rearrange("p (a b) -> p a b", a=64)
    zT = ARE[:, 50432:58624]
    yB2 = ARE[:, 58624:66816].rearrange("p (a b) -> p a b", a=64)
    wbuf3 = ARE[:, 50432:66816].rearrange("p (x a b) -> p x a b", x=2, a=64)
    with nc.sbuf_tensor("fst", [128, 2, 512], BF16) as fst, \
         nc.psum_tensor("psD", [128, 4, 512], F32) as psD:
        evg = {}
        fdma = {}
        for half in range(2):
            for ck4 in range(4):
                ck = half * 4 + ck4
                k.wait(k.y, "pe")
                k.wait(k.y, "ve")
                k.wait(k.y, "se")
                ldy = k.dinc(k.y.dma_start(
                    out=yF[:, :, ck4 * 128:(ck4 + 1) * 128],
                    in_=io["ypre_scr"][ck].rearrange("p (a b) -> p a b", a=64)),
                    "dX")
                k.wait(k.s, "pe")
                k.wait(k.s, "dT")
                ldz = k.dinc(k.s.dma_start(
                    out=zT.rearrange("p f -> p f"),
                    in_=io["z_scr"].rearrange("j p t -> (j p) t")
                    [:, ck * 128:(ck + 1) * 128], transpose=True), "dT")
                k.wait(k.v, "dX", ldy)
                stt_c = {}
                for h in range(64):
                    if h < 2:
                        k.wait(k.t, "ve")
                    else:
                        k.wait(k.t, "ve", stt_c[h - 2])
                    mm = k.inc(k.t.matmul(psD[:, h % 2, 0:128], C_full[:, ck, :],
                                          Hfin[:, h, :], start=True, stop=True),
                               "pe")
                    k.wait(k.v, "pe", mm)
                    stt_c[h] = k.inc(k.v.scalar_tensor_tensor(
                        yB2[:, h, :], psD[:, h % 2, 0:128], pfT[:, ck, h:h + 1],
                        yF[:, h, ck4 * 128:(ck4 + 1) * 128],
                        op0=ALU.mult, op1=ALU.add), "ve")
                scr = yF[:, :, ck4 * 128:(ck4 + 1) * 128]
                k.wait(k.s, "dT", ldz)
                k.wait(k.s, "ve")
                sil = k.inc(k.s.activation(scr, zT.rearrange("p (a b) -> p a b",
                                                             a=64), AF.Silu),
                            "se")
                k.wait(k.v, "se", sil)
                k.inc(k.v.tensor_tensor(zT.rearrange("p (a b) -> p a b", a=64),
                                        yB2, scr, op=ALU.mult), "ve")
                k.wait(k.s, "ve")
                k.inc(k.s.activation(scr, zT.rearrange("p (a b) -> p a b", a=64),
                                     AF.Square, accum_out=stats[:, 80:81]), "se")
                k.wait(k.v, "se")
                k.wait(k.v, "ve")
                k.inc(k.v.tensor_scalar(stats[:, 81:82], stats[:, 80:81],
                                        1.0 / DSSM, None, op0=ALU.mult), "ve")
                k.wait(k.s, "ve")
                k.inc(k.s.activation(stats[:, 81:82], stats[:, 81:82], AF.Ln,
                                     bias=epscol[:]), "se")
                k.wait(k.s, "se")
                k.inc(k.s.activation(stats[:, 81:82], stats[:, 81:82], AF.Exp,
                                     scale=-0.5), "se")
                k.wait(k.v, "se")
                k.inc(k.v.scalar_tensor_tensor(
                    zT.rearrange("p (a b) -> p a b", a=64),
                    zT.rearrange("p (a b) -> p a b", a=64),
                    stats[:, 81:82], nw_sb.rearrange("p (a b) -> p a b", a=64),
                    op0=ALU.mult, op1=ALU.mult), "ve")
                k.wait(k.s, "ve")
                k.dinc(k.s.dma_start(out=yF[:, :, ck4 * 128:(ck4 + 1) * 128],
                                     in_=zT, transpose=True), "dT")
                k.wait(k.v, "dT")
                k.wait(k.t, "dT")
            wl3 = {}

            def load_wo(j):
                nm = "dW" if j % 2 == 0 else "dW2"
                wl3[j] = (nm, k.dinc(k.g.dma_start(
                    out=wbuf3[:, j % 2], in_=io["woutP"][j]), nm))

            k.join(k.g, ("pe", "ve", "se", "dT"))
            load_wo(0)
            load_wo(1)
            for j in range(32):
                g_ = half * 32 + j
                k.wait(k.t, wl3[j][0], wl3[j][1])
                if g_ >= 2:
                    k.wait(k.t, "se", evg[g_ - 2])
                for ci in range(64):
                    mm = k.t.matmul(psD[:, 2 + j % 2, :],
                                    wbuf3[:, j % 2, ci, :], yF[:, ci, :],
                                    start=(ci == 0), stop=(ci == 63))
                pj = k.inc(mm, "pe")
                k.wait(k.s, "pe", pj)
                if g_ >= 2:
                    k.wait(k.s, fdma[g_ - 2][0], fdma[g_ - 2][1])
                evg[g_] = k.inc(k.s.activation(fst[:, j % 2, :],
                                               psD[:, 2 + j % 2, :],
                                               AF.Identity), "se")
                k.wait(k.y, "se", evg[g_])
                nmF = "dS" if j % 2 == 0 else "dS2"
                fdma[g_] = (nmF, k.dinc(k.y.dma_start(
                    out=io["f_scr"][j, :, half * 512:half * 512 + 512],
                    in_=fst[:, j % 2, :]), nmF))
                if j + 2 < 32:
                    k.wait(k.g, "pe")
                    load_wo(j + 2)

    fT2 = ARE[:, 0:32768].rearrange("p (a b) -> p a b", a=8)
    rtru = ARE[:, 32768:49152].bitcast(F32)
    rtru2 = ARE[:, 49152:65536].bitcast(F32)
    rt4_a = rtru[:, 0:4096]
    ru4_a = rtru[:, 4096:8192]
    rt4_b = rtru2[:, 0:4096]
    ru4_b = rtru2[:, 4096:8192]
    k.join(k.y, ("pe", "se", "dS", "dS2", "ve", "dT", "dT2"))
    k.join(k.s, ("dS", "dS2", "ve", "pe"))
    ldf = k.dinc(k.s.dma_start(out=fT2,
                               in_=io["f_scr"].rearrange("j p t -> (j p) t"),
                               transpose=True), "dT")
    emit_postln_residual(
        k, lambda tt: fT2[:, tt, :], ldf,
        lambda tt: io["r_in"][128 + tt * 128:256 + tt * 128, :],
        lambda tt: io["r_out"][tt * 128:(tt + 1) * 128, :],
        8, (rt4_a, rt4_b), (ru4_a, ru4_b), stats, 48, epscol)
    k.wait(k.y, "dS")
    k.wait(k.y, "dS2")


def _build_mamba(groups=None):
    nc = bass.Bass("TRN2", target_bir_lowering=False)
    io = {}
    io["r_in"] = nc.dram_tensor("r_in", [TH, D], F32, kind="ExternalInput").ap()
    io["winP"] = nc.dram_tensor("winP", [131, 128, 32, 128], F32, kind="ExternalInput").ap()
    io["woutP"] = nc.dram_tensor("woutP", [32, 128, 64, 128], F32, kind="ExternalInput").ap()
    io["conv_w"] = nc.dram_tensor("conv_w", [CONVD, 4], F32, kind="ExternalInput").ap()
    io["dt_bias"] = nc.dram_tensor("dt_bias", [64, 1], F32, kind="ExternalInput").ap()
    io["A_log"] = nc.dram_tensor("A_log", [64, 1], F32, kind="ExternalInput").ap()
    io["D_param"] = nc.dram_tensor("D_param", [1, 64], F32, kind="ExternalInput").ap()
    io["norm_w"] = nc.dram_tensor("norm_w", [1, DSSM], F32, kind="ExternalInput").ap()
    io["ident"] = nc.dram_tensor("ident", [128, 128], F32, kind="ExternalInput").ap()
    io["ltmask"] = nc.dram_tensor("ltmask", [128, 128], F32, kind="ExternalInput").ap()
    io["wmask"] = nc.dram_tensor("wmask", [8, 8], F32, kind="ExternalInput").ap()
    io["selmask"] = nc.dram_tensor("selmask", [8, 1], F32, kind="ExternalInput").ap()
    io["r_out"] = nc.dram_tensor("r_out", [T, D], F32, kind="ExternalOutput").ap()
    io["z_scr"] = nc.dram_tensor("z_scr", [64, 128, T], BF16).ap()
    io["xbc_scr"] = nc.dram_tensor("xbc_scr", [66, 128, TH], BF16).ap()
    io["ypre_scr"] = nc.dram_tensor("ypre_scr", [8, 128, DSSM], BF16).ap()
    io["f_scr"] = nc.dram_tensor("f_scr", [32, 128, T], BF16).ap()
    io["cum_scr"] = nc.dram_tensor("cum_scr", [64, T], F32).ap()
    io["pf_scr"] = nc.dram_tensor("pf_scr", [17, 64], F32).ap()
    io["w_scr"] = nc.dram_tensor("w_scr", [8, 64], F32).ap()
    io["st_in"] = nc.dram_tensor("st_in", [128, 64, 128], BF16).ap()
    io["st_out"] = nc.dram_tensor("st_out", [8, 128, 64 * 128], BF16).ap()
    io["ct_in"] = nc.dram_tensor("ct_in", [64], F32).ap()
    io["ct_out"] = nc.dram_tensor("ct_out", [8, 64], F32).ap()
    with ExitStack() as es:
        def sb(name, shape, dt):
            return es.enter_context(nc.sbuf_tensor(name, shape, dt))
        k = Kx(nc, es, SEMS)
        block = es.enter_context(nc.Block())

        @block.sync
        def _(sy):
            _emit_mamba(k, es, sb, nc, io, groups=groups)
    return nc


def _run_mamba(p, R):
    nc = _build_mamba()
    lt = np.tril(np.ones((128, 128), np.float32)).T
    maps = []
    for c in range(NC):
        rin = np.zeros((TH, D), np.float32)
        rin[128:] = R[c * T:(c + 1) * T]
        if c > 0:
            rin[125:128] = R[c * T - 3:c * T]
        wm = np.zeros((8, 8), np.float32)
        for cp in range(8):
            for cpp in range(cp + 1, c):
                wm[cpp, cp] = 1.0
        sel = np.zeros((8, 1), np.float32)
        sel[:c, 0] = 1.0
        maps.append(dict(
            r_in=rin, winP=p["winP_pk"], woutP=p["woutP_pk"], conv_w=p["conv_w"],
            dt_bias=np.ascontiguousarray(p["dt_bias"].reshape(64, 1)),
            A_log=np.ascontiguousarray(p["A_log"].reshape(64, 1)),
            D_param=p["D_param"], norm_w=p["norm_w"], ident=p["ident"],
            ltmask=np.ascontiguousarray(lt), wmask=wm, selmask=sel))
    res = run_bass_kernel_spmd(nc, maps, core_ids=list(range(NC)))
    return np.concatenate([res.results[c]["r_out"] for c in range(NC)], axis=0)





class _NCProxy:
    def __init__(self, nc, prefix):
        self._nc = nc
        self._p = prefix

    def sbuf_tensor(self, name, shape, dt):
        return self._nc.sbuf_tensor(self._p + name, shape, dt)

    def psum_tensor(self, name, shape, dt):
        return self._nc.psum_tensor(self._p + name, shape, dt)

    def __getattr__(self, a):
        return getattr(self._nc, a)


def _full_join(k):
    for eng in (k.t, k.v, k.s, k.g, k.y):
        k.join(eng, ("pe", "ve", "se", "ge", "dW", "dW2", "dV", "dX", "dX2",
                     "dG", "dG2", "dT", "dT2", "dS", "dS2", "cc"))


def _emit_halo_exchange(k, es, sb, nc, io):
    zrows = sb("zrows", [128, D], F32)
    ohot = sb("ohot", [8, 1], F32)
    halv = sb("halv", [8, 3 * D], F32)
    hsel = sb("hsel", [1, 3 * D], F32)
    k.g.memset(zrows[:], 0.0)
    k.inc(k.g.engine_nop(), "ge")
    k.wait(k.y, "ge")
    k.dinc(k.y.dma_start(out=io["r2"][0:125, :], in_=zrows[0:125, :]), "dS")
    k.dinc(k.y.dma_start(out=io["ohot"] if False else ohot[:],
                         in_=io["onehot"]), "dX")
    k.wait(k.y, "dS")
    k.dinc(k.y.dma_start(out=io["hal_in"],
                         in_=io["r2"][1149:1152, :].rearrange("a b -> (a b)")),
           "dS")
    k.wait(k.g, "dS")
    ccx = k.inc(k.g.collective_compute(
        "AllGather", ALU.bypass, replica_groups=[list(range(NC))],
        ins=[io["hal_in"]], outs=[io["hal_out"].rearrange("c f -> (c f)")]), "cc")
    k.wait(k.g, "cc", ccx)
    k.dinc(k.g.dma_start(out=halv[:], in_=io["hal_out"]), "dG")
    with nc.psum_tensor("psH", [128, 512], F32) as psH:
        k.wait(k.t, "dG")
        k.wait(k.t, "dX")
        pes = []
        for i6 in range(24):
            mm = k.t.matmul(psH[0:1, :], ohot[:],
                            halv[:, i6 * 512:(i6 + 1) * 512],
                            start=True, stop=True)
            if i6 % 4 == 3 or i6 == 23:
                pass
            pes.append(k.inc(mm, "pe"))
            k.wait(k.v, "pe", pes[-1])
            k.inc(k.v.tensor_copy(hsel[:, i6 * 512:(i6 + 1) * 512], psH[0:1, :]),
                  "ve")
            k.wait(k.t, "ve")
        k.wait(k.y, "ve")
        k.dinc(k.y.dma_start(out=io["r2"][125:128, :].rearrange("a b -> (a b)"),
                             in_=hsel[:]), "dS")
    k.wait(k.y, "dS")


def _build_fused():
    nc = bass.Bass("TRN2", target_bir_lowering=False)
    io = {}

    def din(name, shape):
        io[name] = nc.dram_tensor(name, shape, F32, kind="ExternalInput").ap()

    din("xin", [XT, D])
    din("wqT", [32, 128, 32, 128]); din("wkv", [2, 128, 32, 128])
    din("woT", [32, 128, 32, 128])
    din("cosT", [128, XT]); din("sinrotT", [128, XT]); din("prot", [128, 128])
    din("ident", [128, 128]); din("maskneg", [128, NKT]); din("uw", [64, T])
    din("up", [16, 128, 2, 1024]); din("down", [16, 128, 8, 256])
    din("winP", [131, 128, 32, 128]); din("woutP", [32, 128, 64, 128])
    din("conv_w", [CONVD, 4])
    din("dt_bias", [64, 1]); din("A_log", [64, 1]); din("D_param", [1, 64])
    din("norm_w", [1, DSSM]); din("ltmask", [128, 128]); din("wmask", [8, 8])
    din("selmask", [8, 1]); din("onehot", [8, 1])
    din("wbT", [32, 128, 32, 128]); din("wrT", [128, 32, 8])
    din("laT", [128, 32, 128]); din("lbT", [128, D]); din("bias_sb", [128, 32])
    io["out"] = nc.dram_tensor("out", [T, D], F32, kind="ExternalOutput").ap()
    io["r1"] = nc.dram_tensor("r1", [T, D], F32).ap()
    io["r2"] = nc.dram_tensor("r2", [TH, D], F32).ap()
    io["r3"] = nc.dram_tensor("r3", [T, D], F32).ap()
    io["rs_scr"] = nc.dram_tensor("rs_scr", [2, 8, 4, 128], BF16).ap()
    io["z_scr"] = nc.dram_tensor("z_scr", [64, 128, T], BF16).ap()
    io["xbc_scr"] = nc.dram_tensor("xbc_scr", [66, 128, TH], BF16).ap()
    io["ypre_scr"] = nc.dram_tensor("ypre_scr", [8, 128, DSSM], BF16).ap()
    io["f_scr"] = nc.dram_tensor("f_scr", [32, 128, T], BF16).ap()
    io["cum_scr"] = nc.dram_tensor("cum_scr", [64, T], F32).ap()
    io["pf_scr"] = nc.dram_tensor("pf_scr", [17, 64], F32).ap()
    io["w_scr"] = nc.dram_tensor("w_scr", [8, 64], F32).ap()
    io["st_in"] = nc.dram_tensor("st_in", [128, 64, 128], BF16).ap()
    io["st_out"] = nc.dram_tensor("st_out", [8, 128, 64 * 128], BF16).ap()
    io["ct_in"] = nc.dram_tensor("ct_in", [64], F32).ap()
    io["ct_out"] = nc.dram_tensor("ct_out", [8, 64], F32).ap()
    io["hal_in"] = nc.dram_tensor("hal_in", [3 * D], F32).ap()
    io["hal_out"] = nc.dram_tensor("hal_out", [8, 3 * D], F32).ap()

    with ExitStack() as es:
        k = Kx(nc, es, SEMS)
        block = es.enter_context(nc.Block())

        @block.sync
        def _(sy):
            def run(tag, fn, sub_io):
                ncp = _NCProxy(nc, tag)
                with ExitStack() as es2:
                    def sb2(name, shape, dt):
                        return es2.enter_context(
                            nc.sbuf_tensor(tag + name, shape, dt))
                    fn(k, es2, sb2, ncp, sub_io)
                _full_join(k)

            run("A_", _emit_l0attn, dict(
                xin=io["xin"], wqT=io["wqT"], wkv=io["wkv"],
                woT=io["woT"], cosT=io["cosT"], sinrotT=io["sinrotT"],
                prot=io["prot"], ident=io["ident"], maskneg=io["maskneg"],
                uw=io["uw"], r_out=io["r1"], rs_scr=io["rs_scr"]))
            run("B_", _emit_l0ffn, dict(
                r_in=io["r1"], up=io["up"], down=io["down"],
                r_out=io["r2"][128:TH, :]))
            run("H_", _emit_halo_exchange, dict(
                r2=io["r2"], onehot=io["onehot"], hal_in=io["hal_in"],
                hal_out=io["hal_out"]))
            run("M_", _emit_mamba, dict(
                r_in=io["r2"], winP=io["winP"], woutP=io["woutP"],
                conv_w=io["conv_w"], dt_bias=io["dt_bias"], A_log=io["A_log"],
                D_param=io["D_param"], norm_w=io["norm_w"], ident=io["ident"],
                ltmask=io["ltmask"], wmask=io["wmask"], selmask=io["selmask"],
                r_out=io["r3"], z_scr=io["z_scr"], xbc_scr=io["xbc_scr"],
                ypre_scr=io["ypre_scr"], f_scr=io["f_scr"],
                cum_scr=io["cum_scr"], pf_scr=io["pf_scr"], w_scr=io["w_scr"],
                st_in=io["st_in"], st_out=io["st_out"], ct_in=io["ct_in"],
                ct_out=io["ct_out"]))
            run("F_", _emit_l1ffn, dict(
                r_in=io["r3"], wbT=io["wbT"], wrT=io["wrT"], laT=io["laT"],
                lbT=io["lbT"], bias_sb=io["bias_sb"], ident=io["ident"],
                r_out=io["out"]))
    return nc


_FUSED_CACHE = {}


def _run_fused(p):
    nc = _build_fused()
    lt = np.tril(np.ones((128, 128), np.float32)).T
    bias128 = np.ascontiguousarray(p["bias_p"].reshape(32, 128).T)
    R = p["x_p"]
    maps = []
    for c in range(NC):
        g0 = c * T - HALO
        xin = np.zeros((XT, D), dtype=np.float32)
        lo = max(0, g0); hi = min(S, g0 + XT)
        xin[lo - g0: hi - g0] = R[lo:hi]
        wm = np.zeros((8, 8), np.float32)
        for cp in range(8):
            for cpp in range(cp + 1, c):
                wm[cpp, cp] = 1.0
        sel = np.zeros((8, 1), np.float32)
        sel[:c, 0] = 1.0
        oh = np.zeros((8, 1), np.float32)
        if c > 0:
            oh[c - 1, 0] = 1.0
        maps.append(dict(
            xin=xin, wqT=p["wqT_pk"], wkv=p["wkv_pk"], woT=p["woT_pk"],
            cosT=p["cosT"][c], sinrotT=p["sinrotT"][c], prot=p["P_rot"],
            ident=p["ident"], maskneg=p["maskneg"][c], uw=p["uw"][c],
            up=p["up_pk"], down=p["down_pk"],
            winP=p["winP_pk"], woutP=p["woutP_pk"], conv_w=p["conv_w"],
            dt_bias=np.ascontiguousarray(p["dt_bias"].reshape(64, 1)),
            A_log=np.ascontiguousarray(p["A_log"].reshape(64, 1)),
            D_param=p["D_param"], norm_w=p["norm_w"],
            ltmask=np.ascontiguousarray(lt), wmask=wm, selmask=sel, onehot=oh,
            wbT=p["wbT_pk"], wrT=p["wrT_pk"], laT=p["laT_pk"], lbT=p["lbT"],
            bias_sb=bias128))
    res = run_bass_kernel_spmd(nc, maps, core_ids=list(range(NC)))
    return np.concatenate([res.results[c]["out"] for c in range(NC)], axis=0)


def kernel(**inputs):
    p = _prep(inputs)
    R = _run_fused(p)
    out = np.empty((S, D), dtype=np.float32)
    out[:, p["perm"]] = R
    return out.reshape(1, S, D)


def kernel_unfused(**inputs):
    p = _prep(inputs)
    R = p["x_p"]
    R = _run_l0attn(p, R)
    R = _run_l0ffn(p, R)
    R = _run_mamba(p, R)
    R = _run_l1ffn(p, R)
    out = np.empty((S, D), dtype=np.float32)
    out[:, p["perm"]] = R
    return out.reshape(1, S, D)
